# version 06
# baseline (speedup 1.0000x reference)
import numpy as np
import ml_dtypes
from contextlib import ExitStack
import concourse.bass as bass
import concourse.mybir as mybir
from concourse.bass_utils import run_bass_kernel_spmd

F32 = mybir.dt.float32
BF16 = mybir.dt.bfloat16
I32 = mybir.dt.int32
AF = mybir.ActivationFunctionType
OP = mybir.AluOpType
AX = mybir.AxisListType.X
BFD = ml_dtypes.bfloat16

B, T, D = 4, 4096, 1024
HALF = T // 2
NEG = -1.0e30


class Prog:

    def __init__(self, nc):
        self.nc = nc
        self.ops = []
        self.stream_pos = {}
        self.last_writer = {}
        self.readers = {}

    def op(self, eng, fn, reads=(), writes=(), dma=False, extra=()):
        stream = eng + "@d" if dma else eng
        oid = len(self.ops)
        pos = self.stream_pos.get(stream, 0)
        self.stream_pos[stream] = pos + 1
        deps = set(extra)
        for k in reads:
            w = self.last_writer.get(k)
            if w is not None:
                deps.add(w)
            self.readers.setdefault(k, []).append(oid)
        for k in writes:
            w = self.last_writer.get(k)
            if w is not None:
                deps.add(w)
            deps.update(self.readers.get(k, ()))
            self.last_writer[k] = oid
            self.readers[k] = []
        deps.discard(oid)
        self.ops.append(dict(eng=eng, fn=fn, dma=dma, stream=stream, pos=pos,
                             deps=deps))
        return oid

    def t(self, fn, reads=(), writes=()):
        return self.op("tensor", fn, reads, writes)

    def v(self, fn, reads=(), writes=()):
        return self.op("vector", fn, reads, writes)

    def s(self, fn, reads=(), writes=()):
        return self.op("scalar", fn, reads, writes)

    def g(self, fn, reads=(), writes=()):
        return self.op("gpsimd", fn, reads, writes)

    def dma(self, eng, fn, reads=(), writes=()):
        return self.op(eng, fn, reads, writes, dma=True)

    parcond = None

    def pdma(self, eng, fn0, fn1, reads=(), writes=()):
        def fn(e):
            with e.If(self.parcond()):
                i0 = fn0(e)
            with e.Else():
                i1 = fn1(e)
            return [i0, i1]
        return self.op(eng, fn, reads, writes, dma=True)

    def row_dma(self, eng, out_ap, src, reads=(), writes=()):
        if isinstance(src, tuple):
            return self.pdma(eng,
                             lambda e, s=src[0]: e.dma_start(out=out_ap, in_=s),
                             lambda e, s=src[1]: e.dma_start(out=out_ap, in_=s),
                             reads, writes)
        return self.dma(eng, lambda e, s=src: e.dma_start(out=out_ap, in_=s),
                        reads, writes)

    def barrier(self):
        tails = {}
        for i, o in enumerate(self.ops):
            tails.setdefault(o["stream"], []).append(i)
        ids = set()
        for s, l in tails.items():
            if s.endswith("@d"):
                ids.update(l[-self.DMA_POOL:])
            else:
                ids.add(l[-1])
        ids = tuple(sorted(ids))
        for eng in ("tensor", "vector", "scalar", "gpsimd", "sync"):
            self.op(eng, lambda e: e.nop(), extra=ids)

    DMA_POOL = 8

    def emit(self):
        nc = self.nc
        ops = self.ops
        n = len(ops)
        K = self.DMA_POOL
        def skip_edge(prod, cons):
            return (prod["stream"] == cons["stream"] == "tensor")

        need_inc = [o["dma"] for o in ops]
        for o in ops:
            for d in o["deps"]:
                if not skip_edge(ops[d], o):
                    need_inc[d] = True
        semkey = [None] * n
        semval = [0] * n
        eng_cnt = {}
        for i, o in enumerate(ops):
            if o["dma"]:
                semkey[i] = (o["eng"] + "@d", o["pos"] % K)
                semval[i] = 16 * (o["pos"] // K + 1)
            else:
                c = eng_cnt.get(o["eng"], 0)
                if need_inc[i]:
                    c += 1
                eng_cnt[o["eng"]] = c
                semkey[i] = (o["eng"], 0)
                semval[i] = c
        used_keys = set()
        for i in range(n):
            if need_inc[i]:
                used_keys.add(semkey[i])
        sems = {}
        sem_ctx = ExitStack()
        for kk in sorted(used_keys):
            sems[kk] = sem_ctx.enter_context(
                nc.semaphore(f"sem_{kk[0].replace('@', '_')}_{kk[1]}"))
        waits = [dict() for _ in range(n)]
        water = {}

        def add_wait(i, kk, v):
            if v <= 0:
                return
            wk = (ops[i]["eng"], kk)
            if water.get(wk, 0) >= v:
                return
            waits[i][kk] = max(waits[i].get(kk, 0), v)

        for i, o in enumerate(ops):
            if o["dma"] and o["pos"] >= K:
                add_wait(i, semkey[i], semval[i] - 16)
            for d in sorted(o["deps"]):
                pd = ops[d]
                if skip_edge(pd, o):
                    continue
                if not need_inc[d]:
                    continue
                add_wait(i, semkey[d], semval[d])
            for kk, v in waits[i].items():
                wk = (o["eng"], kk)
                water[wk] = max(water.get(wk, 0), v)
        final_cnt = {}
        for i, o in enumerate(ops):
            if o["dma"]:
                kk = semkey[i]
                final_cnt[kk] = max(final_cnt.get(kk, 0), semval[i])
        per_eng = {}
        for i, o in enumerate(ops):
            per_eng.setdefault(o["eng"], []).append(i)

        with nc.Block() as block:
            for eng_name in ("sync", "tensor", "vector", "scalar", "gpsimd"):
                if eng_name not in per_eng:
                    continue
                idxs = per_eng[eng_name]

                def body(e, idxs=idxs, eng_name=eng_name):
                    for i in idxs:
                        o = ops[i]
                        for kk, v in sorted(waits[i].items()):
                            e.wait_ge(sems[kk], v)
                        ins = o["fn"](e)
                        if need_inc[i]:
                            if ins is None:
                                raise RuntimeError(
                                    f"op {i} on {eng_name} returned None")
                            for i2 in (ins if isinstance(ins, list) else [ins]):
                                i2.then_inc(sems[semkey[i]],
                                            16 if o["dma"] else 1)
                    for kk, v in sorted(final_cnt.items()):
                        if kk[0] == eng_name + "@d":
                            e.wait_ge(sems[kk], v)

                getattr(block, eng_name)(body)
        sem_ctx.close()



def _l1_consts(par):
    c = {}
    jj = np.arange(512)[None, :]
    pp = np.arange(128)[:, None]
    c["d0"] = (jj - pp).astype(np.float32)
    c["ident"] = np.eye(128, dtype=np.float32)
    s = np.arange(16)
    qt = 2 * s + par
    tri = np.zeros((2, 128, 512), np.float32)
    for h in range(2):
        v = (2 * h + par) % 4
        tri[h] = np.where(jj > pp + 128 * v, NEG, 0.0)
    c["tri_a"] = tri
    offa = np.zeros((1, 128), np.float32)
    for si in range(16):
        for kc in range(8):
            offa[0, 8 * si + kc] = 512 * kc - 128 * qt[si]
    c["offa"] = offa
    c["qps"] = (128 * qt[None, :] + np.arange(128)[:, None]).astype(np.float32)
    ncols = int(sum(2 * s + 2))
    kpos = np.zeros((128, ncols), np.float32)
    kill = np.zeros((128, ncols), np.float32)
    col = 0
    for si in range(16):
        for kc in range(2 * si + 2):
            kpos[:, col] = 128 * kc + np.arange(128)
            if kc > qt[si]:
                kill[:, col] = NEG
                kpos[:, col] = 0.0
            col += 1
    c["kpos"] = kpos
    c["kill"] = kill
    trib = np.zeros((2, 128, 128), np.float32)
    j2 = np.arange(128)[None, :]
    for idx in range(2):
        if idx == par:
            trib[idx] = np.where(pp > j2, NEG, 0.0)
        elif idx > par:
            trib[idx] = NEG
    c["tri_b"] = trib
    return c


def _l1_colmap():
    colmap = {}
    col = 0
    for si in range(16):
        for kc in range(2 * si + 2):
            colmap[(si, kc)] = col
            col += 1
    return colmap, col



def emit_l1(nc, p, ctx, pfx, x2, x2own_row, wqk, wv, slope, d0_in, ident_in,
            identb_in, tri_a_in, tri_b_in, offa_in, kpos_in, kill_in, qps_in,
            x3own, NBCOL, colmap, phases=99):
    sb = lambda name, shape, dt: ctx.enter_context(nc.sbuf_tensor(pfx + name, shape, dt))
    psa = lambda name, shape, dt=F32: ctx.enter_context(nc.psum_tensor(pfx + name, shape, dt))

    xT = [sb(f"xT{i}", [128, T], BF16) for i in range(8)]
    V = [sb(f"V{i}", [128, D], BF16) for i in range(32)]
    wvT = [sb(f"wvT{i}", [128, D], BF16) for i in range(8)]
    qT = [sb(f"qT{i}", [128, HALF], BF16) for i in range(8)]
    tm = [sb(f"tm{i}", [128, D], BF16) for i in range(2)]
    tmf = [sb(f"tmf{i}", [128, D], F32) for i in range(2)]
    identb = sb("identb", [128, 128], BF16)
    identf = sb("identf", [128, 128], F32)
    onesrow_b = sb("onesrow_b", [1, 128], BF16)
    onescol_b = sb("onescol_b", [128, 1], BF16)
    onesrow_f = sb("onesrow_f", [1, 128], F32)
    a0 = sb("a0", [128, 512], F32)
    tri_a = sb("tri_a_s", [128, 1024], BF16)
    tri_b = sb("tri_b_s", [128, 256], BF16)
    offa = sb("offa_s", [1, 128], F32)
    offas = sb("offas", [1, 128], F32)
    kposb = sb("kposb", [128, NBCOL], F32)
    killb = sb("killb", [128, NBCOL], F32)
    biascol = sb("biascol", [128, NBCOL], F32)
    qps = sb("qps_s", [128, 16], F32)
    qpss = sb("qpss", [128, 16], F32)
    slope_c = sb("slope_c", [128, 1], F32)
    ss = sb("ss", [1, 1], F32)
    mcols = sb("mcols", [128, 16], F32)
    mraw = sb("mraw", [128, 1], F32)
    mneg = sb("mneg", [128, 1], F32)
    mrowb = sb("mrowb", [1, 128], BF16)
    ptb = [sb(f"ptb{i}", [128, 128], BF16) for i in range(34)]
    pstage = sb("pstage", [128, 128], F32)
    sums_sb = sb("sums_sb", [1, 128], F32)
    rec = sb("rec", [128, 1], F32)
    att_sb = [sb(f"att_sb{i}", [128, 128], F32) for i in range(2)]
    at_tm = sb("at_tm", [128, D], F32)
    x2r = sb("x2r", [128, D], F32)
    s1 = sb("s1", [128, D], F32)
    acc1 = sb("acc1", [128, 1], F32)
    acc2 = sb("acc2", [128, 1], F32)
    ln1 = sb("ln1", [128, 1], F32)
    ln2 = sb("ln2", [128, 1], F32)
    ln3 = sb("ln3", [128, 1], F32)
    epsc = sb("epsc", [128, 1], F32)

    P0 = psa("P0", [128, 512])
    P1 = psa("P1", [128, 512])
    P2 = psa("P2", [128, 512])
    P3 = psa("P3", [128, 512])
    P4c = psa("P4c", [128, 128], BF16)
    P4d = psa("P4d", [128, 128], BF16)
    P5 = psa("P5", [128, 128])
    P7 = psa("P7", [128, 128])

    for ap_out, ap_in, key in ((ss[:], slope, "ss"), (tmf[0][:, 0:512], d0_in, ("tmf", 0)),
                               (identf[:], ident_in, "identf"),
                               (offa[:], offa_in, "offa"), (kposb[:], kpos_in, "kposb"),
                               (killb[:], kill_in, "killb"), (qps[:], qps_in, "qps")):
        p.dma("sync", lambda e, o=ap_out, i=ap_in: e.dma_start(out=o, in_=i),
              writes=[key])
    p.dma("sync", lambda e: e.dma_start(out=identb[:], in_=identb_in),
          writes=["identb"])
    for i in range(2):
        p.dma("sync", lambda e, i=i: e.dma_start(
            out=tri_a[:, 512 * i:512 * i + 512], in_=tri_a_in[i]), writes=["tri_a"])
        p.dma("sync", lambda e, i=i: e.dma_start(
            out=tri_b[:, 128 * i:128 * i + 128], in_=tri_b_in[i]), writes=["tri_b"])
    p.v(lambda e: e.memset(onesrow_b[:], 1.0), writes=["onesrow_b"])
    p.v(lambda e: e.memset(onescol_b[:], 1.0), writes=["onescol_b"])
    p.v(lambda e: e.memset(onesrow_f[:], 1.0), writes=["onesrow_f"])
    p.v(lambda e: e.memset(epsc[:], 1e-5), writes=["epsc"])
    p.t(lambda e: e.matmul(P5[:, 0:1], onesrow_f[:], ss[:], start=True, stop=True),
        reads=["onesrow_f", "ss"], writes=["P5"])
    p.v(lambda e: e.tensor_copy(slope_c[:], P5[:, 0:1]), reads=["P5"],
        writes=["slope_c"])
    p.v(lambda e: e.tensor_scalar_mul(a0[:], tmf[0][:, 0:512], slope_c[:]),
        reads=[("tmf", 0), "slope_c"], writes=["a0"])
    p.v(lambda e: e.tensor_scalar_mul(offas[:], offa[:], slope_c[0:1, :]),
        reads=["offa", "slope_c"], writes=["offas"])
    p.v(lambda e: e.tensor_scalar_mul(qpss[:], qps[:], slope_c[:]),
        reads=["qps", "slope_c"], writes=["qpss"])
    p.v(lambda e: e.tensor_scalar_mul(biascol[:], kposb[:], slope_c[:]),
        reads=["kposb", "slope_c"], writes=["biascol"])
    p.v(lambda e: e.tensor_add(biascol[:], biascol[:], killb[:]),
        reads=["biascol", "killb"], writes=["biascol"])

    def xo_tile(dc, sl):
        idx = 8 + 2 * dc + (1 if sl >= 8 else 0)
        return V[idx][:, 128 * (sl % 8):128 * (sl % 8) + 128], ("V", idx)

    def load_transpose(src_row, n_tiles, dst_fn):
        for tc in range(n_tiles):
            tmk = ("tm", tc % 2)
            tmfk = ("tmf", tc % 2)
            p.row_dma("sync", tmf[tc % 2][:], src_row(tc),
                      reads=["x2src"], writes=[tmfk])
            p.v(lambda e, tc=tc: e.tensor_copy(tm[tc % 2][:], tmf[tc % 2][:]),
                reads=[tmfk], writes=[tmk])
            for dc in range(8):
                PT = P4c if dc % 2 == 0 else P4d
                ptk = ("P4", dc % 2)
                dst, dstk = dst_fn(dc, tc)
                p.t(lambda e, tc=tc, dc=dc, PT=PT: e.transpose(
                    PT[:], tm[tc % 2][:, 128 * dc:128 * dc + 128],
                    identb[:]), reads=[tmk, "identb"], writes=[ptk])
                p.v(lambda e, dst=dst, PT=PT: e.tensor_copy(dst, PT[:]),
                    reads=[ptk], writes=[dstk])

    if phases >= 2:
        load_transpose(lambda tc: x2[128 * tc:128 * tc + 128, :], 32,
                       lambda dc, tc: (xT[dc][:, 128 * tc:128 * tc + 128], ("xtT", dc)))
        load_transpose(x2own_row, 16, xo_tile)

    for i in range(8 if phases >= 3 else 0):
        p.dma("sync", lambda e, i=i: e.dma_start(
            out=tmf[i % 2][:], in_=wv[128 * i:128 * i + 128, :]),
            writes=[("tmf", i % 2)])
        p.v(lambda e, i=i: e.tensor_copy(V[i][:], tmf[i % 2][:]),
            reads=[("tmf", i % 2)], writes=[("V", i)])
    for c in range(8 if phases >= 3 else 0):
        for r in range(8):
            PT = P4c if r % 2 == 0 else P4d
            ptk = ("P4", r % 2)
            p.t(lambda e, c=c, r=r, PT=PT: e.transpose(
                PT[:], V[r][:, 128 * c:128 * c + 128], identb[:]),
                reads=[("V", r), "identb"], writes=[ptk])
            p.v(lambda e, c=c, r=r, PT=PT: e.tensor_copy(
                wvT[c][:, 128 * r:128 * r + 128], PT[:]),
                reads=[ptk], writes=[("wvT", c)])

    for i in range(8 if phases >= 4 else 0):
        p.dma("sync", lambda e, i=i: e.dma_start(
            out=tmf[i % 2][:], in_=wqk[128 * i:128 * i + 128, :]),
            writes=[("tmf", i % 2)])
        p.v(lambda e, i=i: e.tensor_copy(V[i][:], tmf[i % 2][:]),
            reads=[("tmf", i % 2)], writes=[("V", i)])
    for oc in range(8 if phases >= 4 else 0):
        for sg in range(4):
            PS, pk = (P2, "P2") if sg % 2 == 0 else (P3, "P3")
            vidx = 8 + (1 if sg >= 2 else 0)
            for ic in range(8):
                vt = 8 + 2 * ic + (1 if sg >= 2 else 0)
                coff = 512 * (sg % 2)
                p.t(lambda e, oc=oc, ic=ic, vt=vt, coff=coff, PS=PS: e.matmul(
                    PS[:], V[ic][:, 128 * oc:128 * oc + 128],
                    V[vt][:, coff:coff + 512],
                    start=(ic == 0), stop=(ic == 7)),
                    reads=[("V", ic), ("V", vt)], writes=[pk])
            p.s(lambda e, oc=oc, sg=sg, PS=PS: e.activation(
                qT[oc][:, 512 * sg:512 * sg + 512], PS[:], AF.Copy,
                scale=1.0 / 32.0), reads=[pk], writes=[("qT", oc)])

    for tc in range(32 if phases >= 5 else 0):
        for half, PS, pk in ((0, P0, "P0"), (1, P1, "P1")):
            for ec in range(8):
                p.t(lambda e, tc=tc, ec=ec, half=half, PS=PS: e.matmul(
                    PS[:], xT[ec][:, 128 * tc:128 * tc + 128],
                    wvT[ec][:, 512 * half:512 * half + 512],
                    start=(ec == 0), stop=(ec == 7)),
                    reads=[("xtT", ec), ("wvT", ec)], writes=[pk])
            p.v(lambda e, tc=tc, half=half, PS=PS: e.tensor_copy(
                V[tc][:, 512 * half:512 * half + 512], PS[:]),
                reads=[pk], writes=[("V", tc)])

    nslots = 16 if phases >= 7 else (1 if phases >= 6 else 0)
    for sl in range(nslots):
        nA = sl // 2 + 1
        nB = 2 * sl + 2
        for kc in range(nA):
            PS, pk = (P2, "P2") if kc % 2 == 0 else (P3, "P3")
            for dc in range(8):
                p.t(lambda e, sl=sl, kc=kc, dc=dc, PS=PS: e.matmul(
                    PS[:], qT[dc][:, 128 * sl:128 * sl + 128],
                    xT[dc][:, 512 * kc:512 * kc + 512],
                    start=(dc == 0), stop=(dc == 7)),
                    reads=[("qT", dc), ("xtT", dc)], writes=[pk])
            p.v(lambda e, PS=PS: e.tensor_add(tmf[0][:, 0:512], PS[:], a0[:]),
                reads=[pk, "a0"], writes=[("tmf", 0)])
            if kc == nA - 1:
                p.v(lambda e, sl=sl: e.tensor_add(
                    tmf[0][:, 0:512], tmf[0][:, 0:512],
                    tri_a[:, 512 * (sl % 2):512 * (sl % 2) + 512]),
                    reads=[("tmf", 0), "tri_a"], writes=[("tmf", 0)])
            p.v(lambda e, kc=kc: e.reduce_max(mcols[:, kc:kc + 1],
                                               tmf[0][:, 0:512], axis=AX),
                reads=[("tmf", 0)], writes=["mcols"])
        p.t(lambda e, sl=sl, nA=nA: e.matmul(P5[:, 0:nA], onesrow_f[:],
                                             offas[0:1, 8 * sl:8 * sl + nA],
                                             start=True, stop=True),
            reads=["onesrow_f", "offas"], writes=["P5"])
        p.v(lambda e, nA=nA: e.tensor_add(mcols[:, 0:nA], mcols[:, 0:nA], P5[:, 0:nA]),
            reads=["mcols", "P5"], writes=["mcols"])
        p.v(lambda e, nA=nA: e.reduce_max(mraw[:], mcols[:, 0:nA], axis=AX),
            reads=["mcols"], writes=["mraw"])
        p.v(lambda e, sl=sl: e.tensor_scalar(mneg[:], mraw[:], qpss[:, sl:sl + 1],
                                             -1.0, op0=OP.add, op1=OP.mult),
            reads=["mraw", "qpss"], writes=["mneg"])
        p.t(lambda e: e.transpose(P5[0:1, 0:128], mneg[:], identf[:]),
            reads=["mneg", "identf"], writes=["P5"])
        p.v(lambda e: e.tensor_copy(mrowb[0:1, 0:128], P5[0:1, 0:128]),
            reads=["P5"], writes=["mrowb"])
        for kc in range(nB):
            PS, pk = (P5, "P5") if kc % 2 == 0 else (P7, "P7")
            for dc in range(8):
                p.t(lambda e, sl=sl, kc=kc, dc=dc, PS=PS: e.matmul(
                    PS[:], xT[dc][:, 128 * kc:128 * kc + 128],
                    qT[dc][:, 128 * sl:128 * sl + 128],
                    start=(dc == 0), stop=False),
                    reads=[("xtT", dc), ("qT", dc)], writes=[pk])
            p.t(lambda e, PS=PS: e.matmul(
                PS[:], onesrow_b[:], mrowb[0:1, 0:128],
                start=False, stop=True),
                reads=["onesrow_b", "mrowb"], writes=[pk])
            cidx = colmap[(sl, kc)]
            if kc >= 2 * sl:
                idx = kc - 2 * sl
                p.v(lambda e, idx=idx, PS=PS: e.tensor_add(
                    pstage[:], PS[:], tri_b[:, 128 * idx:128 * idx + 128]),
                    reads=[pk, "tri_b"], writes=["pstage"])
                src, srck = pstage, "pstage"
            else:
                src, srck = PS, pk
            p.s(lambda e, kc=kc, cidx=cidx, src=src: e.activation(
                ptb[kc][:], src[:], AF.Exp, bias=biascol[:, cidx:cidx + 1]),
                reads=[srck, "biascol"], writes=[("ptb", kc)])
        for kc in range(nB):
            p.t(lambda e, sl=sl, kc=kc, nB=nB: e.matmul(
                P2[0:1, 0:128], onescol_b[:], ptb[kc][:],
                start=(kc == 0), stop=(kc == nB - 1)),
                reads=["onescol_b", ("ptb", kc)], writes=["P2"])
        p.v(lambda e: e.tensor_copy(sums_sb[:], P2[0:1, 0:128]),
            reads=["P2"], writes=["sums_sb"])
        p.t(lambda e: e.transpose(P5[:, 0:1], sums_sb[:], identf[0:1, 0:1]),
            reads=["sums_sb", "identf"], writes=["P5"])
        p.v(lambda e: e.reciprocal(rec[:], P5[:, 0:1]), reads=["P5"], writes=["rec"])
        for vc in range(8):
            PA, pak = (P0, "P0") if vc % 2 == 0 else (P1, "P1")
            for kc in range(nB):
                p.t(lambda e, sl=sl, kc=kc, vc=vc, PA=PA, nB=nB: e.matmul(
                    PA[:, 0:128],
                    V[kc][:, 128 * vc:128 * vc + 128], ptb[kc][:],
                    start=(kc == 0), stop=(kc == nB - 1)),
                    reads=[("V", kc), ("ptb", kc)], writes=[pak])
            PT, ptk = (P5, "P5") if vc % 2 == 0 else (P7, "P7")
            ab, abk = att_sb[vc % 2], ("att_sb", vc % 2)
            p.v(lambda e, PA=PA, ab=ab: e.tensor_copy(ab[:], PA[:, 0:128]),
                reads=[pak], writes=[abk])
            p.t(lambda e, ab=ab, PT=PT: e.transpose(PT[:], ab[:], identf[:]),
                reads=[abk, "identf"], writes=[ptk])
            p.v(lambda e, vc=vc, PT=PT: e.tensor_copy(
                at_tm[:, 128 * vc:128 * vc + 128], PT[:]),
                reads=[ptk], writes=["at_tm"])
        p.row_dma("sync", x2r[:], x2own_row(sl),
                  reads=["x2src"], writes=["x2r"])
        p.v(lambda e: e.scalar_tensor_tensor(
            s1[:], at_tm[:], rec[:], x2r[:], op0=OP.mult, op1=OP.add,
            accum_out=acc1[:]),
            reads=["at_tm", "rec", "x2r"], writes=["s1", "acc1"])
        p.s(lambda e: e.activation(tm[0][:], s1[:], AF.Square, accum_out=acc2[:]),
            reads=["s1"], writes=[("tm", 0), "acc2"])
        p.v(lambda e: e.tensor_scalar_mul(ln1[:], acc1[:], 1.0 / D),
            reads=["acc1"], writes=["ln1"])
        p.v(lambda e: e.tensor_scalar_mul(ln2[:], acc2[:], 1.0 / D),
            reads=["acc2"], writes=["ln2"])
        p.v(lambda e: e.tensor_mul(ln3[:], ln1[:], ln1[:]),
            reads=["ln1"], writes=["ln3"])
        p.v(lambda e: e.tensor_sub(ln2[:], ln2[:], ln3[:]),
            reads=["ln2", "ln3"], writes=["ln2"])
        p.s(lambda e: e.activation(ln3[:], ln2[:], AF.Sqrt, bias=epsc[:]),
            reads=["ln2", "epsc"], writes=["ln3"])
        p.v(lambda e: e.reciprocal(ln2[:], ln3[:]), reads=["ln3"], writes=["ln2"])
        p.v(lambda e: e.tensor_scalar_mul(ln3[:], ln1[:], -1.0),
            reads=["ln1"], writes=["ln3"])
        p.v(lambda e: e.tensor_scalar(at_tm[:], s1[:], ln3[:], ln2[:],
                                      op0=OP.add, op1=OP.mult),
            reads=["s1", "ln3", "ln2"], writes=["at_tm"])
        p.dma("sync", lambda e, sl=sl: e.dma_start(
            out=x3own[128 * sl:128 * sl + 128, :], in_=at_tm[:]),
            reads=["at_tm"], writes=[("x3own", sl)])

    if phases < 7:
        for sl2 in range(16 - nslots):
            sl = nslots + sl2
            p.dma("sync", lambda e, sl=sl: e.dma_start(
                out=x3own[128 * sl:128 * sl + 128, :], in_=tmf[1][:]),
                reads=[("tmf", 1)], writes=[("x3own", sl)])





def build_l1(phases=99):
    nc = bass.Bass("TRN2", target_bir_lowering=False, dynamic_dma_scratch_size=2048)
    p = Prog(nc)
    colmap, NBCOL = _l1_colmap()
    x2 = nc.dram_tensor("x2", [T, D], F32, kind="ExternalInput").ap()
    x2own = nc.dram_tensor("x2own", [HALF, D], F32, kind="ExternalInput").ap()
    wqk = nc.dram_tensor("wqk", [D, D], F32, kind="ExternalInput").ap()
    wv = nc.dram_tensor("wv", [D, D], F32, kind="ExternalInput").ap()
    slope = nc.dram_tensor("slope", [1, 1], F32, kind="ExternalInput").ap()
    d0_in = nc.dram_tensor("d0", [128, 512], F32, kind="ExternalInput").ap()
    ident_in = nc.dram_tensor("ident", [128, 128], F32, kind="ExternalInput").ap()
    identb_in = nc.dram_tensor("identb_in", [128, 128], BF16, kind="ExternalInput").ap()
    tri_a_in = nc.dram_tensor("tri_a", [2, 128, 512], BF16, kind="ExternalInput").ap()
    tri_b_in = nc.dram_tensor("tri_b", [2, 128, 128], BF16, kind="ExternalInput").ap()
    offa_in = nc.dram_tensor("offa", [1, 128], F32, kind="ExternalInput").ap()
    kpos_in = nc.dram_tensor("kpos", [128, NBCOL], F32, kind="ExternalInput").ap()
    kill_in = nc.dram_tensor("kill", [128, NBCOL], F32, kind="ExternalInput").ap()
    qps_in = nc.dram_tensor("qps", [128, 16], F32, kind="ExternalInput").ap()
    x3own = nc.dram_tensor("x3own", [HALF, D], F32, kind="ExternalOutput").ap()

    ctx = ExitStack()
    x2own_row = lambda sl: x2own[128 * sl:128 * sl + 128, :]
    emit_l1(nc, p, ctx, "", x2, x2own_row, wqk, wv, slope, d0_in, ident_in,
            identb_in, tri_a_in, tri_b_in, offa_in, kpos_in, kill_in, qps_in,
            x3own, NBCOL, colmap)
    p.emit()
    ctx.close()
    return nc


def run_l1(x2_full, wqk, wv, slope):
    nc = build_l1()
    in_maps = []
    for c in range(8):
        b, par = c // 2, c % 2
        cc = _l1_consts(par)
        qt = 2 * np.arange(16) + par
        rows = (qt[:, None] * 128 + np.arange(128)[None, :]).reshape(-1)
        in_maps.append({
            "x2": np.ascontiguousarray(x2_full[b]),
            "x2own": np.ascontiguousarray(x2_full[b][rows]),
            "wqk": wqk, "wv": wv,
            "slope": np.asarray(slope, np.float32).reshape(1, 1),
            "d0": cc["d0"], "ident": cc["ident"],
            "identb_in": cc["ident"].astype(BFD),
            "tri_a": cc["tri_a"].astype(BFD),
            "tri_b": cc["tri_b"].astype(BFD), "offa": cc["offa"],
            "kpos": cc["kpos"], "kill": cc["kill"], "qps": cc["qps"],
        })
    res = run_bass_kernel_spmd(nc, in_maps, core_ids=list(range(8)))
    x3 = np.zeros((B, T, D), np.float32)
    for c in range(8):
        b, par = c // 2, c % 2
        qt = 2 * np.arange(16) + par
        rows = (qt[:, None] * 128 + np.arange(128)[None, :]).reshape(-1)
        x3[b][rows] = res.results[c]["x3own"]
    return x3





def emit_moe(nc, p, ctx, pfx, x0own, pa_row, pb_row, gate_w, w1_in, b1t_in, w2_in, b2_in, lng_bc_in, lnb_in, identb_in, ident_in, x2own, phases=99):
    NT = 16
    x1own = nc.dram_tensor(pfx + "x1own_d", [HALF, D], F32, kind="Internal").ap()
    acc_d = nc.dram_tensor(pfx + "acc_d", [HALF, D], F32, kind="Internal").ap()
    sb = lambda name, shape, dt: ctx.enter_context(nc.sbuf_tensor(pfx + name, shape, dt))
    psa = lambda name, shape, dt=F32: ctx.enter_context(nc.psum_tensor(pfx + name, shape, dt))

    x1T = [sb(f"x1T{i}", [128, HALF], BF16) for i in range(8)]
    w1b = [sb(f"w1b{i}", [128, 2 * D], BF16) for i in range(8)]
    w2b = [sb(f"w2b{i}", [128, D], BF16) for i in range(16)]
    hT = [sb(f"hT{i}", [128, 512], BF16) for i in range(16)]
    tm = [sb(f"tm{i}", [128, D], BF16) for i in range(2)]
    tmf = [sb(f"tmf{i}", [128, D], F32) for i in range(2)]
    xtf = [sb(f"xtf{i}", [128, 128], F32) for i in range(2)]
    identb = sb("identb", [128, 128], BF16)
    identf = sb("identf", [128, 128], F32)
    onesrow_b = sb("onesrow_b", [1, 128], BF16)
    sc = sb("sc", [128, 128], F32)
    gate_sb = sb("gate_sb", [128, 64], F32)
    rank = sb("rank", [128, 8], F32)
    tmp8 = sb("tmp8", [128, 8], F32)
    t8b = sb("t8b", [128, 8], F32)
    ind1 = sb("ind1", [128, 8], F32)
    ind2 = sb("ind2", [128, 8], F32)
    c1 = sb("c1", [128, 1], F32)
    c2 = sb("c2", [128, 1], F32)
    c3 = sb("c3", [128, 1], F32)
    wE = sb("wE", [128, 128], F32)
    wET = sb("wET", [8, HALF], F32)
    lnb_sb = sb("lnb_sb", [8, D], F32)
    b1sb = [sb(f"b1sb{i}", [128, 16], F32) for i in range(2)]
    gbc = [sb(f"gbc{i}", [128, D], F32) for i in range(2)]
    b2f = sb("b2f", [1, D], F32)
    b2b = sb("b2b", [1, D], BF16)
    x1r = [sb(f"x1r{i}", [128, D], F32) for i in range(2)]
    stile = sb("stile", [128, D], F32)
    y1 = sb("y1", [128, D], F32)
    ystage = [sb(f"ystage{i}", [128, D], F32) for i in range(2)]
    accsb = [sb(f"accsb{i}", [128, D], F32) for i in range(2)]
    acc1a = sb("acc1a", [128, 1], F32)
    acc1b = sb("acc1b", [128, 1], F32)
    acc2 = sb("acc2", [128, 1], F32)
    ln1 = sb("ln1", [128, 1], F32)
    ln2 = sb("ln2", [128, 1], F32)
    ln3 = sb("ln3", [128, 1], F32)
    eps6 = sb("eps6", [128, 1], F32)
    eps5 = sb("eps5", [128, 1], F32)

    P0 = psa("P0", [128, 512])
    P1 = psa("P1", [128, 512])
    P2 = psa("P2", [128, 512])
    P3 = psa("P3", [128, 512])
    P4c = psa("P4c", [128, 128], BF16)
    P4d = psa("P4d", [128, 128], BF16)
    P5 = psa("P5", [128, 128])
    P7 = psa("P7", [128, 128])

    p.dma("sync", lambda e: e.dma_start(out=identb[:], in_=identb_in), writes=["identb"])
    p.dma("sync", lambda e: e.dma_start(out=identf[:], in_=ident_in), writes=["identf"])
    p.dma("sync", lambda e: e.dma_start(out=lnb_sb[:], in_=lnb_in), writes=["lnb_sb"])
    for dc in range(8):
        p.dma("sync", lambda e, dc=dc: e.dma_start(
            out=gate_sb[:, 8 * dc:8 * dc + 8], in_=gate_w[128 * dc:128 * dc + 128, :]),
            writes=["gate_sb"])
    p.v(lambda e: e.memset(onesrow_b[:], 1.0), writes=["onesrow_b"])
    p.v(lambda e: e.memset(eps6[:], 1e-6), writes=["eps6"])
    p.v(lambda e: e.memset(eps5[:], 1e-5), writes=["eps5"])

    for tc in range(NT):
        p.dma("sync", lambda e, tc=tc: e.dma_start(
            out=x1r[0][:], in_=x0own[128 * tc:128 * tc + 128, :]),
            writes=[("x1r", 0)])
        p.row_dma("sync", x1r[1][:], pa_row(tc),
                  reads=["pgsrc"], writes=[("x1r", 1)])
        p.row_dma("sync", accsb[0][:], pb_row(tc),
                  reads=["pgsrc"], writes=[("accsb", 0)])
        p.v(lambda e: e.tensor_add(x1r[1][:], x1r[1][:], accsb[0][:]),
            reads=[("x1r", 1), ("accsb", 0)], writes=[("x1r", 1)])
        p.v(lambda e: e.scalar_tensor_tensor(
            stile[:], x1r[1][:], 1.0, x1r[0][:], op0=OP.mult, op1=OP.add,
            accum_out=acc1a[:]),
            reads=[("x1r", 0), ("x1r", 1)], writes=["stile", "acc1a"])
        p.s(lambda e: e.activation(tm[0][:], stile[:], AF.Square, accum_out=acc2[:]),
            reads=["stile"], writes=[("tm", 0), "acc2"])
        p.v(lambda e: e.tensor_scalar_mul(ln1[:], acc1a[:], 1.0 / D),
            reads=["acc1a"], writes=["ln1"])
        p.v(lambda e: e.tensor_scalar_mul(ln2[:], acc2[:], 1.0 / D),
            reads=["acc2"], writes=["ln2"])
        p.v(lambda e: e.tensor_mul(ln3[:], ln1[:], ln1[:]),
            reads=["ln1"], writes=["ln3"])
        p.v(lambda e: e.tensor_sub(ln2[:], ln2[:], ln3[:]),
            reads=["ln2", "ln3"], writes=["ln2"])
        p.s(lambda e: e.activation(ln3[:], ln2[:], AF.Sqrt, bias=eps5[:]),
            reads=["ln2", "eps5"], writes=["ln3"])
        p.v(lambda e: e.reciprocal(ln2[:], ln3[:]), reads=["ln3"], writes=["ln2"])
        p.v(lambda e: e.tensor_scalar_mul(ln3[:], ln1[:], -1.0),
            reads=["ln1"], writes=["ln3"])
        p.v(lambda e: e.tensor_scalar(y1[:], stile[:], ln3[:], ln2[:],
                                      op0=OP.add, op1=OP.mult),
            reads=["stile", "ln3", "ln2"], writes=["y1"])
        p.dma("sync", lambda e, tc=tc: e.dma_start(
            out=x1own[128 * tc:128 * tc + 128, :], in_=y1[:]),
            reads=["y1"], writes=[("x1d", tc)])

    for tc in range(NT):
        tmfk, tmk = ("tmf", tc % 2), ("tm", tc % 2)
        p.dma("sync", lambda e, tc=tc: e.dma_start(
            out=tmf[tc % 2][:], in_=x1own[128 * tc:128 * tc + 128, :]),
            reads=[("x1d", tc)], writes=[tmfk])
        p.v(lambda e, tc=tc: e.tensor_copy(tm[tc % 2][:], tmf[tc % 2][:]),
            reads=[tmfk], writes=[tmk])
        for dc in range(8):
            PT, ptk = (P4c, ("P4", 0)) if dc % 2 == 0 else (P4d, ("P4", 1))
            p.t(lambda e, tc=tc, dc=dc, PT=PT: e.transpose(
                PT[:], tm[tc % 2][:, 128 * dc:128 * dc + 128], identb[:]),
                reads=[tmk, "identb"], writes=[ptk])
            p.v(lambda e, tc=tc, dc=dc, PT=PT: e.tensor_copy(
                x1T[dc][:, 128 * tc:128 * tc + 128], PT[:]),
                reads=[ptk], writes=[("x1T", dc)])
            PF, pfk = (P5, "P5") if dc % 2 == 0 else (P7, "P7")
            p.t(lambda e, tc=tc, dc=dc, PF=PF: e.transpose(
                PF[:], tmf[tc % 2][:, 128 * dc:128 * dc + 128], identf[:]),
                reads=[tmfk, "identf"], writes=[pfk])
            p.v(lambda e, dc=dc, PF=PF: e.tensor_copy(xtf[dc % 2][:], PF[:]),
                reads=[pfk], writes=[("xtf", dc % 2)])
            p.t(lambda e, tc=tc, dc=dc: e.matmul(
                P2[:, 0:8], xtf[dc % 2][:], gate_sb[:, 8 * dc:8 * dc + 8],
                start=(dc == 0), stop=(dc == 7)),
                reads=[("xtf", dc % 2), "gate_sb"], writes=["P2"])
        p.v(lambda e, tc=tc: e.tensor_copy(sc[:, 8 * tc:8 * tc + 8], P2[:, 0:8]),
            reads=["P2"], writes=["sc"])

    for tc in range(NT):
        scs = sc[:, 8 * tc:8 * tc + 8]
        for e_ in range(8):
            p.v(lambda e, scs=scs, e_=e_: e.tensor_scalar(
                tmp8[:], scs, scs[:, e_:e_ + 1], None, op0=OP.is_gt),
                reads=["sc"], writes=["tmp8"])
            p.v(lambda e, e_=e_: e.reduce_sum(rank[:, e_:e_ + 1], tmp8[:], axis=AX),
                reads=["tmp8"], writes=["rank"])
            if e_ > 0:
                p.v(lambda e, scs=scs, e_=e_: e.tensor_scalar(
                    t8b[:], scs, scs[:, e_:e_ + 1], None, op0=OP.is_equal),
                    reads=["sc"], writes=["t8b"])
                p.v(lambda e, e_=e_: e.reduce_sum(c1[:], t8b[:, 0:e_], axis=AX),
                    reads=["t8b"], writes=["c1"])
                p.v(lambda e, e_=e_: e.tensor_add(
                    rank[:, e_:e_ + 1], rank[:, e_:e_ + 1], c1[:]),
                    reads=["rank", "c1"], writes=["rank"])
        p.v(lambda e: e.tensor_scalar(ind1[:], rank[:], 0.0, None, op0=OP.is_equal),
            reads=["rank"], writes=["ind1"])
        p.v(lambda e: e.tensor_scalar(ind2[:], rank[:], 1.0, None, op0=OP.is_equal),
            reads=["rank"], writes=["ind2"])
        p.v(lambda e, scs=scs: e.tensor_mul(tmp8[:], scs, ind1[:]),
            reads=["sc", "ind1"], writes=["tmp8"])
        p.v(lambda e: e.reduce_sum(c1[:], tmp8[:], axis=AX),
            reads=["tmp8"], writes=["c1"])
        p.v(lambda e, scs=scs: e.tensor_mul(tmp8[:], scs, ind2[:]),
            reads=["sc", "ind2"], writes=["tmp8"])
        p.v(lambda e: e.reduce_sum(c2[:], tmp8[:], axis=AX),
            reads=["tmp8"], writes=["c2"])
        p.v(lambda e: e.tensor_sub(c3[:], c2[:], c1[:]),
            reads=["c1", "c2"], writes=["c3"])
        p.s(lambda e: e.activation(c2[:], c3[:], AF.Sigmoid),
            reads=["c3"], writes=["c2"])
        p.v(lambda e: e.tensor_scalar(c1[:], c2[:], -1.0, 1.0, op0=OP.mult, op1=OP.add),
            reads=["c2"], writes=["c1"])
        p.v(lambda e: e.tensor_scalar_mul(t8b[:], ind2[:], c2[:]),
            reads=["ind2", "c2"], writes=["t8b"])
        p.v(lambda e, tc=tc: e.scalar_tensor_tensor(
            wE[:, 8 * tc:8 * tc + 8], ind1[:], c1[:], t8b[:],
            op0=OP.mult, op1=OP.add),
            reads=["ind1", "c1", "t8b"], writes=["wE"])
        PF, pfk = (P5, "P5") if tc % 2 == 0 else (P7, "P7")
        p.t(lambda e, tc=tc, PF=PF: e.transpose(
            PF[0:8, 0:128], wE[:, 8 * tc:8 * tc + 8], identf[:]),
            reads=["wE", "identf"], writes=[pfk])
        p.v(lambda e, tc=tc, PF=PF: e.tensor_copy(
            wET[:, 128 * tc:128 * tc + 128], PF[0:8, 0:128]),
            reads=[pfk], writes=["wET"])

    for tc in range(NT):
        for oh in range(2):
            PS, pk = (P0, "P0") if oh == 0 else (P1, "P1")
            p.t(lambda e, tc=tc, oh=oh, PS=PS: e.matmul(
                PS[:], wET[:, 128 * tc:128 * tc + 128],
                lnb_sb[:, 512 * oh:512 * oh + 512], start=True, stop=True),
                reads=["wET", "lnb_sb"], writes=[pk])
            p.v(lambda e, tc=tc, oh=oh, PS=PS: e.tensor_copy(
                ystage[tc % 2][:, 512 * oh:512 * oh + 512], PS[:]),
                reads=[pk], writes=[("ystage", tc % 2)])
        p.dma("sync", lambda e, tc=tc: e.dma_start(
            out=acc_d[128 * tc:128 * tc + 128, :], in_=ystage[tc % 2][:]),
            reads=[("ystage", tc % 2)], writes=[("acc", tc)])

    nexp = 8 if phases >= 4 else 0
    for ex in range(nexp):
        exk = ex % 2
        p.dma("sync", lambda e, ex=ex: e.dma_start(
            out=b1sb[ex % 2][:], in_=b1t_in[ex]), writes=[("b1sb", exk)])
        p.dma("sync", lambda e, ex=ex: e.dma_start(
            out=gbc[ex % 2][:], in_=lng_bc_in[ex]), writes=[("gbc", exk)])
        p.dma("sync", lambda e, ex=ex: e.dma_start(
            out=b2f[:], in_=b2_in[ex:ex + 1, :]), writes=["b2f"])
        p.v(lambda e: e.tensor_copy(b2b[:], b2f[:]), reads=["b2f"], writes=["b2b"])
        for dc in range(8):
            for hh in range(2):
                p.dma("sync", lambda e, ex=ex, dc=dc, hh=hh: e.dma_start(
                    out=tmf[hh][:],
                    in_=w1_in[ex, 128 * dc:128 * dc + 128, D * hh:D * hh + D]),
                    writes=[("tmf", hh)])
                p.v(lambda e, dc=dc, hh=hh: e.tensor_copy(
                    w1b[dc][:, D * hh:D * hh + D], tmf[hh][:]),
                    reads=[("tmf", hh)], writes=[("w1b", dc)])
        for hc in range(16):
            p.dma("sync", lambda e, ex=ex, hc=hc: e.dma_start(
                out=tmf[hc % 2][:], in_=w2_in[ex, 128 * hc:128 * hc + 128, :]),
                writes=[("tmf", hc % 2)])
            p.s(lambda e, hc=hc: e.activation(w2b[hc][:], tmf[hc % 2][:], AF.Copy),
                reads=[("tmf", hc % 2)], writes=[("w2b", hc)])
        for tt in range(4):
            for hc in range(16):
                PS, pk = (P2, "P2") if hc % 2 == 0 else (P3, "P3")
                for dc in range(8):
                    p.t(lambda e, tt=tt, hc=hc, dc=dc, PS=PS: e.matmul(
                        PS[:], w1b[dc][:, 128 * hc:128 * hc + 128],
                        x1T[dc][:, 512 * tt:512 * tt + 512],
                        start=(dc == 0), stop=(dc == 7)),
                        reads=[("w1b", dc), ("x1T", dc)], writes=[pk])
                p.s(lambda e, hc=hc, PS=PS, exk=exk: e.activation(
                    hT[hc][:], PS[:], AF.Gelu, bias=b1sb[exk][:, hc:hc + 1]),
                    reads=[pk, ("b1sb", exk)], writes=[("hT", hc)])
            for ts4 in range(4):
                tc = 4 * tt + ts4
                for oh in range(2):
                    PS, pk = (P0, "P0") if oh == 0 else (P1, "P1")
                    for hc in range(16):
                        p.t(lambda e, ts4=ts4, oh=oh, hc=hc, PS=PS: e.matmul(
                            PS[:], hT[hc][:, 128 * ts4:128 * ts4 + 128],
                            w2b[hc][:, 512 * oh:512 * oh + 512],
                            start=(hc == 0), stop=False),
                            reads=[("hT", hc), ("w2b", hc)], writes=[pk])
                    p.t(lambda e, oh=oh, PS=PS: e.matmul(
                        PS[:], onesrow_b[:], b2b[0:1, 512 * oh:512 * oh + 512],
                        start=False, stop=True),
                        reads=["onesrow_b", "b2b"], writes=[pk])
                p.dma("sync", lambda e, tc=tc: e.dma_start(
                    out=x1r[tc % 2][:], in_=x1own[128 * tc:128 * tc + 128, :]),
                    reads=[("x1d", tc)], writes=[("x1r", tc % 2)])
                p.v(lambda e, tc=tc: e.scalar_tensor_tensor(
                    stile[:, 0:512], P0[:], 1.0, x1r[tc % 2][:, 0:512],
                    op0=OP.mult, op1=OP.add, accum_out=acc1a[:]),
                    reads=["P0", ("x1r", tc % 2)], writes=["stile", "acc1a"])
                p.v(lambda e, tc=tc: e.scalar_tensor_tensor(
                    stile[:, 512:1024], P1[:], 1.0, x1r[tc % 2][:, 512:1024],
                    op0=OP.mult, op1=OP.add, accum_out=acc1b[:]),
                    reads=["P1", ("x1r", tc % 2)], writes=["stile", "acc1b"])
                p.s(lambda e: e.activation(tm[0][:], stile[:], AF.Square,
                                           accum_out=acc2[:]),
                    reads=["stile"], writes=[("tm", 0), "acc2"])
                p.v(lambda e: e.tensor_add(acc1a[:], acc1a[:], acc1b[:]),
                    reads=["acc1a", "acc1b"], writes=["acc1a"])
                p.v(lambda e: e.tensor_scalar_mul(ln1[:], acc1a[:], 1.0 / D),
                    reads=["acc1a"], writes=["ln1"])
                p.v(lambda e: e.tensor_scalar_mul(ln2[:], acc2[:], 1.0 / D),
                    reads=["acc2"], writes=["ln2"])
                p.v(lambda e: e.tensor_mul(ln3[:], ln1[:], ln1[:]),
                    reads=["ln1"], writes=["ln3"])
                p.v(lambda e: e.tensor_sub(ln2[:], ln2[:], ln3[:]),
                    reads=["ln2", "ln3"], writes=["ln2"])
                p.s(lambda e: e.activation(ln3[:], ln2[:], AF.Sqrt, bias=eps6[:]),
                    reads=["ln2", "eps6"], writes=["ln3"])
                p.v(lambda e: e.reciprocal(ln2[:], ln3[:]), reads=["ln3"], writes=["ln2"])
                p.v(lambda e: e.tensor_scalar_mul(ln3[:], ln1[:], -1.0),
                    reads=["ln1"], writes=["ln3"])
                p.v(lambda e: e.tensor_scalar(y1[:], stile[:], ln3[:], ln2[:],
                                              op0=OP.add, op1=OP.mult),
                    reads=["stile", "ln3", "ln2"], writes=["y1"])
                p.g(lambda e, exk=exk: e.tensor_mul(y1[:], y1[:], gbc[exk][:]),
                    reads=["y1", ("gbc", exk)], writes=["y1"])
                col = 8 * tc + ex
                p.dma("sync", lambda e, tc=tc: e.dma_start(
                    out=accsb[tc % 2][:], in_=acc_d[128 * tc:128 * tc + 128, :]),
                    reads=[("acc", tc)], writes=[("accsb", tc % 2)])
                p.v(lambda e, col=col, tc=tc: e.scalar_tensor_tensor(
                    ystage[tc % 2][:], y1[:], wE[:, col:col + 1], accsb[tc % 2][:],
                    op0=OP.mult, op1=OP.add),
                    reads=["y1", "wE", ("accsb", tc % 2)], writes=[("ystage", tc % 2)])
                p.dma("sync", lambda e, tc=tc: e.dma_start(
                    out=acc_d[128 * tc:128 * tc + 128, :], in_=ystage[tc % 2][:]),
                    reads=[("ystage", tc % 2)], writes=[("acc", tc)])

    for tc in range(NT):
        p.dma("sync", lambda e, tc=tc: e.dma_start(
            out=accsb[tc % 2][:], in_=acc_d[128 * tc:128 * tc + 128, :]),
            reads=[("acc", tc)], writes=[("accsb", tc % 2)])
        p.dma("sync", lambda e, tc=tc: e.dma_start(
            out=x1r[tc % 2][:], in_=x1own[128 * tc:128 * tc + 128, :]),
            reads=[("x1d", tc)], writes=[("x1r", tc % 2)])
        p.v(lambda e, tc=tc: e.scalar_tensor_tensor(
            stile[:], accsb[tc % 2][:], 1.0, x1r[tc % 2][:],
            op0=OP.mult, op1=OP.add, accum_out=acc1a[:]),
            reads=[("accsb", tc % 2), ("x1r", tc % 2)], writes=["stile", "acc1a"])
        p.s(lambda e: e.activation(tm[0][:], stile[:], AF.Square, accum_out=acc2[:]),
            reads=["stile"], writes=[("tm", 0), "acc2"])
        p.v(lambda e: e.tensor_scalar_mul(ln1[:], acc1a[:], 1.0 / D),
            reads=["acc1a"], writes=["ln1"])
        p.v(lambda e: e.tensor_scalar_mul(ln2[:], acc2[:], 1.0 / D),
            reads=["acc2"], writes=["ln2"])
        p.v(lambda e: e.tensor_mul(ln3[:], ln1[:], ln1[:]),
            reads=["ln1"], writes=["ln3"])
        p.v(lambda e: e.tensor_sub(ln2[:], ln2[:], ln3[:]),
            reads=["ln2", "ln3"], writes=["ln2"])
        p.s(lambda e: e.activation(ln3[:], ln2[:], AF.Sqrt, bias=eps5[:]),
            reads=["ln2", "eps5"], writes=["ln3"])
        p.v(lambda e: e.reciprocal(ln2[:], ln3[:]), reads=["ln3"], writes=["ln2"])
        p.v(lambda e: e.tensor_scalar_mul(ln3[:], ln1[:], -1.0),
            reads=["ln1"], writes=["ln3"])
        p.v(lambda e, tc=tc: e.tensor_scalar(y1[:], stile[:], ln3[:], ln2[:],
                                             op0=OP.add, op1=OP.mult),
            reads=["stile", "ln3", "ln2"], writes=["y1"])
        p.dma("sync", lambda e, tc=tc: e.dma_start(
            out=x2own[128 * tc:128 * tc + 128, :], in_=y1[:]),
            reads=["y1"], writes=[("x2own", tc)])





def build_moe(phases=99):
    nc = bass.Bass("TRN2", target_bir_lowering=False, dynamic_dma_scratch_size=2048)
    p = Prog(nc)
    x0own = nc.dram_tensor("x0own", [HALF, D], F32, kind="ExternalInput").ap()
    paown = nc.dram_tensor("paown", [HALF, D], F32, kind="ExternalInput").ap()
    pbown = nc.dram_tensor("pbown", [HALF, D], F32, kind="ExternalInput").ap()
    gate_w = nc.dram_tensor("gate_w", [D, 8], F32, kind="ExternalInput").ap()
    w1_in = nc.dram_tensor("w1", [8, D, 2 * D], F32, kind="ExternalInput").ap()
    b1t_in = nc.dram_tensor("b1t", [8, 128, 16], F32, kind="ExternalInput").ap()
    w2_in = nc.dram_tensor("w2", [8, 2 * D, D], F32, kind="ExternalInput").ap()
    b2_in = nc.dram_tensor("b2", [8, D], F32, kind="ExternalInput").ap()
    lng_bc_in = nc.dram_tensor("lng_bc", [8, 128, D], F32, kind="ExternalInput").ap()
    lnb_in = nc.dram_tensor("lnb", [8, D], F32, kind="ExternalInput").ap()
    identb_in = nc.dram_tensor("identb_in", [128, 128], BF16, kind="ExternalInput").ap()
    ident_in = nc.dram_tensor("ident", [128, 128], F32, kind="ExternalInput").ap()
    x2own = nc.dram_tensor("x2own", [HALF, D], F32, kind="ExternalOutput").ap()


    ctx = ExitStack()
    pa_row = lambda tc: paown[128 * tc:128 * tc + 128, :]
    pb_row = lambda tc: pbown[128 * tc:128 * tc + 128, :]
    emit_moe(nc, p, ctx, "", x0own, pa_row, pb_row, gate_w, w1_in, b1t_in, w2_in, b2_in, lng_bc_in, lnb_in, identb_in, ident_in, x2own, phases=phases)
    p.emit()
    ctx.close()
    return nc

def run_moe(x0_full, pa, pb, gate_w, w1, b1, w2, b2, ln_g, ln_b):
    nc = build_moe()
    in_maps = []
    for c in range(8):
        b, par = c // 2, c % 2
        sl = slice(par * HALF, (par + 1) * HALF)
        in_maps.append({
            "x0own": np.ascontiguousarray(x0_full[b, sl]),
            "paown": np.ascontiguousarray(pa[b][sl]),
            "pbown": np.ascontiguousarray(pb[b][sl]),
            "gate_w": gate_w, "w1": w1,
            "b1t": np.ascontiguousarray(b1.reshape(8, 16, 128).transpose(0, 2, 1)),
            "w2": w2, "b2": b2,
            "lng_bc": np.ascontiguousarray(
                np.broadcast_to(ln_g[:, None, :], (8, 128, D))),
            "lnb": ln_b,
            "identb_in": np.eye(128).astype(BFD),
            "ident": np.eye(128, dtype=np.float32),
        })
    res = run_bass_kernel_spmd(nc, in_maps, core_ids=list(range(8)))
    x2 = np.zeros((B, T, D), np.float32)
    for c in range(8):
        b, par = c // 2, c % 2
        x2[b, par * HALF:(par + 1) * HALF] = res.results[c]["x2own"]
    return x2



def _l1f_consts(par):
    W0 = 4 * par
    c = {}
    pp = np.arange(128)[:, None]
    cc = np.arange(640)[None, :]
    c["mb"] = np.where((cc - pp < 0) | (cc - pp > 512), NEG, 0.0).astype(np.float32)
    jj = np.arange(128)[None, :]
    c["tri0"] = np.where(pp < jj, NEG, 0.0).astype(np.float32)
    c["tri4"] = np.where(pp > jj, NEG, 0.0).astype(np.float32)
    kill = np.zeros((128, 240), np.float32)
    col = 0
    for ls in range(12):
        w = (W0 - 3 + ls) if ls < 7 else (W0 - 1 + (ls - 7))
        for qs in range(4):
            a0 = 128 * qs
            for kcr in range(5):
                kg = a0 + 128 * kcr + pp[:, 0] - 512
                if w == 0:
                    kill[:, col] = np.where(kg < 0, NEG, 0.0)
                col += 1
    c["kill"] = kill
    msk = np.ones((1, 512), np.float32)
    if par == 0:
        msk[0, 511] = 0.0
    c["mskrow"] = msk
    DH, WIN = 64, 512
    half = np.arange(0, DH, 2, dtype=np.float64)
    inv = 1.0 / 10000.0 ** (half / DH)
    base = (half + 0.4 * DH) / (1.4 * DH)
    tabs = np.zeros((8, 4, 128, 512), np.float32)
    for wi in range(8):
        w = W0 - 4 + wi
        t = np.arange(w * 512, w * 512 + 512, dtype=np.float64)
        freqs = t[:, None] * inv[None, :]
        power = (t - T // 2) / float(WIN // 2)
        scale = base[None, :] ** power[:, None]
        cos = np.cos(freqs); sin = np.sin(freqs)
        qc = np.concatenate([cos * scale, cos * scale], 1).T
        qs_ = np.concatenate([-sin * scale, sin * scale], 1).T
        kc = np.concatenate([cos / scale, cos / scale], 1).T
        ks_ = np.concatenate([-sin / scale, sin / scale], 1).T
        QK = 8.0
        tabs[wi, 0] = np.tile(qc * QK, (2, 1))
        tabs[wi, 1] = np.tile(qs_ * QK, (2, 1))
        tabs[wi, 2] = np.tile(kc, (2, 1))
        tabs[wi, 3] = np.tile(ks_, (2, 1))
    c["ropetabs"] = tabs
    P = np.zeros((128, 128), np.float32)
    for h in range(2):
        for j in range(32):
            P[64 * h + 32 + j, 64 * h + j] = 1.0
            P[64 * h + j, 64 * h + 32 + j] = 1.0
    c["swp"] = P
    return c


def _l1f_blk2():
    a = np.zeros((128, 2), np.float32)
    a[0:64, 0] = 1.0
    a[64:128, 1] = 1.0
    return a


def _l1f_exp2():
    a = np.zeros((2, 128), np.float32)
    a[0, 0:64] = 1.0
    a[1, 64:128] = 1.0
    return a


def _l1f_x3ext(x3_full_b, par):
    W0 = 4 * par
    g0 = (W0 - 4) * 512 - 1
    ext = np.zeros((4097, D), np.float32)
    lo = max(0, -g0)
    ext[lo:] = x3_full_b[g0 + lo:g0 + 4097]
    return np.ascontiguousarray(ext.T)



def emit_l1f(nc, p, ctx, pfx, x3extT, x3own_row, wqkv_in, qsc_in, ksc_in, wo_in, lng_in, lnb_in, w1a_in, w1g_in, b1a_in, b1g_in, w2_in, b2_in, mb_in, tri0_in, tri4_in, kill_in, mskrow_in, tabs_in, swp_in, ident_in, identb_in, blk2_in, exp2_in, outown, extk="x3extT_noop", phases=99):
    HID = 2730
    NHC = 22
    W = 512
    extb_d = nc.dram_tensor(pfx + "extb_d", [D, 4097], BF16, kind="Internal").ap()
    y1f_d = nc.dram_tensor(pfx + "y1f_d", [D, 3584], F32, kind="Internal").ap()
    y1b_d = nc.dram_tensor(pfx + "y1b_d", [D, 3584], BF16, kind="Internal").ap()
    x4f_d = nc.dram_tensor(pfx + "x4f_d", [D, 3073], F32, kind="Internal").ap()
    x4b_d = nc.dram_tensor(pfx + "x4b_d", [D, 3073], BF16, kind="Internal").ap()
    y2f_d = nc.dram_tensor(pfx + "y2f_d", [D, 2560], F32, kind="Internal").ap()
    y2b_d = nc.dram_tensor(pfx + "y2b_d", [D, 2560], BF16, kind="Internal").ap()
    x5f_d = nc.dram_tensor(pfx + "x5f_d", [D, 2048], F32, kind="Internal").ap()
    sb = lambda name, shape, dt: ctx.enter_context(nc.sbuf_tensor(pfx + name, shape, dt))
    psa = lambda name, shape, dt=F32: ctx.enter_context(nc.psum_tensor(pfx + name, shape, dt))

    xsp = [sb(f"xsp{i}", [128, 1025], BF16) for i in range(8)]
    qf = sb("qf", [128, 2048], F32)
    kf = sb("kf", [128, 2048], F32)
    rsc = sb("rsc", [128, 2048], F32)
    tabq = [sb(f"tabq{i}", [128, 512], F32) for i in range(2)]
    tabk = [sb(f"tabk{i}", [128, 512], F32) for i in range(2)]
    swpf = sb("swpf", [128, 128], F32)
    blk2 = sb("blk2", [128, 2], F32)
    exp2 = sb("exp2", [2, 128], F32)
    ss2 = sb("ss2", [2, 512], F32)
    scs = sb("scs", [128, 2], F32)

    mbm = sb("mbm", [128, 640], F32)
    tri0 = sb("tri0_s", [128, 128], F32)
    tri4 = sb("tri4_s", [128, 128], F32)
    killf = sb("killf_s", [128, 240], F32)
    mskrow = sb("mskrow_s", [1, 512], F32)
    scrA = rsc[:, 0:640]
    mcol = sb("mcol", [128, 1], F32)
    mrow = sb("mrowf", [1, 128], BF16)
    pstage = sb("pstagef", [128, 128], F32)
    ptb = [sb(f"ptbf{i}", [128, 128], BF16) for i in range(5)]
    onescol_b = sb("onescol_bf", [128, 1], BF16)

    atout = [qf[:, 512 * i:512 * i + 512] for i in range(4)]
    atoutb = [sb(f"atoutb{i}", [128, 512], BF16) for i in range(4)]
    xwf = [sb(f"xwf{i}", [128, 512], F32) for i in range(2)]
    yst = sb("yst", [128, 512], F32)
    ystb = sb("ystb", [128, 512], BF16)
    onesrow_b = sb("onesrow_bf", [1, 128], BF16)
    onesrow_f = sb("onesrow_ff", [1, 128], F32)
    onescol_f = sb("onescol_ff", [128, 1], F32)
    identb = sb("identbf", [128, 128], BF16)
    identf = sb("identff", [128, 128], F32)
    eps5b = sb("eps5b", [128, 1], F32)
    zT = [sb(f"zT{i}", [128, 1024], BF16) for i in range(8)]
    hTf = [sb(f"hTf{i}", [128, 1024], BF16) for i in range(NHC)]
    w2b = [sb(f"w2bf{i}", [128, D], BF16) for i in range(NHC)]
    wbchunk = [sb(f"wbc{i}", [128, 1024], BF16) for i in range(2)]
    gsb = sb("gsb", [128, 1024], BF16)
    wqb = [zT[i][:, 0:512] for i in range(8)]
    wkb = [zT[i][:, 512:1024] for i in range(8)]
    wvb = [hTf[i][:, 0:512] for i in range(8)]
    wob = [hTf[8 + i][:, :] for i in range(4)]
    qb = [hTf[12 + i // 2][:, 512 * (i % 2):512 * (i % 2) + 512] for i in range(4)]
    kb = [hTf[14 + i][:, :] for i in range(4)]
    vb = [hTf[18 + i // 2][:, 512 * (i % 2):512 * (i % 2) + 512] for i in range(8)]
    QBK = lambda i: ("hTf", 12 + i // 2)
    VBK = lambda i: ("hTf", 18 + i // 2)
    wstr = [qf[:, 0:1024], qf[:, 1024:2048]]
    WSK = [(("qf", 0), ("qf", 1)), (("qf", 2), ("qf", 3))]
    zin = [kf[:, 0:1024], kf[:, 1024:2048]]
    ZIK = [(("kf", 0), ("kf", 1)), (("kf", 2), ("kf", 3))]
    sqd = rsc
    zsq = rsc[:, 0:512]
    b1as = sb("b1as", [128, 22], F32)
    b1gs = sb("b1gs", [128, 22], F32)
    lngs = sb("lngs", [128, 8], F32)
    lnbs = sb("lnbs", [128, 8], F32)
    b2s = sb("b2s", [128, 8], F32)
    statr = sb("statr", [1, 2048], F32)
    statq = sb("statq", [1, 2048], F32)
    strow = sb("strow", [1, 1024], F32)
    srA, srB = statr, statq
    yres = [sb(f"yres{i}", [128, 512], F32) for i in range(2)]
    fin = sb("finf", [128, D], F32)
    finb = sb("finb", [128, 128], F32)
    x3r = [sb("x3rf0", [128, D], F32)] * 2
    acc1 = sb("acc1f", [128, 1], F32)
    acc2 = sb("acc2f", [128, 1], F32)
    ln1 = sb("ln1f", [128, 1], F32)
    ln2 = sb("ln2f", [128, 1], F32)
    ln3 = sb("ln3f", [128, 1], F32)

    P0 = psa("P0", [128, 512])
    P1 = psa("P1", [128, 512])
    PA = psa("PA", [128, 640])
    P5 = psa("P5", [128, 128])
    P7 = psa("P7", [128, 128])
    PR = psa("PR", [128, 512])
    PQ = psa("PQ", [1, 512])

    for o, i_, k in ((mbm[:], mb_in, "mbm"), (tri0[:], tri0_in, "tri0"),
                     (tri4[:], tri4_in, "tri4"), (killf[:], kill_in, "killf"),
                     (mskrow[:], mskrow_in, "mskrow"), (swpf[:], swp_in, "swpf"),
                     (identf[:], ident_in, "identf"), (identb[:], identb_in, "identb")):
        p.dma("sync", lambda e, o=o, i_=i_: e.dma_start(out=o, in_=i_), writes=[k])
    p.v(lambda e: e.memset(onesrow_b[:], 1.0), writes=["onesrow_b"])
    p.v(lambda e: e.memset(onesrow_f[:], 1.0), writes=["onesrow_f"])
    p.v(lambda e: e.memset(onescol_b[:], 1.0), writes=["onescol_b"])
    p.v(lambda e: e.memset(onescol_f[:], 1.0), writes=["onescol_f"])
    p.v(lambda e: e.memset(eps5b[:], 1e-5), writes=["eps5b"])
    p.dma("sync", lambda e: e.dma_start(out=blk2[:], in_=blk2_in), writes=["blk2"])
    p.dma("sync", lambda e: e.dma_start(out=exp2[:], in_=exp2_in), writes=["exp2"])

    for g in range(9):
        w_ = 512 if g < 8 else 1
        for dc in range(8):
            p.dma("sync", lambda e, g=g, dc=dc, w_=w_: e.dma_start(
                out=wstr[dc % 2][:, 0:w_],
                in_=x3extT[128 * dc:128 * dc + 128, 512 * g:512 * g + w_],
                allow_slow_non_contiguous=(w_ == 1)),
                reads=[extk], writes=list(WSK[dc % 2]))
            p.v(lambda e, dc=dc, w_=w_: e.tensor_copy(
                wbchunk[dc % 2][:, 0:w_], wstr[dc % 2][:, 0:w_]),
                reads=list(WSK[dc % 2]), writes=[("wbchunk", dc % 2)])
            p.dma("sync", lambda e, g=g, dc=dc, w_=w_: e.dma_start(
                out=extb_d[128 * dc:128 * dc + 128, 512 * g:512 * g + w_],
                in_=wbchunk[dc % 2][:, 0:w_],
                allow_slow_non_contiguous=(w_ == 1)),
                reads=[("wbchunk", dc % 2)], writes=["extb"])

    def load_bf(dst, dstk, src_ap, n=1024):
        p.dma("sync", lambda e, dst=dst, src_ap=src_ap, n=n: e.dma_start(
            out=wstr[0][:, 0:n], in_=src_ap), writes=list(WSK[0]))
        p.v(lambda e, dst=dst, n=n: e.tensor_copy(dst, wstr[0][:, 0:n]),
            reads=list(WSK[0]), writes=[dstk])

    def attn_layer(l, nwin, lslot0, tslot0, src_f, srcf_k, src_b, srcb_k,
                   src_co0, dst_f, dstf_k, dst_b, dstb_k, masked_slot0):
        for ic in range(8):
            load_bf(wqb[ic], ("zT", ic), wqkv_in[l, 128 * ic:128 * ic + 128, 0:512], 512)
            load_bf(wkb[ic], ("zT", ic), wqkv_in[l, 128 * ic:128 * ic + 128, 512:1024], 512)
            load_bf(wvb[ic], ("hTf", ic), wqkv_in[l, 128 * ic:128 * ic + 128, 1024:1536], 512)
        for ic in range(4):
            load_bf(wob[ic], ("hTf", 8 + ic), wo_in[l, 128 * ic:128 * ic + 128, :], 1024)
        p.dma("sync", lambda e, l=l: e.dma_start(out=scs[:, 0:1], in_=qsc_in[l]),
              writes=["scal"])
        p.dma("sync", lambda e, l=l: e.dma_start(out=scs[:, 1:2], in_=ksc_in[l]),
              writes=["scal"])

        def hsl(dc, base, wdt):
            if dc < 4:
                return xsp[dc][:, base + 1:base + 1 + wdt]
            return xsp[dc][:, base:base + wdt]

        def norm_rope(bufap, bufks, scol, ti0, ti1, out_fn, outk_fn):
            bks = list(bufks)
            p.s(lambda e: e.activation(sqd[:], bufap, AF.Square),
                reads=bks, writes=["rsc"])
            for pt in range(4):
                PS, pk2 = (PR, "PR") if pt % 2 == 0 else (PA[:, 0:512], "PA")
                p.t(lambda e, pt=pt, PS=PS: e.matmul(
                    PS[0:2, 0:512], blk2[:], sqd[:, 512 * pt:512 * pt + 512],
                    start=True, stop=True), reads=["blk2", "rsc"], writes=[pk2])
                p.v(lambda e, PS=PS: e.tensor_copy(ss2[:], PS[0:2, 0:512]),
                    reads=[pk2], writes=["ss2"])
                p.s(lambda e: e.activation(ss2[:], ss2[:], AF.Sqrt),
                    reads=["ss2"], writes=["ss2"])
                p.v(lambda e: e.tensor_scalar_max(ss2[:], ss2[:], 1e-12),
                    reads=["ss2"], writes=["ss2"])
                p.v(lambda e: e.reciprocal(ss2[:], ss2[:]), reads=["ss2"], writes=["ss2"])
                p.t(lambda e, PS=PS: e.matmul(
                    PS[0:128, 0:512], exp2[:], ss2[:], start=True, stop=True),
                    reads=["exp2", "ss2"], writes=[pk2])
                p.v(lambda e, pt=pt, PS=PS, bufap=bufap: e.tensor_mul(
                    bufap[:, 512 * pt:512 * pt + 512],
                    bufap[:, 512 * pt:512 * pt + 512], PS[0:128, 0:512]),
                    reads=bks + [pk2], writes=bks)
            p.v(lambda e, bufap=bufap, scol=scol: e.tensor_scalar_mul(
                bufap, bufap, scs[:, scol:scol + 1]),
                reads=bks + ["scal"], writes=bks)
            for pt in range(4):
                PS, pk2 = (P5, "P5") if pt % 2 == 0 else (P7, "P7")
                for su in range(4):
                    cc0 = 512 * pt + 128 * su
                    p.t(lambda e, cc0=cc0, PS=PS, bufap=bufap: e.matmul(
                        PS[:], swpf[:], bufap[:, cc0:cc0 + 128],
                        start=True, stop=True), reads=["swpf"] + bks, writes=[pk2])
                    p.v(lambda e, cc0=cc0, PS=PS: e.tensor_copy(
                        rsc[:, cc0:cc0 + 128], PS[:]), reads=[pk2], writes=["rsc"])
            for pt in range(4):
                p.v(lambda e, pt=pt, ti1=ti1: e.tensor_mul(
                    rsc[:, 512 * pt:512 * pt + 512],
                    rsc[:, 512 * pt:512 * pt + 512], ti1),
                    reads=["rsc", "tabs"], writes=["rsc"])
                p.v(lambda e, pt=pt, ti0=ti0, bufap=bufap: e.tensor_mul(
                    bufap[:, 512 * pt:512 * pt + 512],
                    bufap[:, 512 * pt:512 * pt + 512], ti0),
                    reads=bks + ["tabs"], writes=bks)
                p.v(lambda e, pt=pt, bufap=bufap: e.tensor_add(
                    bufap[:, 512 * pt:512 * pt + 512],
                    bufap[:, 512 * pt:512 * pt + 512],
                    rsc[:, 512 * pt:512 * pt + 512]),
                    reads=bks + ["rsc"], writes=bks)
            for pt in range(4):
                p.v(lambda e, pt=pt, bufap=bufap: e.tensor_copy(
                    out_fn(pt), bufap[:, 512 * pt:512 * pt + 512]),
                    reads=bks, writes=[outk_fn(pt)])

        for wi in range(-1, nwin):
            co = src_co0 + wi * W
            if wi > -1:
                for pt in range(4):
                    p.v(lambda e, pt=pt: e.tensor_copy(
                        kb[pt][:, 0:512], kb[pt][:, 512:1024]),
                        reads=[("hTf", 14 + pt)], writes=[("hTf", 14 + pt)])
                for i4 in range(4):
                    p.v(lambda e, i4=i4: e.tensor_copy(vb[i4], vb[4 + i4]),
                        reads=[VBK(4 + i4)], writes=[VBK(i4)])
            lo = 511 if wi == -1 else -1
            for dc in range(8):
                p.dma("sync", lambda e, dc=dc, co=co, lo=lo: e.dma_start(
                    out=xsp[dc][:, (512 if lo == 511 else 0):1025],
                    in_=src_b[128 * dc:128 * dc + 128,
                              co + (0 if lo == 511 else -512) - 1:co + 512]),
                    reads=[srcb_k], writes=[("xsp", dc)])
            ts_ = tslot0 + wi
            for r in range(2):
                p.dma("sync", lambda e, ts_=ts_, r=r: e.dma_start(
                    out=tabq[r][:], in_=tabs_in[ts_, r]), writes=["tabs"])
                p.dma("sync", lambda e, ts_=ts_, r=r: e.dma_start(
                    out=tabk[r][:], in_=tabs_in[ts_, 2 + r]), writes=["tabs"])
            if wi >= 0:
                for oc in range(4):
                    PS, pk = (P0, "P0") if oc % 2 == 0 else (P1, "P1")
                    for ic in range(8):
                        p.t(lambda e, oc=oc, ic=ic, PS=PS: e.matmul(
                            PS[:], wqb[ic][:, 128 * oc:128 * oc + 128],
                            hsl(ic, W, W), start=(ic == 0), stop=(ic == 7)),
                            reads=[("zT", ic), ("xsp", ic)], writes=[pk])
                    p.v(lambda e, oc=oc, PS=PS: e.tensor_copy(
                        qf[:, 512 * oc:512 * oc + 512], PS[:]),
                        reads=[pk], writes=[("qf", oc)])
            for oc in range(4):
                PS, pk = (P0, "P0") if oc % 2 == 0 else (P1, "P1")
                for ic in range(8):
                    p.t(lambda e, oc=oc, ic=ic, PS=PS: e.matmul(
                        PS[:], wkb[ic][:, 128 * oc:128 * oc + 128],
                        hsl(ic, W, W), start=(ic == 0), stop=(ic == 7)),
                        reads=[("zT", ic), ("xsp", ic)], writes=[pk])
                p.v(lambda e, oc=oc, PS=PS: e.tensor_copy(
                    kf[:, 512 * oc:512 * oc + 512], PS[:]),
                    reads=[pk], writes=[("kf", oc)])
            for tc4 in range(4):
                PS, pk = (P0, "P0") if tc4 % 2 == 0 else (P1, "P1")
                for ic in range(8):
                    p.t(lambda e, tc4=tc4, ic=ic, PS=PS: e.matmul(
                        PS[:], hsl(ic, W + 128 * tc4, 128), wvb[ic],
                        start=(ic == 0), stop=(ic == 7)),
                        reads=[("hTf", ic), ("xsp", ic)], writes=[pk])
                p.v(lambda e, tc4=tc4, PS=PS: e.tensor_copy(
                    vb[4 + tc4], PS[:]), reads=[pk], writes=[VBK(4 + tc4)])
            norm_rope(kf[:], [("kf", i) for i in range(4)], 1,
                      tabk[0][:], tabk[1][:],
                      lambda pt: kb[pt][:, 512:1024], lambda pt: ("hTf", 14 + pt))
            if wi < 0:
                continue
            norm_rope(qf[:], [("qf", i) for i in range(4)], 0,
                      tabq[0][:], tabq[1][:],
                      lambda pt: qb[pt], lambda pt: QBK(pt))
            for h in range(8):
                pt, hb = h // 2, 64 * (h % 2)
                for qs in range(4):
                    a0 = 128 * qs
                    p.t(lambda e, pt=pt, hb=hb, a0=a0: e.matmul(
                        PA[:, 0:512], qb[pt][hb:hb + 64, a0:a0 + 128],
                        kb[pt][hb:hb + 64, a0:a0 + 512], start=True, stop=True),
                        reads=[QBK(pt), ("hTf", 14 + pt)], writes=["PA"])
                    p.t(lambda e, pt=pt, hb=hb, a0=a0: e.matmul(
                        PA[:, 512:640], qb[pt][hb:hb + 64, a0:a0 + 128],
                        kb[pt][hb:hb + 64, a0 + 512:a0 + 640], start=True, stop=True),
                        reads=[QBK(pt), ("hTf", 14 + pt)], writes=["PA"])
                    p.v(lambda e: e.tensor_add(scrA, PA[:], mbm[:]),
                        reads=["PA", "mbm"], writes=["rsc"])
                    p.v(lambda e: e.reduce_max(mcol[:], scrA, axis=AX, negate=True),
                        reads=["rsc"], writes=["mcol"])
                    p.t(lambda e: e.transpose(P5[0:1, 0:128], mcol[:], identf[:]),
                        reads=["mcol", "identf"], writes=["P5"])
                    p.v(lambda e: e.tensor_copy(mrow[:], P5[0:1, 0:128]),
                        reads=["P5"], writes=["mrow"])
                    for kcr in range(5):
                        PS, pk = (P5, "P5") if kcr % 2 == 0 else (P7, "P7")
                        p.t(lambda e, pt=pt, hb=hb, a0=a0, kcr=kcr, PS=PS: e.matmul(
                            PS[:], kb[pt][hb:hb + 64, a0 + 128 * kcr:a0 + 128 * kcr + 128],
                            qb[pt][hb:hb + 64, a0:a0 + 128], start=True, stop=False),
                            reads=[("hTf", 14 + pt), QBK(pt)], writes=[pk])
                        p.t(lambda e, PS=PS: e.matmul(
                            PS[:], onesrow_b[:], mrow[:], start=False, stop=True),
                            reads=["onesrow_b", "mrow"], writes=[pk])
                        kcol = ((lslot0 + wi) * 4 + qs) * 5 + kcr
                        if kcr == 0 or kcr == 4:
                            tri = tri0 if kcr == 0 else tri4
                            trik = "tri0" if kcr == 0 else "tri4"
                            p.v(lambda e, tri=tri, PS=PS: e.tensor_add(
                                pstage[:], PS[:], tri[:]),
                                reads=[pk, trik], writes=["pstage"])
                            src, srck = pstage, "pstage"
                        else:
                            src, srck = PS, pk
                        p.s(lambda e, kcr=kcr, kcol=kcol, src=src: e.activation(
                            ptb[kcr][:], src[:], AF.Exp,
                            bias=killf[:, kcol:kcol + 1]),
                            reads=[srck, "killf"], writes=[("ptb", kcr)])
                    for kcr in range(5):
                        vt = qs + kcr
                        p.t(lambda e, vt=vt, h=h, kcr=kcr: e.matmul(
                            P0[0:64, 0:128], vb[vt][:, 64 * h:64 * h + 64],
                            ptb[kcr][:], start=(kcr == 0), stop=(kcr == 4)),
                            reads=[VBK(vt), ("ptb", kcr)], writes=["P0"])
                    for kcr in range(5):
                        p.t(lambda e, kcr=kcr: e.matmul(
                            P1[0:1, 0:128], onescol_b[:], ptb[kcr][:],
                            start=(kcr == 0), stop=(kcr == 4)),
                            reads=["onescol_b", ("ptb", kcr)], writes=["P1"])
                    p.v(lambda e, pt=pt, hb=hb, a0=a0: e.tensor_copy(
                        atout[pt][hb:hb + 64, a0:a0 + 128], P0[0:64, 0:128]),
                        reads=["P0"], writes=[("qf", pt)])
                    srX, srk = (srA, "statr") if h % 2 == 0 else (srB, "statq")
                    p.v(lambda e, pt=pt, a0=a0, srX=srX: e.tensor_copy(
                        srX[0:1, 512 * pt + a0:512 * pt + a0 + 128],
                        P1[0:1, 0:128]),
                        reads=["P1"], writes=[srk])
            p.v(lambda e: e.reciprocal(srA[:], srA[:]), reads=["statr"], writes=["statr"])
            p.v(lambda e: e.reciprocal(srB[:], srB[:]), reads=["statq"], writes=["statq"])
            for h in range(8):
                pt, hb = h // 2, 64 * (h % 2)
                srX, srk = (srA, "statr") if h % 2 == 0 else (srB, "statq")
                PS, pk2 = (PR, "PR") if h % 2 == 0 else (PA[:, 0:512], "PA")
                p.t(lambda e, pt=pt, srX=srX, PS=PS: e.matmul(
                    PS[0:64, 0:512], onesrow_f[0:1, 0:64],
                    srX[0:1, 512 * pt:512 * pt + 512],
                    start=True, stop=True), reads=["onesrow_f", srk], writes=[pk2])
                p.v(lambda e, pt=pt, hb=hb, PS=PS: e.tensor_mul(
                    atout[pt][hb:hb + 64, :], atout[pt][hb:hb + 64, :],
                    PS[0:64, 0:512]),
                    reads=[("qf", pt), pk2], writes=[("qf", pt)])
            for pt in range(4):
                p.v(lambda e, pt=pt: e.tensor_copy(atoutb[pt][:], atout[pt]),
                    reads=[("qf", pt)], writes=[("atoutb", pt)])
            if masked_slot0 and wi == 0:
                p.t(lambda e: e.matmul(PR[:], onesrow_f[:], mskrow[:],
                                       start=True, stop=True),
                    reads=["onesrow_f", "mskrow"], writes=["PR"])
            for dc in range(8):
                PS, pk = (P0, "P0") if dc % 2 == 0 else (P1, "P1")
                for ic in range(4):
                    p.t(lambda e, dc=dc, ic=ic, PS=PS: e.matmul(
                        PS[:], wob[ic][:, 128 * dc:128 * dc + 128], atoutb[ic][:],
                        start=(ic == 0), stop=(ic == 3)),
                        reads=[("wob", ic), ("atoutb", ic)], writes=[pk])
                p.dma("sync", lambda e, dc=dc, co=co: e.dma_start(
                    out=xwf[dc % 2][:], in_=src_f[128 * dc:128 * dc + 128, co:co + 512]),
                    reads=[srcf_k], writes=[("xwf", dc % 2)])
                p.v(lambda e, dc=dc, PS=PS: e.scalar_tensor_tensor(
                    yst[:], PS[:], 1.0, xwf[dc % 2][:], op0=OP.mult, op1=OP.add),
                    reads=[pk, ("xwf", dc % 2)], writes=["yst"])
                if masked_slot0 and wi == 0:
                    p.v(lambda e: e.tensor_mul(yst[:], yst[:], PR[:]),
                        reads=["yst", "PR"], writes=["yst"])
                p.dma("sync", lambda e, dc=dc, wi=wi: e.dma_start(
                    out=dst_f[128 * dc:128 * dc + 128, W * wi:W * wi + W],
                    in_=yst[:]), reads=["yst"], writes=[dstf_k])
                p.v(lambda e: e.tensor_copy(ystb[:], yst[:]),
                    reads=["yst"], writes=["ystb"])
                p.dma("sync", lambda e, dc=dc, wi=wi: e.dma_start(
                    out=dst_b[128 * dc:128 * dc + 128, W * wi:W * wi + W],
                    in_=ystb[:]), reads=["ystb"], writes=[dstb_k])

    def ffn_phase(l, src_f, srck_f, src_off, n_tokens, dst_f, dstk_f,
                  dst_b, dstk_b):
        p.dma("sync", lambda e, l=l: e.dma_start(out=b1as[:], in_=b1a_in[l]),
              writes=["b1as"])
        p.dma("sync", lambda e, l=l: e.dma_start(out=b1gs[:], in_=b1g_in[l]),
              writes=["b1gs"])
        p.dma("sync", lambda e, l=l: e.dma_start(out=lngs[:], in_=lng_in[l]),
              writes=["lngs"])
        p.dma("sync", lambda e, l=l: e.dma_start(out=lnbs[:], in_=lnb_in[l]),
              writes=["lnbs"])
        p.dma("sync", lambda e, l=l: e.dma_start(out=b2s[:], in_=b2_in[l]),
              writes=["b2s"])
        for hc in range(NHC):
            hw = 128 if hc < 21 else 42
            p.dma("sync", lambda e, l=l, hc=hc, hw=hw: e.dma_start(
                out=wstr[0][0:hw, :], in_=w2_in[l, 128 * hc:128 * hc + hw, :]),
                writes=list(WSK[0]))
            p.v(lambda e, hc=hc, hw=hw: e.tensor_copy(
                w2b[hc][0:hw, :], wstr[0][0:hw, :]),
                reads=list(WSK[0]), writes=[("w2b", hc)])
        slabs = []
        off = 0
        while off < n_tokens:
            ns = min(1024, n_tokens - off)
            slabs.append((off, ns))
            off += ns
        for off0, ns in slabs:
            cgs = []
            o2 = 0
            while o2 < ns:
                cw = min(512, ns - o2)
                cgs.append((o2, cw))
                o2 += cw
            for cg, cw in cgs:
                for dc in range(8):
                    sh = 1 if dc >= 4 else 0
                    p.dma("sync", lambda e, dc=dc, cg=cg, cw=cw, sh=sh, off0=off0: e.dma_start(
                        out=zin[dc % 2][:, 0:cw],
                        in_=src_f[128 * dc:128 * dc + 128,
                                  src_off + off0 + cg - sh:src_off + off0 + cg - sh + cw],
                        allow_slow_non_contiguous=(cw == 1)),
                        reads=[srck_f], writes=list(ZIK[dc % 2]))
                    p.s(lambda e, dc=dc, cw=cw: e.activation(
                        zsq[:, 0:cw], zin[dc % 2][:, 0:cw], AF.Square),
                        reads=list(ZIK[dc % 2]), writes=["rsc"])
                    p.t(lambda e, dc=dc, cw=cw: e.matmul(
                        PQ[0:1, 0:cw], onescol_f[:], zin[dc % 2][:, 0:cw],
                        start=(dc == 0), stop=(dc == 7)),
                        reads=list(ZIK[dc % 2]) + ["onescol_f"], writes=["PQ"])
                    p.t(lambda e, dc=dc, cw=cw: e.matmul(
                        PR[0:1, 0:cw], onescol_f[:], zsq[:, 0:cw],
                        start=(dc == 0), stop=(dc == 7)),
                        reads=["rsc", "onescol_f"], writes=["PR"])
                p.v(lambda e, cg=cg, cw=cw: e.tensor_copy(
                    statr[0:1, cg:cg + cw], PQ[0:1, 0:cw]),
                    reads=["PQ"], writes=["statr"])
                p.v(lambda e, cg=cg, cw=cw: e.tensor_copy(
                    statq[0:1, cg:cg + cw], PR[0:1, 0:cw]),
                    reads=["PR"], writes=["statq"])
            p.v(lambda e, ns=ns: e.tensor_scalar_mul(statr[0:1, 0:ns],
                                                     statr[0:1, 0:ns], 1.0 / D),
                reads=["statr"], writes=["statr"])
            p.v(lambda e, ns=ns: e.tensor_scalar_mul(statq[0:1, 0:ns],
                                                     statq[0:1, 0:ns], 1.0 / D),
                reads=["statq"], writes=["statq"])
            p.v(lambda e, ns=ns: e.tensor_mul(strow[0:1, 0:ns], statr[0:1, 0:ns],
                                              statr[0:1, 0:ns]),
                reads=["statr"], writes=["strow"])
            p.v(lambda e, ns=ns: e.tensor_sub(statq[0:1, 0:ns], statq[0:1, 0:ns],
                                              strow[0:1, 0:ns]),
                reads=["statq", "strow"], writes=["statq"])
            p.s(lambda e, ns=ns: e.activation(statq[0:1, 0:ns], statq[0:1, 0:ns],
                                              AF.Sqrt, bias=eps5b[0:1, :]),
                reads=["statq", "eps5b"], writes=["statq"])
            p.v(lambda e, ns=ns: e.reciprocal(statq[0:1, 0:ns], statq[0:1, 0:ns]),
                reads=["statq"], writes=["statq"])
            p.v(lambda e, ns=ns: e.tensor_mul(strow[0:1, 0:ns], statr[0:1, 0:ns],
                                              statq[0:1, 0:ns]),
                reads=["statr", "statq"], writes=["strow"])
            p.v(lambda e, ns=ns: e.tensor_scalar_mul(strow[0:1, 0:ns],
                                                     strow[0:1, 0:ns], -1.0),
                reads=["strow"], writes=["strow"])
            for cg, cw in cgs:
                p.t(lambda e, cg=cg, cw=cw: e.matmul(
                    PR[:, 0:cw], onesrow_f[:], statq[0:1, cg:cg + cw],
                    start=True, stop=True), reads=["onesrow_f", "statq"], writes=["PR"])
                p.t(lambda e, cg=cg, cw=cw: e.matmul(
                    PA[:, 0:cw], onesrow_f[:], strow[0:1, cg:cg + cw],
                    start=True, stop=True), reads=["onesrow_f", "strow"], writes=["PA"])
                for dc in range(8):
                    sh = 1 if dc >= 4 else 0
                    p.dma("sync", lambda e, dc=dc, cg=cg, cw=cw, sh=sh, off0=off0: e.dma_start(
                        out=zin[dc % 2][:, 0:cw],
                        in_=src_f[128 * dc:128 * dc + 128,
                                  src_off + off0 + cg - sh:src_off + off0 + cg - sh + cw],
                        allow_slow_non_contiguous=(cw == 1)),
                        reads=[srck_f], writes=list(ZIK[dc % 2]))
                    p.v(lambda e, dc=dc, cw=cw: e.tensor_mul(
                        zin[dc % 2][:, 0:cw], zin[dc % 2][:, 0:cw], PR[:, 0:cw]),
                        reads=list(ZIK[dc % 2]) + ["PR"], writes=list(ZIK[dc % 2]))
                    p.v(lambda e, dc=dc, cw=cw: e.tensor_add(
                        zin[dc % 2][:, 0:cw], zin[dc % 2][:, 0:cw], PA[:, 0:cw]),
                        reads=list(ZIK[dc % 2]) + ["PA"], writes=list(ZIK[dc % 2]))
                    p.v(lambda e, dc=dc, cg=cg, cw=cw: e.tensor_scalar(
                        zT[dc][:, cg:cg + cw], zin[dc % 2][:, 0:cw],
                        lngs[:, dc:dc + 1], lnbs[:, dc:dc + 1],
                        op0=OP.mult, op1=OP.add),
                        reads=list(ZIK[dc % 2]) + ["lngs", "lnbs"], writes=[("zT", dc)])
            for hc in range(NHC):
                hw = 128 if hc < 21 else 42
                p.dma("sync", lambda e, l=l, hc=hc, hw=hw: e.dma_start(
                    out=wstr[0][:, 0:8 * hw].rearrange("p (a c) -> p a c", a=8),
                    in_=w1g_in[l][:, 128 * hc:128 * hc + hw].rearrange(
                        "(a p) c -> p a c", p=128)),
                    writes=list(WSK[0]))
                p.v(lambda e, hw=hw: e.tensor_copy(
                    wbchunk[0][:, 0:8 * hw], wstr[0][:, 0:8 * hw]),
                    reads=list(WSK[0]), writes=[("wbchunk", 0)])
                p.dma("sync", lambda e, l=l, hc=hc, hw=hw: e.dma_start(
                    out=wstr[1][:, 0:8 * hw].rearrange("p (a c) -> p a c", a=8),
                    in_=w1a_in[l][:, 128 * hc:128 * hc + hw].rearrange(
                        "(a p) c -> p a c", p=128)),
                    writes=list(WSK[1]))
                p.v(lambda e, hw=hw: e.tensor_copy(
                    wbchunk[1][:, 0:8 * hw], wstr[1][:, 0:8 * hw]),
                    reads=list(WSK[1]), writes=[("wbchunk", 1)])
                for cg, cw in cgs:
                    for ic in range(8):
                        p.t(lambda e, hc=hc, hw=hw, ic=ic, cg=cg, cw=cw: e.matmul(
                            P0[0:hw, 0:cw], wbchunk[0][:, hw * ic:hw * ic + hw],
                            zT[ic][:, cg:cg + cw], start=(ic == 0), stop=(ic == 7)),
                            reads=[("wbchunk", 0), ("zT", ic)], writes=["P0"])
                    p.s(lambda e, hc=hc, hw=hw, cg=cg, cw=cw: e.activation(
                        gsb[0:hw, cg:cg + cw], P0[0:hw, 0:cw], AF.Gelu,
                        bias=b1gs[0:hw, hc:hc + 1]),
                        reads=["P0", "b1gs"], writes=["gsb"])
                    for ic in range(8):
                        p.t(lambda e, hc=hc, hw=hw, ic=ic, cg=cg, cw=cw: e.matmul(
                            P1[0:hw, 0:cw], wbchunk[1][:, hw * ic:hw * ic + hw],
                            zT[ic][:, cg:cg + cw], start=(ic == 0), stop=(ic == 7)),
                            reads=[("wbchunk", 1), ("zT", ic)], writes=["P1"])
                    p.v(lambda e, hc=hc, hw=hw, cg=cg, cw=cw: e.scalar_tensor_tensor(
                        hTf[hc][0:hw, cg:cg + cw], P1[0:hw, 0:cw],
                        b1as[0:hw, hc:hc + 1], gsb[0:hw, cg:cg + cw],
                        op0=OP.add, op1=OP.mult),
                        reads=["P1", "b1as", "gsb"], writes=[("hTf", hc)])
            for dc in range(8):
                for cg, cw in cgs:
                    for hc in range(NHC):
                        hw = 128 if hc < 21 else 42
                        p.t(lambda e, dc=dc, hc=hc, hw=hw, cg=cg, cw=cw: e.matmul(
                            PA[:, 0:cw], w2b[hc][0:hw, 128 * dc:128 * dc + 128],
                            hTf[hc][0:hw, cg:cg + cw],
                            start=(hc == 0), stop=(hc == NHC - 1)),
                            reads=[("w2b", hc), ("hTf", hc)], writes=["PA"])
                    p.dma("sync", lambda e, dc=dc, cg=cg, cw=cw, off0=off0: e.dma_start(
                        out=yres[dc % 2][:, 0:cw],
                        in_=src_f[128 * dc:128 * dc + 128,
                                  src_off + off0 + cg:src_off + off0 + cg + cw],
                        allow_slow_non_contiguous=(cw == 1)),
                        reads=[srck_f], writes=[("yres", dc % 2)])
                    p.v(lambda e, dc=dc, cg=cg, cw=cw: e.scalar_tensor_tensor(
                        yst[:, 0:cw], PA[:, 0:cw], b2s[:, dc:dc + 1],
                        yres[dc % 2][:, 0:cw], op0=OP.add, op1=OP.add),
                        reads=["PA", "b2s", ("yres", dc % 2)], writes=["yst"])
                    p.dma("sync", lambda e, dc=dc, cg=cg, cw=cw, off0=off0: e.dma_start(
                        out=dst_f[128 * dc:128 * dc + 128,
                                  off0 + cg:off0 + cg + cw], in_=yst[:, 0:cw],
                        allow_slow_non_contiguous=(cw == 1)),
                        reads=["yst"], writes=[dstk_f])
                    if dst_b is not None:
                        p.v(lambda e, cw=cw: e.tensor_copy(ystb[:, 0:cw],
                                                           yst[:, 0:cw]),
                            reads=["yst"], writes=["ystb"])
                        p.dma("sync", lambda e, dc=dc, cg=cg, cw=cw, off0=off0: e.dma_start(
                            out=dst_b[128 * dc:128 * dc + 128,
                                      off0 + cg:off0 + cg + cw], in_=ystb[:, 0:cw],
                            allow_slow_non_contiguous=(cw == 1)),
                            reads=["ystb"], writes=[dstk_b])

    if phases >= 2:
        attn_layer(0, 7, 0, 1, x3extT, extk, extb_d, "extb", 513,
                   y1f_d, ("y1f",), y1b_d, ("y1b",), False)
    if phases >= 3:
        ffn_phase(0, y1f_d, ("y1f",), 511, 3073, x4f_d, ("x4f",), x4b_d, ("x4b",))
    if phases >= 4:
        attn_layer(1, 5, 7, 3, x4f_d, ("x4f",), x4b_d, ("x4b",), 513,
                   y2f_d, ("y2f",), y2b_d, ("y2b",), True)
    if phases >= 5:
        ffn_phase(1, y2f_d, ("y2f",), 512, 2048, x5f_d, ("x5f",), None, None)
    if phases >= 6:
        for tc in range(16):
            for dc in range(8):
                PT, ptk = (P5, "P5") if dc % 2 == 0 else (P7, "P7")
                p.dma("sync", lambda e, tc=tc, dc=dc: e.dma_start(
                    out=finb[:], in_=x5f_d[128 * dc:128 * dc + 128,
                                           128 * tc:128 * tc + 128]),
                    reads=[("x5f",)], writes=["finb"])
                p.t(lambda e, PT=PT: e.transpose(PT[:], finb[:], identf[:]),
                    reads=["finb", "identf"], writes=[ptk])
                p.v(lambda e, dc=dc, PT=PT: e.tensor_copy(
                    fin[:, 128 * dc:128 * dc + 128], PT[:]),
                    reads=[ptk], writes=["fin"])
            p.row_dma("sync", x3r[tc % 2][:], x3own_row(tc),
                      reads=["x3src"], writes=[("x3r", 0)])
            p.v(lambda e, tc=tc: e.scalar_tensor_tensor(
                fin[:], fin[:], 1.0, x3r[tc % 2][:], op0=OP.mult, op1=OP.add,
                accum_out=acc1[:]),
                reads=["fin", ("x3r", 0)], writes=["fin", "acc1"])
            p.s(lambda e: e.activation(rsc[:, 0:1024], fin[:], AF.Square,
                                       accum_out=acc2[:]),
                reads=["fin"], writes=["rsc", "acc2"])
            p.v(lambda e: e.tensor_scalar_mul(ln1[:], acc1[:], 1.0 / D),
                reads=["acc1"], writes=["ln1"])
            p.v(lambda e: e.tensor_scalar_mul(ln2[:], acc2[:], 1.0 / D),
                reads=["acc2"], writes=["ln2"])
            p.v(lambda e: e.tensor_mul(ln3[:], ln1[:], ln1[:]),
                reads=["ln1"], writes=["ln3"])
            p.v(lambda e: e.tensor_sub(ln2[:], ln2[:], ln3[:]),
                reads=["ln2", "ln3"], writes=["ln2"])
            p.s(lambda e: e.activation(ln3[:], ln2[:], AF.Sqrt, bias=eps5b[:]),
                reads=["ln2", "eps5b"], writes=["ln3"])
            p.v(lambda e: e.reciprocal(ln2[:], ln3[:]), reads=["ln3"], writes=["ln2"])
            p.v(lambda e: e.tensor_scalar_mul(ln3[:], ln1[:], -1.0),
                reads=["ln1"], writes=["ln3"])
            p.v(lambda e, tc=tc: e.tensor_scalar(
                x3r[tc % 2][:], fin[:], ln3[:], ln2[:], op0=OP.add, op1=OP.mult),
                reads=["fin", "ln3", "ln2"], writes=[("x3r", 0)])
            p.dma("sync", lambda e, tc=tc: e.dma_start(
                out=outown[128 * tc:128 * tc + 128, :], in_=x3r[tc % 2][:]),
                reads=[("x3r", 0)], writes=[("out", tc)])





def build_l1f(phases=99):
    nc = bass.Bass("TRN2", target_bir_lowering=False, dynamic_dma_scratch_size=2048)
    p = Prog(nc)
    HID = 2730
    x3extT = nc.dram_tensor("x3extT", [D, 4097], F32, kind="ExternalInput").ap()
    x3own = nc.dram_tensor("x3own_in", [HALF, D], F32, kind="ExternalInput").ap()
    wqkv_in = nc.dram_tensor("wqkv", [2, D, 1536], F32, kind="ExternalInput").ap()
    qsc_in = nc.dram_tensor("qsc", [2, 128, 1], F32, kind="ExternalInput").ap()
    ksc_in = nc.dram_tensor("ksc", [2, 128, 1], F32, kind="ExternalInput").ap()
    wo_in = nc.dram_tensor("wo", [2, 512, D], F32, kind="ExternalInput").ap()
    lng_in = nc.dram_tensor("lng", [2, 128, 8], F32, kind="ExternalInput").ap()
    lnb_in = nc.dram_tensor("lnb", [2, 128, 8], F32, kind="ExternalInput").ap()
    w1a_in = nc.dram_tensor("w1a", [2, D, HID], F32, kind="ExternalInput").ap()
    w1g_in = nc.dram_tensor("w1g", [2, D, HID], F32, kind="ExternalInput").ap()
    b1a_in = nc.dram_tensor("b1a", [2, 128, 22], F32, kind="ExternalInput").ap()
    b1g_in = nc.dram_tensor("b1g", [2, 128, 22], F32, kind="ExternalInput").ap()
    w2_in = nc.dram_tensor("w2f", [2, HID, D], F32, kind="ExternalInput").ap()
    b2_in = nc.dram_tensor("b2f", [2, 128, 8], F32, kind="ExternalInput").ap()
    mb_in = nc.dram_tensor("mb", [128, 640], F32, kind="ExternalInput").ap()
    tri0_in = nc.dram_tensor("tri0", [128, 128], F32, kind="ExternalInput").ap()
    tri4_in = nc.dram_tensor("tri4", [128, 128], F32, kind="ExternalInput").ap()
    kill_in = nc.dram_tensor("killf", [128, 240], F32, kind="ExternalInput").ap()
    mskrow_in = nc.dram_tensor("mskrow", [1, 512], F32, kind="ExternalInput").ap()
    tabs_in = nc.dram_tensor("ropetabs", [8, 4, 128, 512], F32, kind="ExternalInput").ap()
    swp_in = nc.dram_tensor("swp", [128, 128], F32, kind="ExternalInput").ap()
    ident_in = nc.dram_tensor("ident", [128, 128], F32, kind="ExternalInput").ap()
    identb_in = nc.dram_tensor("identb_in", [128, 128], BF16, kind="ExternalInput").ap()
    blk2_in = nc.dram_tensor("blk2c", [128, 2], F32, kind="ExternalInput").ap()
    exp2_in = nc.dram_tensor("exp2c", [2, 128], F32, kind="ExternalInput").ap()
    outown = nc.dram_tensor("outown", [HALF, D], F32, kind="ExternalOutput").ap()



    ctx = ExitStack()
    x3own_row = lambda tc: x3own[128 * tc:128 * tc + 128, :]
    emit_l1f(nc, p, ctx, "", x3extT, x3own_row, wqkv_in, qsc_in, ksc_in, wo_in, lng_in, lnb_in, w1a_in, w1g_in, b1a_in, b1g_in, w2_in, b2_in, mb_in, tri0_in, tri4_in, kill_in, mskrow_in, tabs_in, swp_in, ident_in, identb_in, blk2_in, exp2_in, outown, phases=phases)
    p.emit()
    ctx.close()
    return nc

def run_l1f(x3_full, wqkv, q_scale, k_scale, wo, ln_g, ln_b, w1, b1, w2, b2):
    nc = build_l1f()
    w1a = np.ascontiguousarray(w1[:, :, :2730])
    w1g = np.ascontiguousarray(w1[:, :, 2730:])
    def colpack(v):
        n = v.shape[1]
        nb = (n + 127) // 128
        a = np.zeros((2, 128 * nb), np.float32)
        a[:, :n] = v
        return np.ascontiguousarray(a.reshape(2, nb, 128).transpose(0, 2, 1))
    in_maps = []
    for c in range(8):
        b, par = c // 2, c % 2
        cc = _l1f_consts(par)
        in_maps.append({
            "x3extT": _l1f_x3ext(x3_full[b], par),
            "x3own_in": np.ascontiguousarray(x3_full[b, par * HALF:(par + 1) * HALF]),
            "wqkv": wqkv,
            "qsc": np.ascontiguousarray(np.tile(q_scale, (1, 2))[:, :, None]),
            "ksc": np.ascontiguousarray(np.tile(k_scale, (1, 2))[:, :, None]),
            "wo": wo, "lng": colpack(ln_g), "lnb": colpack(ln_b),
            "w1a": w1a, "w1g": w1g,
            "b1a": colpack(b1[:, :2730]), "b1g": colpack(b1[:, 2730:]),
            "w2f": w2, "b2f": colpack(b2),
            "mb": cc["mb"], "tri0": cc["tri0"], "tri4": cc["tri4"],
            "killf": cc["kill"], "mskrow": cc["mskrow"],
            "ropetabs": cc["ropetabs"], "swp": cc["swp"],
            "ident": np.eye(128, dtype=np.float32),
            "identb_in": np.eye(128).astype(BFD),
            "blk2c": _l1f_blk2(), "exp2c": _l1f_exp2(),
        })
    res = run_bass_kernel_spmd(nc, in_maps, core_ids=list(range(8)))
    out = np.zeros((B, T, D), np.float32)
    for c in range(8):
        b, par = c // 2, c % 2
        out[b, par * HALF:(par + 1) * HALF] = res.results[c]["outown"]
    return out



def _l0_consts():
    c = {}
    pp = np.arange(128)[:, None]
    jj = np.arange(512)[None, :]
    tri = np.zeros((4, 128, 512), np.float32)
    for v in range(4):
        tri[v] = np.where(jj > pp + 128 * v, NEG, 0.0)
    c["tri_a"] = tri
    c["tri_b"] = np.where(pp > jj[:, :128], NEG, 0.0).astype(np.float32)
    f = 1.0 / 10000.0 ** (np.arange(0, 64, 2, dtype=np.float64) / 64)
    t = np.arange(T, dtype=np.float64)
    a = np.outer(f, t)
    c["cos32"] = np.cos(a).astype(np.float32)
    c["sin32"] = np.sin(a).astype(np.float32)
    P = np.zeros((128, 128), np.float32)
    for b_ in range(2):
        for j in range(32):
            P[64 * b_ + 32 + j, 64 * b_ + j] = 1.0
            P[64 * b_ + j, 64 * b_ + 32 + j] = 1.0
    c["swp"] = P
    c["slt"] = (np.arange(128)[:, None] < np.arange(128)[None, :]).astype(np.float32)
    c["slt32"] = (np.arange(32)[:, None] < np.arange(32)[None, :]).astype(np.float32)
    c["slotrow"] = np.arange(256, dtype=np.float32)[None, :]
    return c


def emit_l0(nc, p, ctx, pfx, x0, w_cq, g_qn, w_dqn, w_dqr, w_ckv, g_kn, w_dkn, w_dv, w_kr, w_imp, w_selk, w_selv, w_wink, w_winv, w_gate, w_proj, tri_a_in, tri_b_in, cos_in, sin_in, swp_in, slt_in, slt32_in, slot_in, ident_in, identb_in, partial, phases=99):
    NH, VD, NOPE, RD, KEEP = 8, 32, 32, 64, 256
    SC = 1.0 / np.sqrt(96.0)
    xtm_d = nc.dram_tensor(pfx + "xtm_d", [T, D], BF16, kind="Internal").ap()
    sb = lambda name, shape, dt: ctx.enter_context(nc.sbuf_tensor(pfx + name, shape, dt))
    psa = lambda name, shape, dt=F32: ctx.enter_context(nc.psum_tensor(pfx + name, shape, dt))

    xT = [sb(f"xT{i}", [128, T], BF16) for i in range(8)]
    accb = [sb(f"accb{i}", [128, T], BF16) for i in range(2)]
    kropeT = sb("kropeT", [64, T], BF16)
    nqT = sb("nqT", [96, T], BF16)
    ckvT = sb("ckvT", [32, T], BF16)
    ctab = [sb("ctab0", [32, 512], F32)] * 2
    stab = [sb("stab0", [32, 512], F32)] * 2
    rp1 = sb("rp1", [32, 512], F32)
    rp2 = sb("rp2", [32, 512], F32)
    scr5B = sb("scr5B", [32, 512], F32)
    v1b = [sb(f"v1b{i}", [128, 264], BF16) for i in range(32)]
    vwb = [sb(f"vwb{i}", [128, 264], BF16) for i in range(32)]
    vsb = [sb(f"vsb{i}", [128, 264], BF16) for i in range(2)]
    k1t = sb("k1t", [97, T], BF16)
    kwt = sb("kwt", [97, T], BF16)
    kst = sb("kst", [97, 256], BF16)
    qt_ = sb("qt_", [97, T], BF16)
    tm = [accb[0][:, 0:D], accb[0][:, D:2 * D]]
    tmf = [sb(f"tmf{i}", [128, D], F32) for i in range(2)]
    tri_a = sb("tri_a_s", [128, 2048], BF16)
    tri_b = sb("tri_b_s", [128, 128], BF16)
    swpf = sb("swpf", [128, 128], F32)
    sltf = sb("sltf", [128, 128], F32)
    slt32f = sb("slt32f", [32, 32], F32)
    slotb = sb("slotb", [128, 256], F32)
    identb = sb("identbl", [128, 128], BF16)
    identf = sb("identfl", [128, 128], F32)
    onesrow_f = sb("onesrow_fl", [1, 128], F32)
    onescolf = sb("onescolf", [128, 1], F32)
    scol = sb("scol", [128, 32], F32)
    prefS = sb("prefS", [128, 32], F32)
    indS = sb("indS", [128, 32], F32)
    indT = sb("indT", [128, 32], F32)
    gch = [sb(f"gch{i}", [128, 256], BF16) for i in range(2)]
    selT = [sb(f"selT{i}", [128, 256], BF16) for i in range(8)]
    r32a = sb("r32a", [1, 32], F32)
    r32b = sb("r32b", [1, 32], F32)
    c11 = [sb(f"c11_{i}", [1, 1], F32) for i in range(6)]
    gsc = sb("gsc", [128, 4], F32)
    nqf = sb("nqf", [96, 512], F32)
    rmsr = sb("rmsr", [1, 512], F32)
    scr5 = sb("scr5", [128, 512], F32)
    scr5b = sb("scr5b", [128, 512], BF16)
    mcols = sb("mcols", [128, 16], F32)
    mraw = sb("mraw", [128, 1], F32)
    mrawq = sb("mrawq", [128, 4], F32)
    mrow = sb("mrowl", [1, 512], BF16)
    ptb = [sb(f"ptbl{i}", [128, 512], BF16) for i in range(2)]
    avsb = sb("avsb", [32, 512], F32)
    comb = sb("comb", [32, 512], F32)
    prow = sb("prow", [1, 512], F32)
    gq = sb("gq", [96, 1], F32)
    gk = sb("gk", [32, 1], F32)
    wst = sb("wst", [128, 768], BF16)
    outt = None

    outt = tmf[1]
    P0 = psa("P0", [128, 512])
    P1 = psa("P1", [128, 512])
    P2 = psa("P2", [128, 512])
    P3 = psa("P3", [128, 512])
    PAV = psa("PAV", [33, 512])
    P4c = psa("P4c", [128, 128], BF16)
    P4d = psa("P4d", [128, 128], BF16)
    P5 = psa("P5", [128, 128])

    for o, i_, k in ((swpf[:], swp_in, "swpf"), (sltf[:], slt_in, "sltf"),
                     (slt32f[:], slt32_in, "slt32f"),
                     (identf[:], ident_in, "identf"), (identb[:], identb_in, "identb"),
                     (tri_b[:], tri_b_in, "tri_b"),
                     (gq[:], g_qn, "gq"), (gk[:], g_kn, "gk")):
        p.dma("sync", lambda e, o=o, i_=i_: e.dma_start(out=o, in_=i_), writes=[k])
    for v in range(4):
        p.dma("sync", lambda e, v=v: e.dma_start(
            out=tri_a[:, 512 * v:512 * v + 512], in_=tri_a_in[v]), writes=["tri_a"])
    p.v(lambda e: e.memset(onesrow_f[:], 1.0), writes=["onesrow_f"])
    p.v(lambda e: e.memset(onescolf[:], 1.0), writes=["onescolf"])
    p.v(lambda e: e.memset(k1t[96:97, :], 1.0), writes=["k1t_ones"])
    p.v(lambda e: e.memset(kwt[96:97, :], 1.0), writes=["kwt_ones"])
    p.v(lambda e: e.memset(kst[96:97, :], 1.0), writes=["kst_ones"])
    p.dma("sync", lambda e: e.dma_start(out=slotb[0:1, :], in_=slot_in), writes=["slotb"])
    p.t(lambda e: e.matmul(P0[:, 0:256], onesrow_f[:], slotb[0:1, 0:256],
                           start=True, stop=True),
        reads=["onesrow_f", "slotb"], writes=["P0"])
    p.v(lambda e: e.tensor_copy(slotb[:], P0[:, 0:256]), reads=["P0"], writes=["slotb"])

    wimpb = sb("wimpb", [128, D], F32)
    p.dma("sync", lambda e: e.dma_start(
        out=wimpb[0:1, :], in_=w_imp.rearrange("d 1 -> 1 d")), writes=["wimpb"])
    for g in range(2):
        p.t(lambda e, g=g: e.matmul(P0[:], onesrow_f[:],
                                    wimpb[0:1, 512 * g:512 * g + 512],
                                    start=True, stop=True),
            reads=["onesrow_f", "wimpb"], writes=["P0"])
        p.v(lambda e, g=g: e.tensor_copy(wimpb[:, 512 * g:512 * g + 512], P0[:]),
            reads=["P0"], writes=["wimpb"])
    for tc in range(32):
        tmfk, tmk = ("tmf", tc % 2), ("accb", 0)
        p.dma("sync", lambda e, tc=tc: e.dma_start(
            out=tmf[tc % 2][:], in_=x0[128 * tc:128 * tc + 128, :]), writes=[tmfk])
        p.v(lambda e, tc=tc: e.tensor_copy(tm[tc % 2][:], tmf[tc % 2][:]),
            reads=[tmfk], writes=[tmk])
        p.dma("sync", lambda e, tc=tc: e.dma_start(
            out=xtm_d[128 * tc:128 * tc + 128, :], in_=tm[tc % 2][:]),
            reads=[tmk], writes=["xtm"])
        p.v(lambda e, tc=tc: e.scalar_tensor_tensor(
            scr5[:], tmf[tc % 2][:, 0:512], 1.0, wimpb[:, 0:512],
            op0=OP.mult, op1=OP.mult, accum_out=mraw[:]),
            reads=[tmfk, "wimpb"], writes=["scr5", "mraw"])
        p.v(lambda e, tc=tc: e.scalar_tensor_tensor(
            scr5[:], tmf[tc % 2][:, 512:1024], 1.0, wimpb[:, 512:1024],
            op0=OP.mult, op1=OP.mult, accum_out=scol[:, tc:tc + 1]),
            reads=[tmfk, "wimpb"], writes=["scr5", ("scol", "b")])
        p.v(lambda e, tc=tc: e.tensor_add(scol[:, tc:tc + 1], scol[:, tc:tc + 1],
                                          mraw[:]),
            reads=[("scol", "b"), "mraw"], writes=["scol"])
        for dc in range(8):
            PT, ptk = (P4c, ("P4", 0)) if dc % 2 == 0 else (P4d, ("P4", 1))
            p.t(lambda e, tc=tc, dc=dc, PT=PT: e.transpose(
                PT[:], tm[tc % 2][:, 128 * dc:128 * dc + 128], identb[:]),
                reads=[tmk, "identb"], writes=[ptk])
            p.v(lambda e, tc=tc, dc=dc, PT=PT: e.tensor_copy(
                xT[dc][:, 128 * tc:128 * tc + 128], PT[:]),
                reads=[ptk], writes=[("xT", dc)])

    if phases >= 2:
        p.v(lambda e: e.reduce_max(mraw[:], scol[:], axis=AX),
            reads=["scol"], writes=["mraw"])
        p.t(lambda e: e.transpose(P5[0:1, 0:128], mraw[:], identf[:]),
            reads=["mraw", "identf"], writes=["P5"])
        p.v(lambda e: e.tensor_copy(rmsr[0:1, 0:128], P5[0:1, 0:128]),
            reads=["P5"], writes=["rmsr"])
        p.v(lambda e: e.reduce_max(c11[1][:], rmsr[0:1, 0:128], axis=AX),
            reads=["rmsr"], writes=[("c11", 1)])
        p.v(lambda e: e.reduce_max(mraw[:], scol[:], axis=AX, negate=True),
            reads=["scol"], writes=["mraw"])
        p.t(lambda e: e.transpose(P5[0:1, 0:128], mraw[:], identf[:]),
            reads=["mraw", "identf"], writes=["P5"])
        p.v(lambda e: e.tensor_copy(rmsr[0:1, 0:128], P5[0:1, 0:128]),
            reads=["P5"], writes=["rmsr"])
        p.v(lambda e: e.reduce_max(c11[2][:], rmsr[0:1, 0:128], axis=AX),
            reads=["rmsr"], writes=[("c11", 2)])
        p.v(lambda e: e.tensor_scalar(c11[0][:], c11[2][:], -1.0, -1.0,
                                      op0=OP.mult, op1=OP.add),
            reads=[("c11", 2)], writes=[("c11", 0)])
        for it in range(45):
            p.v(lambda e: e.tensor_add(c11[3][:], c11[0][:], c11[1][:]),
                reads=[("c11", 0), ("c11", 1)], writes=[("c11", 3)])
            p.v(lambda e: e.tensor_scalar_mul(c11[3][:], c11[3][:], 0.5),
                reads=[("c11", 3)], writes=[("c11", 3)])
            p.t(lambda e: e.matmul(P5[:, 0:1], onesrow_f[:], c11[3][:],
                                   start=True, stop=True),
                reads=["onesrow_f", ("c11", 3)], writes=["P5"])
            p.v(lambda e: e.tensor_copy(mraw[:], P5[:, 0:1]),
                reads=["P5"], writes=["mraw"])
            p.v(lambda e: e.tensor_scalar(indS[:], scol[:], mraw[:], None,
                                          op0=OP.is_gt),
                reads=["scol", "mraw"], writes=["indS"])
            p.v(lambda e: e.reduce_sum(mraw[:], indS[:], axis=AX),
                reads=["indS"], writes=["mraw"])
            p.t(lambda e: e.matmul(P5[0:1, 0:128], mraw[:], identf[:],
                                    start=True, stop=True),
                reads=["mraw", "identf"], writes=["P5"])
            p.v(lambda e: e.tensor_copy(rmsr[0:1, 0:128], P5[0:1, 0:128]),
                reads=["P5"], writes=["rmsr"])
            p.v(lambda e: e.reduce_sum(c11[4][:], rmsr[0:1, 0:128], axis=AX),
                reads=["rmsr"], writes=[("c11", 4)])
            p.v(lambda e: e.tensor_scalar(c11[5][:], c11[4][:], 256.0, None,
                                          op0=OP.is_ge),
                reads=[("c11", 4)], writes=[("c11", 5)])
            p.v(lambda e: e.tensor_sub(c11[4][:], c11[3][:], c11[0][:]),
                reads=[("c11", 3), ("c11", 0)], writes=[("c11", 4)])
            p.v(lambda e: e.scalar_tensor_tensor(c11[0][:], c11[4][:], c11[5][:],
                                                 c11[0][:], op0=OP.mult, op1=OP.add),
                reads=[("c11", 4), ("c11", 5), ("c11", 0)], writes=[("c11", 0)])
            p.v(lambda e: e.tensor_scalar(c11[5][:], c11[5][:], -1.0, 1.0,
                                          op0=OP.mult, op1=OP.add),
                reads=[("c11", 5)], writes=[("c11", 5)])
            p.v(lambda e: e.tensor_sub(c11[4][:], c11[3][:], c11[1][:]),
                reads=[("c11", 3), ("c11", 1)], writes=[("c11", 4)])
            p.v(lambda e: e.scalar_tensor_tensor(c11[1][:], c11[4][:], c11[5][:],
                                                 c11[1][:], op0=OP.mult, op1=OP.add),
                reads=[("c11", 4), ("c11", 5), ("c11", 1)], writes=[("c11", 1)])
        p.t(lambda e: e.matmul(P5[:, 0:1], onesrow_f[:], c11[1][:],
                               start=True, stop=True),
            reads=["onesrow_f", ("c11", 1)], writes=["P5"])
        p.v(lambda e: e.tensor_copy(mraw[:], P5[:, 0:1]), reads=["P5"], writes=["mraw"])
        p.v(lambda e: e.tensor_scalar(indS[:], scol[:], mraw[:], None, op0=OP.is_gt),
            reads=["scol", "mraw"], writes=["indS"])
        p.t(lambda e: e.matmul(P5[:, 0:1], onesrow_f[:], c11[0][:],
                               start=True, stop=True),
            reads=["onesrow_f", ("c11", 0)], writes=["P5"])
        p.v(lambda e: e.tensor_copy(mraw[:], P5[:, 0:1]), reads=["P5"], writes=["mraw"])
        p.v(lambda e: e.tensor_scalar(indT[:], scol[:], mraw[:], None, op0=OP.is_gt),
            reads=["scol", "mraw"], writes=["indT"])
        p.v(lambda e: e.tensor_sub(indT[:], indT[:], indS[:]),
            reads=["indT", "indS"], writes=["indT"])

        def prefix32(src, srck, dst, dstk):
            p.t(lambda e, src=src: e.matmul(P2[:, 0:32], sltf[:], src[:],
                                            start=True, stop=True),
                reads=["sltf", srck], writes=["P2"])
            p.t(lambda e, src=src: e.matmul(P5[0:1, 0:32], onescolf[:], src[:],
                                            start=True, stop=True),
                reads=["onescolf", srck], writes=["P5"])
            p.v(lambda e: e.tensor_copy(r32a[:], P5[0:1, 0:32]),
                reads=["P5"], writes=["r32a"])
            p.t(lambda e: e.transpose(P5[0:32, 0:1], r32a[:], identf[0:1, 0:1]),
                reads=["r32a", "identf"], writes=["P5"])
            p.v(lambda e: e.tensor_copy(mraw[0:32, 0:1], P5[0:32, 0:1]),
                reads=["P5"], writes=["mraw"])
            p.t(lambda e: e.matmul(P5[0:32, 0:1], slt32f[:], mraw[0:32, 0:1],
                                   start=True, stop=True),
                reads=["slt32f", "mraw"], writes=["P5"])
            p.v(lambda e: e.tensor_copy(mraw[0:32, 0:1], P5[0:32, 0:1]),
                reads=["P5"], writes=["mraw"])
            p.t(lambda e: e.transpose(P5[0:1, 0:32], mraw[0:32, 0:1],
                                      identf[0:32, 0:32]),
                reads=["mraw", "identf"], writes=["P5"])
            p.v(lambda e: e.tensor_copy(r32b[:], P5[0:1, 0:32]),
                reads=["P5"], writes=["r32b"])
            p.t(lambda e: e.matmul(P3[:, 0:32], onesrow_f[:], r32b[:],
                                   start=True, stop=True),
                reads=["onesrow_f", "r32b"], writes=["P3"])
            p.v(lambda e, dst=dst: e.tensor_copy(dst[:], P2[:, 0:32]),
                reads=["P2"], writes=[dstk])
            p.v(lambda e, dst=dst: e.tensor_add(dst[:], dst[:], P3[:, 0:32]),
                reads=[dstk, "P3"], writes=[dstk])

        prefix32(indT, "indT", prefS, "prefS")
        p.t(lambda e: e.matmul(P5[0:1, 0:32], onescolf[:], indS[:],
                               start=True, stop=True),
            reads=["onescolf", "indS"], writes=["P5"])
        p.v(lambda e: e.tensor_copy(r32a[:], P5[0:1, 0:32]), reads=["P5"], writes=["r32a"])
        p.v(lambda e: e.reduce_sum(c11[2][:], r32a[:], axis=AX),
            reads=["r32a"], writes=[("c11", 2)])
        p.t(lambda e: e.matmul(P5[:, 0:1], onesrow_f[:], c11[2][:],
                               start=True, stop=True),
            reads=["onesrow_f", ("c11", 2)], writes=["P5"])
        p.v(lambda e: e.tensor_copy(mraw[:], P5[:, 0:1]), reads=["P5"], writes=["mraw"])
        p.v(lambda e: e.tensor_scalar(prefS[:], prefS[:], mraw[:], None, op0=OP.add),
            reads=["prefS", "mraw"], writes=["prefS"])
        p.v(lambda e: e.tensor_scalar(prefS[:], prefS[:], 256.0, None, op0=OP.is_lt),
            reads=["prefS"], writes=["prefS"])
        p.v(lambda e: e.tensor_mul(indT[:], indT[:], prefS[:]),
            reads=["indT", "prefS"], writes=["indT"])
        p.v(lambda e: e.tensor_add(indS[:], indS[:], indT[:]),
            reads=["indS", "indT"], writes=["indS"])
        prefix32(indS, "indS", prefS, "prefS")
        for sh in range(2):
            for tc in range(32):
                p.v(lambda e, tc=tc: e.tensor_scalar(
                    gch[tc % 2][:], slotb[:, 0:256],
                    prefS[:, tc:tc + 1], indS[:, tc:tc + 1],
                    op0=OP.is_equal, op1=OP.mult),
                    reads=["slotb", "prefS", "indS"], writes=[("gch", tc % 2)])
                p.dma("sync", lambda e, tc=tc: e.dma_start(
                    out=tm[tc % 2][:], in_=xtm_d[128 * tc:128 * tc + 128, :]),
                    reads=["xtm"], writes=[("accb", 0)])
                for oh in range(2):
                    PS, pk = (P2, "P2") if oh == 0 else (P3, "P3")
                    p.t(lambda e, tc=tc, sh=sh, oh=oh, PS=PS: e.matmul(
                        PS[:], gch[tc % 2][:, 128 * sh:128 * sh + 128],
                        tm[tc % 2][:, 512 * oh:512 * oh + 512],
                        start=(tc == 0), stop=(tc == 31)),
                        reads=[("gch", tc % 2), ("accb", 0)], writes=[pk])
            for oh in range(2):
                PS, pk = (P2, "P2") if oh == 0 else (P3, "P3")
                for q4 in range(4):
                    p.v(lambda e, PS=PS, q4=q4: e.tensor_copy(
                        scr5b[:, 128 * q4:128 * q4 + 128],
                        PS[:, 128 * q4:128 * q4 + 128]),
                        reads=[pk], writes=["scr5b"])
                for q4 in range(4):
                    dc = 4 * oh + q4
                    PT, ptk = (P4c, ("P4", 0)) if q4 % 2 == 0 else (P4d, ("P4", 1))
                    p.t(lambda e, q4=q4, PT=PT: e.transpose(
                        PT[:], scr5b[:, 128 * q4:128 * q4 + 128], identb[:]),
                        reads=["scr5b", "identb"], writes=[ptk])
                    p.v(lambda e, dc=dc, sh=sh, PT=PT: e.tensor_copy(
                        selT[dc][:, 128 * sh:128 * sh + 128], PT[:]),
                        reads=[ptk], writes=[("selT", dc)])
    wprojb = [tri_a[:, 1024 * i:1024 * i + 1024] for i in range(2)]

    def loadw_bf(dst, dstk, src_ap, n, rows=128):
        p.dma("sync", lambda e, src_ap=src_ap, n=n, rows=rows: e.dma_start(
            out=tmf[0][0:rows, 0:n], in_=src_ap), writes=[("tmf", 0)])
        p.v(lambda e, dst=dst, n=n, rows=rows: e.tensor_copy(
            dst, tmf[0][0:rows, 0:n]),
            reads=[("tmf", 0)], writes=[dstk])

    def rope_rows(dst_bf, dstk, src1, src2, srck, base, width, possl, scale, dco=0):
        for cg in range(0, width, 512):
            cw = min(512, width - cg)
            p.dma("sync", lambda e, cg=cg, cw=cw, possl=possl: e.dma_start(
                out=ctab[0][:, 0:cw],
                in_=cos_in[:, possl + cg:possl + cg + cw]),
                writes=["ctab"])
            p.dma("sync", lambda e, cg=cg, cw=cw, possl=possl: e.dma_start(
                out=stab[0][:, 0:cw],
                in_=sin_in[:, possl + cg:possl + cg + cw]),
                writes=["stab"])
            tk, sk = "ctab", "stab"
            ct, st = ctab[0], stab[0]
            p.v(lambda e, cg=cg, cw=cw, src1=src1, ct=ct: e.tensor_mul(
                rp1[:, 0:cw], src1[0:32, cg:cg + cw], ct[:, 0:cw]),
                reads=[srck, tk], writes=["rp1"])
            p.v(lambda e, cg=cg, cw=cw, src2=src2, st=st: e.tensor_mul(
                rp2[:, 0:cw], src2[0:32, cg:cg + cw], st[:, 0:cw]),
                reads=[srck, sk], writes=["rp2"])
            p.v(lambda e, cw=cw: e.tensor_sub(rp1[:, 0:cw], rp1[:, 0:cw],
                                              rp2[:, 0:cw]),
                reads=["rp1", "rp2"], writes=["rp1"])
            p.v(lambda e, cg=cg, cw=cw, dst_bf=dst_bf, base=base, scale=scale,
                dco=dco: e.tensor_scalar_mul(
                    dst_bf[base:base + 32, dco + cg:dco + cg + cw],
                    rp1[:, 0:cw], scale),
                reads=["rp1"], writes=[dstk])
            p.v(lambda e, cg=cg, cw=cw, src1=src1, st=st: e.tensor_mul(
                rp1[:, 0:cw], src1[0:32, cg:cg + cw], st[:, 0:cw]),
                reads=[srck, sk], writes=["rp1"])
            p.v(lambda e, cg=cg, cw=cw, src2=src2, ct=ct: e.tensor_mul(
                rp2[:, 0:cw], src2[0:32, cg:cg + cw], ct[:, 0:cw]),
                reads=[srck, tk], writes=["rp2"])
            p.v(lambda e, cw=cw: e.tensor_add(rp1[:, 0:cw], rp1[:, 0:cw],
                                              rp2[:, 0:cw]),
                reads=["rp1", "rp2"], writes=["rp1"])
            p.v(lambda e, cg=cg, cw=cw, dst_bf=dst_bf, base=base, scale=scale,
                dco=dco: e.tensor_scalar_mul(
                    dst_bf[base + 32:base + 64, dco + cg:dco + cg + cw],
                    rp1[:, 0:cw], scale),
                reads=["rp1"], writes=[dstk])

    eps6l = sb("eps6l", [1, 1], F32)
    SCF = float(1.0 / np.sqrt(96.0))
    if phases >= 3:
        p.v(lambda e: e.memset(eps6l[:], 1e-6), writes=["eps6l"])
        for ic in range(8):
            loadw_bf(wst[:, 96 * ic:96 * ic + 96], "wst",
                     w_cq[128 * ic:128 * ic + 128, :], 96)

        def rmsproj(nrows, wtiles, wk, gcol, gk_, outT, outk, eps):
            for grp in range(8):
                for ic in range(8):
                    p.t(lambda e, grp=grp, ic=ic, wtiles=wtiles, nrows=nrows:
                        e.matmul(P0[0:nrows, 0:512], wtiles(ic),
                                 xT[ic][:, 512 * grp:512 * grp + 512],
                                 start=(ic == 0), stop=(ic == 7)),
                        reads=[wk, ("xT", ic)], writes=["P0"])
                p.v(lambda e, nrows=nrows: e.tensor_copy(
                    nqf[0:nrows, 0:512], P0[0:nrows, 0:512]),
                    reads=["P0"], writes=["nqf"])
                p.s(lambda e, nrows=nrows: e.activation(
                    scr5[0:nrows, :], nqf[0:nrows, 0:512], AF.Square),
                    reads=["nqf"], writes=["scr5"])
                p.t(lambda e, nrows=nrows: e.matmul(
                    P1[0:1, 0:512], onescolf[0:nrows, :], scr5[0:nrows, :],
                    start=True, stop=True),
                    reads=["onescolf", "scr5"], writes=["P1"])
                p.s(lambda e, nrows=nrows, eps=eps: e.activation(
                    rmsr[:], P1[0:1, 0:512], AF.Sqrt, bias=eps6l[:],
                    scale=1.0 / nrows),
                    reads=["P1", "eps6l"], writes=["rmsr"])
                p.v(lambda e: e.reciprocal(rmsr[:], rmsr[:]),
                    reads=["rmsr"], writes=["rmsr"])
                p.t(lambda e, nrows=nrows: e.matmul(
                    P1[0:nrows, 0:512], onesrow_f[0:1, 0:nrows], rmsr[:],
                    start=True, stop=True),
                    reads=["onesrow_f", "rmsr"], writes=["P1"])
                p.v(lambda e, nrows=nrows: e.tensor_mul(
                    nqf[0:nrows, 0:512], nqf[0:nrows, 0:512], P1[0:nrows, 0:512]),
                    reads=["nqf", "P1"], writes=["nqf"])
                p.v(lambda e, grp=grp, nrows=nrows, outT=outT, gcol=gcol:
                    e.tensor_scalar_mul(outT[0:nrows, 512 * grp:512 * grp + 512],
                                        nqf[0:nrows, 0:512], gcol),
                    reads=["nqf", gk_], writes=[outk])

        rmsproj(96, lambda ic: wst[:, 96 * ic:96 * ic + 96], "wst",
                gq[0:96, :], "gq", nqT, "nqT", 1e-6)
        for ic in range(8):
            loadw_bf(wst[:, 32 * ic:32 * ic + 32], "wst",
                     w_ckv[128 * ic:128 * ic + 128, :], 32)
        rmsproj(32, lambda ic: wst[:, 32 * ic:32 * ic + 32], "wst",
                gk[0:32, :], "gk", ckvT, "ckvT", 1e-6)
        for ic in range(8):
            loadw_bf(wst[:, 64 * ic:64 * ic + 64], "wst",
                     w_kr[128 * ic:128 * ic + 128, :], 64)
        for grp in range(8):
            for ic in range(8):
                p.t(lambda e, grp=grp, ic=ic: e.matmul(
                    P0[0:64, 0:512], wst[:, 64 * ic:64 * ic + 64],
                    xT[ic][:, 512 * grp:512 * grp + 512],
                    start=(ic == 0), stop=(ic == 7)),
                    reads=["wst", ("xT", ic)], writes=["P0"])
            p.v(lambda e, grp=grp: e.tensor_copy(scr5[0:32, :], P0[0:32, 0:512]),
                reads=["P0"], writes=["scr5"])
            p.v(lambda e, grp=grp: e.tensor_copy(scr5B[:], P0[32:64, 0:512]),
                reads=["P0"], writes=["scr5"])
            rope_rows(kropeT, "kropeT", scr5, scr5B, "scr5", 0, 512, 512 * grp,
                      1.0 / 16.0, dco=512 * grp)
        loadw_bf(wst[0:32, 0:256], "wst", w_dv, 256, rows=32)
        for tc in range(32):
            p.t(lambda e, tc=tc: e.matmul(
                P0[:, 0:256], ckvT[:, 128 * tc:128 * tc + 128], wst[0:32, 0:256],
                start=True, stop=True),
                reads=["ckvT", "wst"], writes=["P0"])
            p.v(lambda e, tc=tc: e.tensor_copy(
                v1b[tc].rearrange("p (h c) -> p h c", h=8)[:, :, 0:32],
                P0[:, 0:256].rearrange("p (h c) -> p h c", h=8)),
                reads=["P0"], writes=[("v1b", tc)])
            p.v(lambda e, tc=tc: e.memset(
                v1b[tc].rearrange("p (h c) -> p h c", h=8)[:, :, 32:33], 1.0),
                reads=[], writes=[("v1b", tc)])
        wvstage = [accb[1][:, 256 * i:256 * i + 256] for i in range(8)]
        for ic in range(8):
            loadw_bf(wvstage[ic][:], ("wvst", ic),
                     w_winv[128 * ic:128 * ic + 128, :], 256)
        for tc in range(32):
            for ic in range(8):
                p.t(lambda e, tc=tc, ic=ic: e.matmul(
                    P0[:, 0:256], xT[ic][:, 128 * tc:128 * tc + 128],
                    wvstage[ic][:], start=(ic == 0), stop=(ic == 7)),
                    reads=[("xT", ic), ("wvst", ic)], writes=["P0"])
            p.v(lambda e, tc=tc: e.tensor_copy(
                vwb[tc].rearrange("p (h c) -> p h c", h=8)[:, :, 0:32],
                P0[:, 0:256].rearrange("p (h c) -> p h c", h=8)),
                reads=["P0"], writes=[("vwb", tc)])
            p.v(lambda e, tc=tc: e.memset(
                vwb[tc].rearrange("p (h c) -> p h c", h=8)[:, :, 32:33], 1.0),
                reads=[], writes=[("vwb", tc)])
        for ic in range(8):
            loadw_bf(wvstage[ic][:], ("wvst", ic),
                     w_selv[128 * ic:128 * ic + 128, :], 256)
        for sh in range(2):
            for ic in range(8):
                p.t(lambda e, sh=sh, ic=ic: e.matmul(
                    P0[:, 0:256], selT[ic][:, 128 * sh:128 * sh + 128],
                    wvstage[ic][:], start=(ic == 0), stop=(ic == 7)),
                    reads=[("selT", ic), ("wvst", ic)], writes=["P0"])
            p.v(lambda e, sh=sh: e.tensor_copy(
                vsb[sh].rearrange("p (h c) -> p h c", h=8)[:, :, 0:32],
                P0[:, 0:256].rearrange("p (h c) -> p h c", h=8)),
                reads=["P0"], writes=[("vsb", sh)])
            p.v(lambda e, sh=sh: e.memset(
                vsb[sh].rearrange("p (h c) -> p h c", h=8)[:, :, 32:33], 1.0),
                reads=[], writes=[("vsb", sh)])
        for ic in range(8):
            loadw_bf(wst[:, 3 * ic:3 * ic + 3], "wst",
                     w_gate[128 * ic:128 * ic + 128, :], 3)
        p.v(lambda e: e.memset(mcols[0:3, 0:1], 0.0), writes=["mcols"])
        for grp in range(8):
            for ic in range(8):
                p.t(lambda e, grp=grp, ic=ic: e.matmul(
                    P0[0:3, 0:512], wst[:, 3 * ic:3 * ic + 3],
                    xT[ic][:, 512 * grp:512 * grp + 512],
                    start=(ic == 0), stop=(ic == 7)),
                    reads=["wst", ("xT", ic)], writes=["P0"])
            p.v(lambda e: e.reduce_sum(mraw[0:3, 0:1], P0[0:3, 0:512], axis=AX),
                reads=["P0"], writes=["mraw"])
            p.v(lambda e: e.tensor_add(mcols[0:3, 0:1], mcols[0:3, 0:1],
                                       mraw[0:3, 0:1]),
                reads=["mcols", "mraw"], writes=["mcols"])
        p.t(lambda e: e.transpose(P5[0:1, 0:3], mcols[0:3, 0:1],
                                  identf[0:3, 0:3]),
            reads=["mcols", "identf"], writes=["P5"])
        p.v(lambda e: e.tensor_scalar_mul(r32a[0:1, 0:3], P5[0:1, 0:3], 1.0 / T),
            reads=["P5"], writes=["r32a"])
        p.v(lambda e: e.reduce_max(c11[0][:], r32a[0:1, 0:3], axis=AX, negate=True),
            reads=["r32a"], writes=[("c11", 0)])
        p.s(lambda e: e.activation(r32b[0:1, 0:3], r32a[0:1, 0:3], AF.Exp,
                                   bias=c11[0][:]),
            reads=["r32a", ("c11", 0)], writes=["r32b"])
        p.v(lambda e: e.reduce_sum(c11[1][:], r32b[0:1, 0:3], axis=AX),
            reads=["r32b"], writes=[("c11", 1)])
        p.v(lambda e: e.reciprocal(c11[1][:], c11[1][:]),
            reads=[("c11", 1)], writes=[("c11", 1)])
        p.v(lambda e: e.tensor_scalar_mul(r32b[0:1, 0:3], r32b[0:1, 0:3],
                                          c11[1][:]),
            reads=["r32b", ("c11", 1)], writes=["r32b"])
        p.t(lambda e: e.matmul(P5[:, 0:3], onesrow_f[:], r32b[0:1, 0:3],
                               start=True, stop=True),
            reads=["onesrow_f", "r32b"], writes=["P5"])
        p.v(lambda e: e.tensor_copy(gsc[:, 0:3], P5[:, 0:3]),
            reads=["P5"], writes=["gsc"])

    if phases >= 4:
        for h in range(NH):
            loadw_bf(wst[0:96, 0:32], "wst", w_dqn[:, 32 * h:32 * h + 32], 32,
                     rows=96)
            loadw_bf(wst[0:96, 32:96], "wst", w_dqr[:, 64 * h:64 * h + 64], 64,
                     rows=96)
            for grp in range(8):
                p.t(lambda e, grp=grp: e.matmul(
                    P0[0:32, 0:512], wst[0:96, 0:32],
                    nqT[:, 512 * grp:512 * grp + 512], start=True, stop=True),
                    reads=["wst", "nqT"], writes=["P0"])
                p.t(lambda e, grp=grp: e.matmul(
                    P1[0:64, 0:512], wst[0:96, 32:96],
                    nqT[:, 512 * grp:512 * grp + 512], start=True, stop=True),
                    reads=["wst", "nqT"], writes=["P1"])
                p.s(lambda e, grp=grp: e.activation(
                    qt_[0:32, 512 * grp:512 * grp + 512], P0[0:32, 0:512],
                    AF.Copy, scale=SCF),
                    reads=["P0"], writes=["qt_"])
                p.v(lambda e, grp=grp: e.tensor_copy(scr5[0:32, :], P1[0:32, 0:512]),
                    reads=["P1"], writes=["scr5"])
                p.v(lambda e, grp=grp: e.tensor_copy(scr5B[:], P1[32:64, 0:512]),
                    reads=["P1"], writes=["scr5"])
                rope_rows(qt_, "qt_", scr5, scr5B, "scr5", 32, 512, 512 * grp, SCF,
                          dco=512 * grp)
            loadw_bf(wst[0:32, 0:32], "wst", w_dkn[:, 32 * h:32 * h + 32], 32, rows=32)
            for grp in range(8):
                p.t(lambda e, grp=grp: e.matmul(
                    P0[0:32, 0:512], wst[0:32, 0:32],
                    ckvT[:, 512 * grp:512 * grp + 512], start=True, stop=True),
                    reads=["wst", "ckvT"], writes=["P0"])
                p.s(lambda e, grp=grp: e.activation(
                    k1t[0:32, 512 * grp:512 * grp + 512], P0[0:32, 0:512], AF.Copy),
                    reads=["P0"], writes=["k1t"])
            p.v(lambda e: e.tensor_copy(k1t[32:64, :], kropeT[0:32, :]),
                reads=["kropeT"], writes=["k1t"])
            p.v(lambda e: e.tensor_copy(k1t[64:96, :], kropeT[32:64, :]),
                reads=["kropeT"], writes=["k1t"])
            loadw_bf(wst[:, 0:96], "wst", w_wink[0:128, 96 * h:96 * h + 96], 96)
            for ic in range(1, 8):
                loadw_bf(wst[:, 96 * 0 + 0:96] if False else
                         wvstage[ic][:, 0:96], ("wvst", ic),
                         w_wink[128 * ic:128 * ic + 128, 96 * h:96 * h + 96], 96)
            for grp in range(8):
                for ic in range(8):
                    lhs = wst[:, 0:96] if ic == 0 else wvstage[ic][:, 0:96]
                    lk = "wst" if ic == 0 else ("wvst", ic)
                    p.t(lambda e, grp=grp, ic=ic, lhs=lhs: e.matmul(
                        P0[0:96, 0:512], lhs,
                        xT[ic][:, 512 * grp:512 * grp + 512],
                        start=(ic == 0), stop=(ic == 7)),
                        reads=[lk, ("xT", ic)], writes=["P0"])
                p.s(lambda e, grp=grp: e.activation(
                    kwt[0:32, 512 * grp:512 * grp + 512], P0[0:32, 0:512], AF.Copy),
                    reads=["P0"], writes=["kwt"])
                p.v(lambda e, grp=grp: e.tensor_copy(scr5[0:32, :], P0[32:64, 0:512]),
                    reads=["P0"], writes=["scr5"])
                p.v(lambda e, grp=grp: e.tensor_copy(scr5B[:], P0[64:96, 0:512]),
                    reads=["P0"], writes=["scr5"])
                rope_rows(kwt, "kwt", scr5, scr5B, "scr5", 32, 512, 512 * grp, 1.0,
                          dco=512 * grp)
            loadw_bf(wst[:, 0:96], "wst", w_selk[0:128, 96 * h:96 * h + 96], 96)
            for ic in range(1, 8):
                loadw_bf(wvstage[ic][:, 0:96], ("wvst", ic),
                         w_selk[128 * ic:128 * ic + 128, 96 * h:96 * h + 96], 96)
            for ic in range(8):
                lhs = wst[:, 0:96] if ic == 0 else wvstage[ic][:, 0:96]
                lk = "wst" if ic == 0 else ("wvst", ic)
                p.t(lambda e, ic=ic, lhs=lhs: e.matmul(
                    P0[0:96, 0:256], lhs, selT[ic][:],
                    start=(ic == 0), stop=(ic == 7)),
                    reads=[lk, ("selT", ic)], writes=["P0"])
            p.s(lambda e: e.activation(kst[0:32, :], P0[0:32, 0:256], AF.Copy),
                reads=["P0"], writes=["kst"])
            p.v(lambda e: e.tensor_copy(scr5[0:32, 0:256], P0[32:64, 0:256]),
                reads=["P0"], writes=["scr5"])
            p.v(lambda e: e.tensor_copy(scr5B[:, 0:256], P0[64:96, 0:256]),
                reads=["P0"], writes=["scr5"])
            rope_rows(kst, "kst", scr5, scr5B, "scr5", 32, 256, 0, 1.0)
            for qg in range(8):
                pend_tail = None
                for br in range(3):
                    kt, ktk, nkc = ((k1t, "k1t", 4 * (qg + 1)) if br == 0 else
                                    (kst, "kst", 2) if br == 1 else
                                    (kwt, "kwt", 4 * (qg + 1)))
                    vt_ = (v1b if br == 0 else vsb if br == 1 else vwb)
                    vkk = ("v1b" if br == 0 else "vsb" if br == 1 else "vwb")
                    pend_mt = None
                    for q4 in range(4):
                        qt = 4 * qg + q4
                        nA = qt // 4 + 1 if br != 1 else 1
                        mb_ = 8 * (q4 % 2)
                        mck = ("mcols", q4 % 2)
                        if q4 == 0 and pend_tail is not None:
                            pend_tail[0]()
                        for kc in range(nA):
                            PS, pk = (P2, "P2") if kc % 2 == 0 else (P3, "P3")
                            kwid = 256 if br == 1 else 512
                            p.t(lambda e, qt=qt, kc=kc, kt=kt, PS=PS, kwid=kwid:
                                e.matmul(PS[:, 0:kwid],
                                         qt_[0:96, 128 * qt:128 * qt + 128],
                                         kt[0:96, kwid * kc:kwid * kc + kwid],
                                         start=True, stop=True),
                                reads=["qt_", ktk], writes=[pk])
                            if br != 1 and kc == nA - 1:
                                p.v(lambda e, qt=qt, PS=PS: e.tensor_add(
                                    scr5[:], PS[:, 0:512],
                                    tri_a[:, 512 * (qt % 4):512 * (qt % 4) + 512]),
                                    reads=[pk, "tri_a"], writes=["scr5"])
                                p.v(lambda e, kc=kc, mb_=mb_: e.reduce_max(
                                    mcols[:, mb_ + kc:mb_ + kc + 1], scr5[:],
                                    axis=AX),
                                    reads=["scr5"], writes=[mck])
                            else:
                                p.v(lambda e, kc=kc, PS=PS, kwid=kwid, mb_=mb_:
                                    e.reduce_max(
                                    mcols[:, mb_ + kc:mb_ + kc + 1],
                                    PS[:, 0:kwid], axis=AX),
                                    reads=[pk], writes=[mck])
                        p.v(lambda e, nA=nA, mb_=mb_, q4=q4: e.reduce_max(
                            mrawq[:, q4:q4 + 1], mcols[:, mb_:mb_ + nA],
                            axis=AX, negate=True),
                            reads=[mck], writes=[("mrawq", q4)])
                        if q4 == 0 and pend_tail is not None:
                            pend_tail[1]()
                            pend_tail = None
                        if pend_mt is not None:
                            pend_mt()

                        def mk_mt(q4=q4, qg=qg):
                            def go():
                                p.t(lambda e, q4=q4: e.transpose(
                                    P5[0:1, 0:128], mrawq[:, q4:q4 + 1],
                                    identf[:]),
                                    reads=[("mrawq", q4), "identf"],
                                    writes=["P5"])
                                p.v(lambda e, qg=qg, q4=q4: e.tensor_copy(
                                    qt_[96:97, 512 * qg + 128 * q4:
                                        512 * qg + 128 * q4 + 128],
                                    P5[0:1, 0:128]),
                                    reads=["P5"], writes=["qt_"])
                            return go
                        pend_mt = mk_mt()
                    if pend_tail is not None:
                        pend_tail[0]()
                        pend_tail[1]()
                        pend_tail = None
                    pend_mt()
                    nkcb = 2 if br == 1 else 4 * (qg + 1)
                    pend_ea = None
                    for kc in range(nkcb):
                        off = 0
                        if br != 1 and kc // 4 == qg:
                            off = 128 * (kc % 4)
                        PS, pk = (P2, "P2") if kc % 2 == 0 else (P3, "P3")
                        p.t(lambda e, qg=qg, kc=kc, kt=kt, PS=PS, off=off: e.matmul(
                            PS[:, off:512], kt[0:97, 128 * kc:128 * kc + 128],
                            qt_[0:97, 512 * qg + off:512 * qg + 512],
                            start=True, stop=True),
                            reads=[ktk, "qt_"], writes=[pk])
                        if br != 1 and kc // 4 == qg:
                            j = kc % 4
                            p.v(lambda e, j=j, PS=PS: e.tensor_add(
                                PS[:, 128 * j:128 * j + 128],
                                PS[:, 128 * j:128 * j + 128], tri_b[:]),
                                reads=[pk, "tri_b"], writes=[pk])
                        if pend_ea is not None:
                            pend_ea()

                        def mk_ea(kc=kc, PS=PS, pk=pk, off=off, vt_=vt_,
                                  vkk=vkk, nkcb=nkcb, h=h):
                            def go():
                                p.s(lambda e: e.activation(
                                    ptb[kc % 2][:, off:512], PS[:, off:512],
                                    AF.Exp),
                                    reads=[pk], writes=[("ptb", kc % 2)])
                                p.t(lambda e: e.matmul(
                                    PAV[:, off:512],
                                    vt_[kc].rearrange("p (g c) -> p g c",
                                                      g=8)[:, h, 0:33],
                                    ptb[kc % 2][:, off:512],
                                    start=(kc == 0), stop=(kc == nkcb - 1)),
                                    reads=[(vkk, kc), ("ptb", kc % 2)],
                                    writes=["PAV"])
                            return go
                        pend_ea = mk_ea()
                    pend_ea()

                    def mk_tail(br=br, h=h, qg=qg):
                        def front():
                            p.v(lambda e: e.tensor_copy(avsb[:],
                                                        PAV[0:32, 0:512]),
                                reads=["PAV"], writes=["avsb"])
                            p.v(lambda e: e.reciprocal(prow[:],
                                                       PAV[32:33, 0:512]),
                                reads=["PAV"], writes=["prow"])
                        def back():
                            p.t(lambda e: e.matmul(PAV[0:32, 0:512],
                                                   onesrow_f[0:1, 0:32],
                                                   prow[:],
                                                   start=True, stop=True),
                                reads=["onesrow_f", "prow"], writes=["PAV"])
                            p.v(lambda e: e.tensor_mul(avsb[:], avsb[:],
                                                       PAV[0:32, 0:512]),
                                reads=["avsb", "PAV"], writes=["avsb"])
                            if br == 0:
                                p.v(lambda e: e.tensor_scalar_mul(
                                    comb[:], avsb[:], gsc[0:32, br:br + 1]),
                                    reads=["avsb", "gsc"], writes=["comb"])
                            else:
                                p.v(lambda e: e.scalar_tensor_tensor(
                                    comb[:], avsb[:], gsc[0:32, br:br + 1],
                                    comb[:], op0=OP.mult, op1=OP.add),
                                    reads=["avsb", "gsc", "comb"],
                                    writes=["comb"])
                        return (front, back)
                    pend_tail = mk_tail()
                pend_tail[0]()
                pend_tail[1]()
                p.v(lambda e, h=h, qg=qg: e.tensor_copy(
                    accb[h // 4][32 * (h % 4):32 * (h % 4) + 32,
                                 512 * qg:512 * qg + 512], comb[:]),
                    reads=["comb"], writes=[("accb", h // 4)])

    if phases >= 5:
        loadw_bf(wprojb[0][:], "tri_a", w_proj[0:128, :], 1024)
        loadw_bf(wprojb[1][:], "tri_a", w_proj[128:256, :], 1024)
        for tc in range(32):
            for oh in range(2):
                PS, pk = (P0, "P0") if oh == 0 else (P1, "P1")
                for cc in range(2):
                    p.t(lambda e, tc=tc, oh=oh, cc=cc, PS=PS: e.matmul(
                        PS[:], accb[cc][:, 128 * tc:128 * tc + 128],
                        wprojb[cc][:, 512 * oh:512 * oh + 512],
                        start=(cc == 0), stop=(cc == 1)),
                        reads=[("accb", cc), "tri_a"], writes=[pk])
                p.v(lambda e, oh=oh, PS=PS: e.tensor_copy(
                    outt[:, 512 * oh:512 * oh + 512], PS[:]),
                    reads=[pk], writes=["outt"])
            p.dma("sync", lambda e, tc=tc: e.dma_start(
                out=partial[128 * tc:128 * tc + 128, :], in_=outt[:]),
                reads=["outt"], writes=[("part", tc)])





def build_l0(phases=99):
    nc = bass.Bass("TRN2", target_bir_lowering=False, dynamic_dma_scratch_size=2048)
    p = Prog(nc)
    x0 = nc.dram_tensor("x0", [T, D], F32, kind="ExternalInput").ap()
    w_cq = nc.dram_tensor("w_cq", [D, 96], F32, kind="ExternalInput").ap()
    g_qn = nc.dram_tensor("g_qn", [96, 1], F32, kind="ExternalInput").ap()
    w_dqn = nc.dram_tensor("w_dqn", [96, 256], F32, kind="ExternalInput").ap()
    w_dqr = nc.dram_tensor("w_dqr", [96, 512], F32, kind="ExternalInput").ap()
    w_ckv = nc.dram_tensor("w_ckv", [D, 32], F32, kind="ExternalInput").ap()
    g_kn = nc.dram_tensor("g_kn", [32, 1], F32, kind="ExternalInput").ap()
    w_dkn = nc.dram_tensor("w_dkn", [32, 256], F32, kind="ExternalInput").ap()
    w_dv = nc.dram_tensor("w_dv", [32, 256], F32, kind="ExternalInput").ap()
    w_kr = nc.dram_tensor("w_kr", [D, 64], F32, kind="ExternalInput").ap()
    w_imp = nc.dram_tensor("w_imp", [D, 1], F32, kind="ExternalInput").ap()
    w_selk = nc.dram_tensor("w_selk", [D, 768], F32, kind="ExternalInput").ap()
    w_selv = nc.dram_tensor("w_selv", [D, 256], F32, kind="ExternalInput").ap()
    w_wink = nc.dram_tensor("w_wink", [D, 768], F32, kind="ExternalInput").ap()
    w_winv = nc.dram_tensor("w_winv", [D, 256], F32, kind="ExternalInput").ap()
    w_gate = nc.dram_tensor("w_gate", [D, 3], F32, kind="ExternalInput").ap()
    w_proj = nc.dram_tensor("w_proj", [256, D], F32, kind="ExternalInput").ap()
    tri_a_in = nc.dram_tensor("tri_a", [4, 128, 512], BF16, kind="ExternalInput").ap()
    tri_b_in = nc.dram_tensor("tri_b", [128, 128], BF16, kind="ExternalInput").ap()
    cos_in = nc.dram_tensor("cos32", [32, T], F32, kind="ExternalInput").ap()
    sin_in = nc.dram_tensor("sin32", [32, T], F32, kind="ExternalInput").ap()
    swp_in = nc.dram_tensor("swp", [128, 128], F32, kind="ExternalInput").ap()
    slt_in = nc.dram_tensor("slt", [128, 128], F32, kind="ExternalInput").ap()
    slt32_in = nc.dram_tensor("slt32", [32, 32], F32, kind="ExternalInput").ap()
    slot_in = nc.dram_tensor("slotrow", [1, 256], F32, kind="ExternalInput").ap()
    ident_in = nc.dram_tensor("ident", [128, 128], F32, kind="ExternalInput").ap()
    identb_in = nc.dram_tensor("identb_in", [128, 128], BF16, kind="ExternalInput").ap()
    partial = nc.dram_tensor("partial", [T, D], F32, kind="ExternalOutput").ap()


    ctx = ExitStack()

    emit_l0(nc, p, ctx, "", x0, w_cq, g_qn, w_dqn, w_dqr, w_ckv, g_kn, w_dkn, w_dv, w_kr, w_imp, w_selk, w_selv, w_wink, w_winv, w_gate, w_proj, tri_a_in, tri_b_in, cos_in, sin_in, swp_in, slt_in, slt32_in, slot_in, ident_in, identb_in, partial, phases=phases)
    p.emit()
    ctx.close()
    return nc

def _l0_inmap(ins_d, x0_b, hg):
    cc = _l0_consts()
    gq = ins_d["L0_attn_g_qnorm"].astype(np.float32).reshape(96, 1)
    gk = ins_d["L0_attn_g_kvnorm"].astype(np.float32).reshape(32, 1)
    return {
        "x0": np.ascontiguousarray(x0_b),
        "w_cq": ins_d["L0_attn_w_cq"], "g_qn": gq,
        "w_dqn": np.ascontiguousarray(ins_d["L0_attn_w_dq_nope"][:, 256 * hg:256 * hg + 256]),
        "w_dqr": np.ascontiguousarray(ins_d["L0_attn_w_dq_rope"][:, 512 * hg:512 * hg + 512]),
        "w_ckv": ins_d["L0_attn_w_ckv"], "g_kn": gk,
        "w_dkn": np.ascontiguousarray(ins_d["L0_attn_w_dk_nope"][:, 256 * hg:256 * hg + 256]),
        "w_dv": np.ascontiguousarray(ins_d["L0_attn_w_dv"][:, 256 * hg:256 * hg + 256]),
        "w_kr": ins_d["L0_attn_w_krope"], "w_imp": ins_d["L0_attn_w_imp"],
        "w_selk": np.ascontiguousarray(ins_d["L0_attn_w_selk"][:, 768 * hg:768 * hg + 768]),
        "w_selv": np.ascontiguousarray(ins_d["L0_attn_w_selv"][:, 256 * hg:256 * hg + 256]),
        "w_wink": np.ascontiguousarray(ins_d["L0_attn_w_wink"][:, 768 * hg:768 * hg + 768]),
        "w_winv": np.ascontiguousarray(ins_d["L0_attn_w_winv"][:, 256 * hg:256 * hg + 256]),
        "w_gate": ins_d["L0_attn_w_gate"],
        "w_proj": np.ascontiguousarray(ins_d["L0_attn_w_proj"][256 * hg:256 * hg + 256, :]),
        "tri_a": cc["tri_a"].astype(BFD), "tri_b": cc["tri_b"].astype(BFD),
        "cos32": cc["cos32"], "sin32": cc["sin32"], "swp": cc["swp"],
        "slt": cc["slt"], "slt32": cc["slt32"], "slotrow": cc["slotrow"],
        "ident": np.eye(128, dtype=np.float32),
        "identb_in": np.eye(128).astype(BFD),
    }


def run_l0(ins_d, x0):
    nc = build_l0()
    in_maps = [_l0_inmap(ins_d, x0[c // 2], c % 2) for c in range(8)]
    res = run_bass_kernel_spmd(nc, in_maps, core_ids=list(range(8)))
    return [res.results[c]["partial"] for c in range(8)]


def kernel(**inputs):
    return run_fused(inputs)


def kernel_unfused(**inputs):
    ins = {k: np.asarray(v) for k, v in inputs.items()}
    x0 = np.ascontiguousarray(ins["L0_attn_x"].astype(np.float32))

    parts = run_l0(ins, x0)
    pa = [parts[2 * b] for b in range(B)]
    pb = [parts[2 * b + 1] for b in range(B)]

    x2 = run_moe(x0, pa, pb, ins["L0_ffn_gate_w"], ins["L0_ffn_w1"],
                 ins["L0_ffn_b1"], ins["L0_ffn_w2"], ins["L0_ffn_b2"],
                 ins["L0_ffn_ln_g"], ins["L0_ffn_ln_b"])

    x3 = run_l1(x2, ins["L1_attn_W_qk"], ins["L1_attn_W_v"],
                ins["L1_attn_alibi_slope"])

    out = run_l1f(x3, ins["L1_ffn_wqkv"], ins["L1_ffn_q_scale"],
                  ins["L1_ffn_k_scale"], ins["L1_ffn_wo"], ins["L1_ffn_ln_g"],
                  ins["L1_ffn_ln_b"], ins["L1_ffn_w1"], ins["L1_ffn_b1"],
                  ins["L1_ffn_w2"], ins["L1_ffn_b2"])
    return out



def emit_ext(nc, p, ctx, pfx, x3g_d, ext_d, ident_in):
    sb = lambda name, shape, dt: ctx.enter_context(nc.sbuf_tensor(pfx + name, shape, dt))
    psa = lambda name, shape, dt=F32: ctx.enter_context(nc.psum_tensor(pfx + name, shape, dt))
    zt = sb("zt", [128, 512], F32)
    ft = [sb(f"ft{i}", [128, D], F32) for i in range(2)]
    fsb = [sb(f"fsb{i}", [128, 128], F32) for i in range(2)]
    idf = sb("idf", [128, 128], F32)
    PT0 = psa("PT0", [128, 128])
    PT1 = psa("PT1", [128, 128])
    p.v(lambda e: e.memset(zt[:], 0.0), writes=["zt"])
    p.dma("sync", lambda e: e.dma_start(out=idf[:], in_=ident_in),
          writes=["idf"])
    for dc in range(8):
        for g in range(4):
            p.dma("sync", lambda e, dc=dc, g=g: e.dma_start(
                out=ext_d[128 * dc:128 * dc + 128, 512 * g:512 * g + 512],
                in_=zt[:]), reads=["zt"], writes=[("extz", dc)])
        p.dma("sync", lambda e, dc=dc: e.dma_start(
            out=ext_d[128 * dc:128 * dc + 128, 2048:2050], in_=zt[:, 0:2]),
            reads=["zt"], writes=[("extz", dc)])
    for j in range(32):
        row = (j % 2) * 2048 + (j // 2) * 128
        p.dma("sync", lambda e, j=j, row=row: e.dma_start(
            out=ft[j % 2][:], in_=x3g_d[row:row + 128, :]),
            reads=["x3gath"], writes=[("ft", j % 2)])
        for dc in range(8):
            PT, ptk = (PT0, "PT0") if dc % 2 == 0 else (PT1, "PT1")
            p.t(lambda e, j=j, dc=dc, PT=PT: e.transpose(
                PT[:], ft[j % 2][:, 128 * dc:128 * dc + 128], idf[:]),
                reads=[("ft", j % 2), "idf"], writes=[ptk])
            p.v(lambda e, dc=dc, PT=PT: e.tensor_copy(fsb[dc % 2][:], PT[:]),
                reads=[ptk], writes=[("fsb", dc % 2)])
            c0a, c0b = 2049 + 128 * j, 1 + 128 * j
            p.pdma("sync",
                   lambda e, dc=dc, c0=c0a: e.dma_start(
                       out=ext_d[128 * dc:128 * dc + 128, c0:c0 + 128],
                       in_=fsb[dc % 2][:]),
                   lambda e, dc=dc, c0=c0b: e.dma_start(
                       out=ext_d[128 * dc:128 * dc + 128, c0:c0 + 128],
                       in_=fsb[dc % 2][:]),
                   reads=[("fsb", dc % 2), ("extz", dc)],
                   writes=[("extw", j, dc)])


def build_fused():
    nc = bass.Bass("TRN2", target_bir_lowering=False, dynamic_dma_scratch_size=2048)
    p = Prog(nc)
    colmap, NBCOL = _l1_colmap()
    HID = 2730
    E = lambda n, sh, dt=F32: nc.dram_tensor(n, sh, dt, kind="ExternalInput").ap()
    x0 = E("x0", [T, D]); w_cq = E("w_cq", [D, 96]); g_qn = E("g_qn", [96, 1])
    w_dqn = E("w_dqn", [96, 256]); w_dqr = E("w_dqr", [96, 512])
    w_ckv = E("w_ckv", [D, 32]); g_kn = E("g_kn", [32, 1])
    w_dkn = E("w_dkn", [32, 256]); w_dv = E("w_dv", [32, 256])
    w_kr = E("w_kr", [D, 64]); w_imp = E("w_imp", [D, 1])
    w_selk = E("w_selk", [D, 768]); w_selv = E("w_selv", [D, 256])
    w_wink = E("w_wink", [D, 768]); w_winv = E("w_winv", [D, 256])
    w_gate = E("w_gate", [D, 3]); w_proj = E("w_proj", [256, D])
    tri_a_l0 = E("tri_a_l0", [4, 128, 512], BF16)
    tri_b_l0 = E("tri_b_l0", [128, 128], BF16)
    cos32 = E("cos32", [32, T]); sin32 = E("sin32", [32, T])
    swp_l0 = E("swp_l0", [128, 128]); slt = E("slt", [128, 128])
    slt32 = E("slt32", [32, 32]); slotrow = E("slotrow", [1, 256])
    ident = E("ident", [128, 128]); identb = E("identb_in", [128, 128], BF16)
    x0own = E("x0own", [HALF, D]); gate_w = E("gate_w", [D, 8])
    w1m = E("w1", [8, D, 2 * D]); b1t = E("b1t", [8, 128, 16])
    w2m = E("w2", [8, 2 * D, D]); b2m = E("b2", [8, D])
    lng_bc = E("lng_bc", [8, 128, D]); lnb_moe = E("lnb", [8, D])
    wqk = E("wqk", [D, D]); wv = E("wv", [D, D]); slope = E("slope", [1, 1])
    d0 = E("d0", [128, 512]); tri_a_l1 = E("tri_a", [2, 128, 512], BF16)
    tri_b_l1 = E("tri_b", [2, 128, 128], BF16); offa = E("offa", [1, 128])
    kpos = E("kpos", [128, NBCOL]); kill = E("kill", [128, NBCOL])
    qps = E("qps", [128, 16])
    wqkv = E("wqkv", [2, D, 1536]); qsc = E("qsc", [2, 128, 1])
    ksc = E("ksc", [2, 128, 1]); wo = E("wo", [2, 512, D])
    lng_f = E("lng", [2, 128, 8]); lnb_f = E("lnb_f", [2, 128, 8])
    w1a = E("w1a", [2, D, HID]); w1g = E("w1g", [2, D, HID])
    b1a = E("b1a", [2, 128, 22]); b1g = E("b1g", [2, 128, 22])
    w2f = E("w2f", [2, HID, D]); b2f = E("b2f", [2, 128, 8])
    mb = E("mb", [128, 640]); tri0 = E("tri0", [128, 128])
    tri4 = E("tri4", [128, 128]); killf = E("killf", [128, 240])
    mskrow = E("mskrow", [1, 512]); ropetabs = E("ropetabs", [8, 4, 128, 512])
    swp_f = E("swp", [128, 128]); blk2c = E("blk2c", [128, 2])
    exp2c = E("exp2c", [2, 128])
    outown = nc.dram_tensor("outown", [HALF, D], F32, kind="ExternalOutput").ap()
    I = lambda n, sh, dt=F32: nc.dram_tensor(n, sh, dt, kind="Internal").ap()
    partial_d = I("partial_d", [T, D])
    pg_d = I("pg_d", [2 * T, D])
    x2own_d = I("x2own_d", [HALF, D])
    x2f_d = I("x2f_d", [T, D])
    x3s_d = I("x3s_d", [HALF, D])
    x3g_d = I("x3g_d", [T, D])
    ext_d = I("ext_d", [D, 6145])

    pidcell = []
    def pid_init(e):
        pid = nc.partition_id([mybir.EngineType.SP])
        pidcell.append(pid % 2)
        return e.nop()
    p.op("sync", pid_init)
    p.parcond = lambda: pidcell[0] == 0

    c = ExitStack()
    emit_l0(nc, p, c, "a_", x0, w_cq, g_qn, w_dqn, w_dqr, w_ckv, g_kn, w_dkn,
            w_dv, w_kr, w_imp, w_selk, w_selv, w_wink, w_winv, w_gate, w_proj,
            tri_a_l0, tri_b_l0, cos32, sin32, swp_l0, slt, slt32, slotrow,
            ident, identb, partial_d)
    p.barrier()
    c.close()
    p.g(lambda e: e.collective_compute(
        "AllGather", OP.bypass, replica_groups=[[0, 1], [2, 3], [4, 5], [6, 7]],
        ins=[partial_d.opt()], outs=[pg_d.opt()]),
        reads=[("part", tc) for tc in range(32)], writes=["pgsrc"])

    def _pair(dram, o0, o1, tc):
        return (dram[o0 + 128 * tc:o0 + 128 * tc + 128, :],
                dram[o1 + 128 * tc:o1 + 128 * tc + 128, :])
    pa_row = lambda tc: _pair(pg_d, 0, 2048, tc)
    pb_row = lambda tc: _pair(pg_d, 4096, 6144, tc)
    c = ExitStack()
    emit_moe(nc, p, c, "b_", x0own, pa_row, pb_row, gate_w, w1m, b1t, w2m, b2m,
             lng_bc, lnb_moe, identb, ident, x2own_d)
    p.barrier()
    c.close()
    p.g(lambda e: e.collective_compute(
        "AllGather", OP.bypass, replica_groups=[[0, 1], [2, 3], [4, 5], [6, 7]],
        ins=[x2own_d.opt()], outs=[x2f_d.opt()]),
        reads=[("x2own", tc) for tc in range(16)], writes=["x2src"])

    x2own_row = lambda sl: _pair(x2f_d, 0, 128, 2 * sl)
    c = ExitStack()
    emit_l1(nc, p, c, "c_", x2f_d, x2own_row, wqk, wv, slope, d0, ident,
            identb, tri_a_l1, tri_b_l1, offa, kpos, kill, qps, x3s_d,
            NBCOL, colmap)
    p.barrier()
    c.close()
    p.g(lambda e: e.collective_compute(
        "AllGather", OP.bypass, replica_groups=[[0, 1], [2, 3], [4, 5], [6, 7]],
        ins=[x3s_d.opt()], outs=[x3g_d.opt()]),
        reads=[("x3own", sl) for sl in range(16)], writes=["x3gath", "x3src"])

    c = ExitStack()
    emit_ext(nc, p, c, "e_", x3g_d, ext_d, ident)
    p.barrier()
    c.close()

    def x3own_row(tc):
        off = 2048 * (tc % 2) + 128 * (tc // 2)
        return _pair(x3g_d, off, off + 1024, 0)
    c = ExitStack()
    emit_l1f(nc, p, c, "d_", ext_d, x3own_row, wqkv, qsc, ksc, wo, lng_f,
             lnb_f, w1a, w1g, b1a, b1g, w2f, b2f, mb, tri0, tri4, killf,
             mskrow, ropetabs, swp_f, ident, identb, blk2c, exp2c, outown)
    p.emit()
    c.close()
    return nc


def run_fused(inputs):
    ins = {k: np.asarray(v) for k, v in inputs.items()}
    x0 = np.ascontiguousarray(ins["L0_attn_x"].astype(np.float32))
    nc = build_fused()
    b1 = ins["L1_ffn_b1"]
    w1f = ins["L1_ffn_w1"]

    def colpack(v):
        n = v.shape[1]
        nb = (n + 127) // 128
        a = np.zeros((2, 128 * nb), np.float32)
        a[:, :n] = v
        return np.ascontiguousarray(a.reshape(2, nb, 128).transpose(0, 2, 1))

    in_maps = []
    for cidx in range(8):
        b, par = cidx // 2, cidx % 2
        im = _l0_inmap(ins, x0[b], par)
        im = {("tri_a_l0" if k == "tri_a" else
               "tri_b_l0" if k == "tri_b" else
               "swp_l0" if k == "swp" else k): v for k, v in im.items()}
        im.update({
            "x0own": np.ascontiguousarray(x0[b, par * HALF:(par + 1) * HALF]),
            "gate_w": ins["L0_ffn_gate_w"], "w1": ins["L0_ffn_w1"],
            "b1t": np.ascontiguousarray(
                ins["L0_ffn_b1"].reshape(8, 16, 128).transpose(0, 2, 1)),
            "w2": ins["L0_ffn_w2"], "b2": ins["L0_ffn_b2"],
            "lng_bc": np.ascontiguousarray(
                np.broadcast_to(ins["L0_ffn_ln_g"][:, None, :], (8, 128, D))),
            "lnb": ins["L0_ffn_ln_b"],
        })
        cc1 = _l1_consts(par)
        im.update({
            "wqk": ins["L1_attn_W_qk"], "wv": ins["L1_attn_W_v"],
            "slope": np.asarray(ins["L1_attn_alibi_slope"], np.float32).reshape(1, 1),
            "d0": cc1["d0"], "tri_a": cc1["tri_a"].astype(BFD),
            "tri_b": cc1["tri_b"].astype(BFD), "offa": cc1["offa"],
            "kpos": cc1["kpos"], "kill": cc1["kill"], "qps": cc1["qps"],
        })
        ccf = _l1f_consts(par)
        im.update({
            "wqkv": ins["L1_ffn_wqkv"],
            "qsc": np.ascontiguousarray(np.tile(ins["L1_ffn_q_scale"], (1, 2))[:, :, None]),
            "ksc": np.ascontiguousarray(np.tile(ins["L1_ffn_k_scale"], (1, 2))[:, :, None]),
            "wo": ins["L1_ffn_wo"], "lng": colpack(ins["L1_ffn_ln_g"]),
            "lnb_f": colpack(ins["L1_ffn_ln_b"]),
            "w1a": np.ascontiguousarray(w1f[:, :, :2730]),
            "w1g": np.ascontiguousarray(w1f[:, :, 2730:]),
            "b1a": colpack(b1[:, :2730]), "b1g": colpack(b1[:, 2730:]),
            "w2f": ins["L1_ffn_w2"], "b2f": colpack(ins["L1_ffn_b2"]),
            "mb": ccf["mb"], "tri0": ccf["tri0"], "tri4": ccf["tri4"],
            "killf": ccf["kill"], "mskrow": ccf["mskrow"],
            "ropetabs": ccf["ropetabs"], "swp": ccf["swp"],
            "blk2c": _l1f_blk2(), "exp2c": _l1f_exp2(),
        })
        in_maps.append(im)
    res = run_bass_kernel_spmd(nc, in_maps, core_ids=list(range(8)))
    out = np.zeros((B, T, D), np.float32)
    for cidx in range(8):
        b, par = cidx // 2, cidx % 2
        out[b, par * HALF:(par + 1) * HALF] = res.results[cidx]["outown"]
    return out


# version 07 (final)
# speedup vs baseline: 1.1749x; 1.1749x over previous
import numpy as np
import ml_dtypes
from contextlib import ExitStack
import concourse.bass as bass
import concourse.mybir as mybir
from concourse.bass_utils import run_bass_kernel_spmd

F32 = mybir.dt.float32
BF16 = mybir.dt.bfloat16
I32 = mybir.dt.int32
AF = mybir.ActivationFunctionType
OP = mybir.AluOpType
AX = mybir.AxisListType.X
BFD = ml_dtypes.bfloat16

B, T, D = 4, 4096, 1024
HALF = T // 2
NEG = -1.0e30


class Prog:

    def __init__(self, nc):
        self.nc = nc
        self.ops = []
        self.stream_pos = {}
        self.last_writer = {}
        self.readers = {}

    def op(self, eng, fn, reads=(), writes=(), dma=False, extra=()):
        stream = eng + "@d" if dma else eng
        oid = len(self.ops)
        pos = self.stream_pos.get(stream, 0)
        self.stream_pos[stream] = pos + 1
        deps = set(extra)
        for k in reads:
            w = self.last_writer.get(k)
            if w is not None:
                deps.add(w)
            self.readers.setdefault(k, []).append(oid)
        for k in writes:
            w = self.last_writer.get(k)
            if w is not None:
                deps.add(w)
            deps.update(self.readers.get(k, ()))
            self.last_writer[k] = oid
            self.readers[k] = []
        deps.discard(oid)
        self.ops.append(dict(eng=eng, fn=fn, dma=dma, stream=stream, pos=pos,
                             deps=deps))
        return oid

    def t(self, fn, reads=(), writes=()):
        return self.op("tensor", fn, reads, writes)

    def v(self, fn, reads=(), writes=()):
        return self.op("vector", fn, reads, writes)

    def s(self, fn, reads=(), writes=()):
        return self.op("scalar", fn, reads, writes)

    def g(self, fn, reads=(), writes=()):
        return self.op("gpsimd", fn, reads, writes)

    def dma(self, eng, fn, reads=(), writes=()):
        return self.op(eng, fn, reads, writes, dma=True)

    parcond = None

    def pdma(self, eng, fn0, fn1, reads=(), writes=()):
        def fn(e):
            with e.If(self.parcond()):
                i0 = fn0(e)
            with e.Else():
                i1 = fn1(e)
            return [i0, i1]
        return self.op(eng, fn, reads, writes, dma=True)

    def row_dma(self, eng, out_ap, src, reads=(), writes=()):
        if isinstance(src, tuple):
            return self.pdma(eng,
                             lambda e, s=src[0]: e.dma_start(out=out_ap, in_=s),
                             lambda e, s=src[1]: e.dma_start(out=out_ap, in_=s),
                             reads, writes)
        return self.dma(eng, lambda e, s=src: e.dma_start(out=out_ap, in_=s),
                        reads, writes)

    def barrier(self):
        tails = {}
        for i, o in enumerate(self.ops):
            tails.setdefault(o["stream"], []).append(i)
        ids = set()
        for s, l in tails.items():
            if s.endswith("@d"):
                ids.update(l[-self.DMA_POOL:])
            else:
                ids.add(l[-1])
        ids = tuple(sorted(ids))
        for eng in ("tensor", "vector", "scalar", "gpsimd", "sync"):
            self.op(eng, lambda e: e.nop(), extra=ids)

    DMA_POOL = 8

    def emit(self):
        nc = self.nc
        ops = self.ops
        n = len(ops)
        K = self.DMA_POOL
        def skip_edge(prod, cons):
            return (prod["stream"] == cons["stream"] == "tensor")

        need_inc = [o["dma"] for o in ops]
        for o in ops:
            for d in o["deps"]:
                if not skip_edge(ops[d], o):
                    need_inc[d] = True
        semkey = [None] * n
        semval = [0] * n
        eng_cnt = {}
        for i, o in enumerate(ops):
            if o["dma"]:
                semkey[i] = (o["eng"] + "@d", o["pos"] % K)
                semval[i] = 16 * (o["pos"] // K + 1)
            else:
                c = eng_cnt.get(o["eng"], 0)
                if need_inc[i]:
                    c += 1
                eng_cnt[o["eng"]] = c
                semkey[i] = (o["eng"], 0)
                semval[i] = c
        used_keys = set()
        for i in range(n):
            if need_inc[i]:
                used_keys.add(semkey[i])
        sems = {}
        sem_ctx = ExitStack()
        for kk in sorted(used_keys):
            sems[kk] = sem_ctx.enter_context(
                nc.semaphore(f"sem_{kk[0].replace('@', '_')}_{kk[1]}"))
        waits = [dict() for _ in range(n)]
        water = {}

        def add_wait(i, kk, v):
            if v <= 0:
                return
            wk = (ops[i]["eng"], kk)
            if water.get(wk, 0) >= v:
                return
            waits[i][kk] = max(waits[i].get(kk, 0), v)

        for i, o in enumerate(ops):
            if o["dma"] and o["pos"] >= K:
                add_wait(i, semkey[i], semval[i] - 16)
            for d in sorted(o["deps"]):
                pd = ops[d]
                if skip_edge(pd, o):
                    continue
                if not need_inc[d]:
                    continue
                add_wait(i, semkey[d], semval[d])
            for kk, v in waits[i].items():
                wk = (o["eng"], kk)
                water[wk] = max(water.get(wk, 0), v)
        final_cnt = {}
        for i, o in enumerate(ops):
            if o["dma"]:
                kk = semkey[i]
                final_cnt[kk] = max(final_cnt.get(kk, 0), semval[i])
        per_eng = {}
        for i, o in enumerate(ops):
            per_eng.setdefault(o["eng"], []).append(i)

        with nc.Block() as block:
            for eng_name in ("sync", "tensor", "vector", "scalar", "gpsimd"):
                if eng_name not in per_eng:
                    continue
                idxs = per_eng[eng_name]

                def body(e, idxs=idxs, eng_name=eng_name):
                    for i in idxs:
                        o = ops[i]
                        for kk, v in sorted(waits[i].items()):
                            e.wait_ge(sems[kk], v)
                        ins = o["fn"](e)
                        if need_inc[i]:
                            if ins is None:
                                raise RuntimeError(
                                    f"op {i} on {eng_name} returned None")
                            for i2 in (ins if isinstance(ins, list) else [ins]):
                                i2.then_inc(sems[semkey[i]],
                                            16 if o["dma"] else 1)
                    for kk, v in sorted(final_cnt.items()):
                        if kk[0] == eng_name + "@d":
                            e.wait_ge(sems[kk], v)

                getattr(block, eng_name)(body)
        sem_ctx.close()



def _l1_consts(par):
    c = {}
    jj = np.arange(512)[None, :]
    pp = np.arange(128)[:, None]
    c["d0"] = (jj - pp).astype(np.float32)
    c["ident"] = np.eye(128, dtype=np.float32)
    s = np.arange(16)
    qt = 2 * s + par
    tri = np.zeros((2, 128, 512), np.float32)
    for h in range(2):
        v = (2 * h + par) % 4
        tri[h] = np.where(jj > pp + 128 * v, NEG, 0.0)
    c["tri_a"] = tri
    offa = np.zeros((1, 128), np.float32)
    for si in range(16):
        for kc in range(8):
            offa[0, 8 * si + kc] = 512 * kc - 128 * qt[si]
    c["offa"] = offa
    c["qps"] = (128 * qt[None, :] + np.arange(128)[:, None]).astype(np.float32)
    ncols = int(sum(2 * s + 2))
    kpos = np.zeros((128, ncols), np.float32)
    kill = np.zeros((128, ncols), np.float32)
    col = 0
    for si in range(16):
        for kc in range(2 * si + 2):
            kpos[:, col] = 128 * kc + np.arange(128)
            if kc > qt[si]:
                kill[:, col] = NEG
                kpos[:, col] = 0.0
            col += 1
    c["kpos"] = kpos
    c["kill"] = kill
    trib = np.zeros((2, 128, 128), np.float32)
    j2 = np.arange(128)[None, :]
    for idx in range(2):
        if idx == par:
            trib[idx] = np.where(pp > j2, NEG, 0.0)
        elif idx > par:
            trib[idx] = NEG
    c["tri_b"] = trib
    return c


def _l1_colmap():
    colmap = {}
    col = 0
    for si in range(16):
        for kc in range(2 * si + 2):
            colmap[(si, kc)] = col
            col += 1
    return colmap, col



def emit_l1(nc, p, ctx, pfx, x2, x2own_row, wqk, wv, slope, d0_in, ident_in,
            identb_in, tri_a_in, tri_b_in, offa_in, kpos_in, kill_in, qps_in,
            x3own, NBCOL, colmap, phases=99):
    sb = lambda name, shape, dt: ctx.enter_context(nc.sbuf_tensor(pfx + name, shape, dt))
    psa = lambda name, shape, dt=F32: ctx.enter_context(nc.psum_tensor(pfx + name, shape, dt))

    xT = [sb(f"xT{i}", [128, T], BF16) for i in range(8)]
    V = [sb(f"V{i}", [128, D], BF16) for i in range(32)]
    wvT = [sb(f"wvT{i}", [128, D], BF16) for i in range(8)]
    qT = [sb(f"qT{i}", [128, HALF], BF16) for i in range(8)]
    tm = [sb(f"tm{i}", [128, D], BF16) for i in range(2)]
    tmf = [sb(f"tmf{i}", [128, D], F32) for i in range(2)]
    identb = sb("identb", [128, 128], BF16)
    identf = sb("identf", [128, 128], F32)
    onesrow_b = sb("onesrow_b", [1, 128], BF16)
    onescol_b = sb("onescol_b", [128, 1], BF16)
    onesrow_f = sb("onesrow_f", [1, 128], F32)
    a0 = sb("a0", [128, 512], F32)
    tri_a = sb("tri_a_s", [128, 1024], BF16)
    tri_b = sb("tri_b_s", [128, 256], BF16)
    offa = sb("offa_s", [1, 128], F32)
    offas = sb("offas", [1, 128], F32)
    kposb = sb("kposb", [128, NBCOL], F32)
    killb = sb("killb", [128, NBCOL], F32)
    biascol = sb("biascol", [128, NBCOL], F32)
    qps = sb("qps_s", [128, 16], F32)
    qpss = sb("qpss", [128, 16], F32)
    slope_c = sb("slope_c", [128, 1], F32)
    ss = sb("ss", [1, 1], F32)
    mcols = sb("mcols", [128, 16], F32)
    mraw = sb("mraw", [128, 1], F32)
    mneg = sb("mneg", [128, 1], F32)
    mrowb = sb("mrowb", [1, 128], BF16)
    ptb = [sb(f"ptb{i}", [128, 128], BF16) for i in range(34)]
    pstage = sb("pstage", [128, 128], F32)
    sums_sb = sb("sums_sb", [1, 128], F32)
    rec = sb("rec", [128, 1], F32)
    att_sb = [sb(f"att_sb{i}", [128, 128], F32) for i in range(2)]
    at_tm = sb("at_tm", [128, D], F32)
    x2r = sb("x2r", [128, D], F32)
    s1 = sb("s1", [128, D], F32)
    acc1 = sb("acc1", [128, 1], F32)
    acc2 = sb("acc2", [128, 1], F32)
    ln1 = sb("ln1", [128, 1], F32)
    ln2 = sb("ln2", [128, 1], F32)
    ln3 = sb("ln3", [128, 1], F32)
    epsc = sb("epsc", [128, 1], F32)

    P0 = psa("P0", [128, 512])
    P1 = psa("P1", [128, 512])
    P2 = psa("P2", [128, 512])
    P3 = psa("P3", [128, 512])
    P4c = psa("P4c", [128, 128], BF16)
    P4d = psa("P4d", [128, 128], BF16)
    P5 = psa("P5", [128, 128])
    P7 = psa("P7", [128, 128])

    for ap_out, ap_in, key in ((ss[:], slope, "ss"), (tmf[0][:, 0:512], d0_in, ("tmf", 0)),
                               (identf[:], ident_in, "identf"),
                               (offa[:], offa_in, "offa"), (kposb[:], kpos_in, "kposb"),
                               (killb[:], kill_in, "killb"), (qps[:], qps_in, "qps")):
        p.dma("sync", lambda e, o=ap_out, i=ap_in: e.dma_start(out=o, in_=i),
              writes=[key])
    p.dma("sync", lambda e: e.dma_start(out=identb[:], in_=identb_in),
          writes=["identb"])
    for i in range(2):
        p.dma("sync", lambda e, i=i: e.dma_start(
            out=tri_a[:, 512 * i:512 * i + 512], in_=tri_a_in[i]), writes=["tri_a"])
        p.dma("sync", lambda e, i=i: e.dma_start(
            out=tri_b[:, 128 * i:128 * i + 128], in_=tri_b_in[i]), writes=["tri_b"])
    p.v(lambda e: e.memset(onesrow_b[:], 1.0), writes=["onesrow_b"])
    p.v(lambda e: e.memset(onescol_b[:], 1.0), writes=["onescol_b"])
    p.v(lambda e: e.memset(onesrow_f[:], 1.0), writes=["onesrow_f"])
    p.v(lambda e: e.memset(epsc[:], 1e-5), writes=["epsc"])
    p.t(lambda e: e.matmul(P5[:, 0:1], onesrow_f[:], ss[:], start=True, stop=True),
        reads=["onesrow_f", "ss"], writes=["P5"])
    p.v(lambda e: e.tensor_copy(slope_c[:], P5[:, 0:1]), reads=["P5"],
        writes=["slope_c"])
    p.v(lambda e: e.tensor_scalar_mul(a0[:], tmf[0][:, 0:512], slope_c[:]),
        reads=[("tmf", 0), "slope_c"], writes=["a0"])
    p.v(lambda e: e.tensor_scalar_mul(offas[:], offa[:], slope_c[0:1, :]),
        reads=["offa", "slope_c"], writes=["offas"])
    p.v(lambda e: e.tensor_scalar_mul(qpss[:], qps[:], slope_c[:]),
        reads=["qps", "slope_c"], writes=["qpss"])
    p.v(lambda e: e.tensor_scalar_mul(biascol[:], kposb[:], slope_c[:]),
        reads=["kposb", "slope_c"], writes=["biascol"])
    p.v(lambda e: e.tensor_add(biascol[:], biascol[:], killb[:]),
        reads=["biascol", "killb"], writes=["biascol"])

    def xo_tile(dc, sl):
        idx = 8 + 2 * dc + (1 if sl >= 8 else 0)
        return V[idx][:, 128 * (sl % 8):128 * (sl % 8) + 128], ("V", idx)

    def load_transpose(src_row, n_tiles, dst_fn):
        for tc in range(n_tiles):
            tmk = ("tm", tc % 2)
            tmfk = ("tmf", tc % 2)
            p.row_dma("sync", tmf[tc % 2][:], src_row(tc),
                      reads=["x2src"], writes=[tmfk])
            p.v(lambda e, tc=tc: e.tensor_copy(tm[tc % 2][:], tmf[tc % 2][:]),
                reads=[tmfk], writes=[tmk])
            for dc in range(8):
                PT = P4c if dc % 2 == 0 else P4d
                ptk = ("P4", dc % 2)
                dst, dstk = dst_fn(dc, tc)
                p.t(lambda e, tc=tc, dc=dc, PT=PT: e.transpose(
                    PT[:], tm[tc % 2][:, 128 * dc:128 * dc + 128],
                    identb[:]), reads=[tmk, "identb"], writes=[ptk])
                p.v(lambda e, dst=dst, PT=PT: e.tensor_copy(dst, PT[:]),
                    reads=[ptk], writes=[dstk])

    if phases >= 2:
        load_transpose(lambda tc: x2[128 * tc:128 * tc + 128, :], 32,
                       lambda dc, tc: (xT[dc][:, 128 * tc:128 * tc + 128], ("xtT", dc)))
        load_transpose(x2own_row, 16, xo_tile)

    for i in range(8 if phases >= 3 else 0):
        p.dma("sync", lambda e, i=i: e.dma_start(
            out=tmf[i % 2][:], in_=wv[128 * i:128 * i + 128, :]),
            writes=[("tmf", i % 2)])
        p.v(lambda e, i=i: e.tensor_copy(V[i][:], tmf[i % 2][:]),
            reads=[("tmf", i % 2)], writes=[("V", i)])
    for c in range(8 if phases >= 3 else 0):
        for r in range(8):
            PT = P4c if r % 2 == 0 else P4d
            ptk = ("P4", r % 2)
            p.t(lambda e, c=c, r=r, PT=PT: e.transpose(
                PT[:], V[r][:, 128 * c:128 * c + 128], identb[:]),
                reads=[("V", r), "identb"], writes=[ptk])
            p.v(lambda e, c=c, r=r, PT=PT: e.tensor_copy(
                wvT[c][:, 128 * r:128 * r + 128], PT[:]),
                reads=[ptk], writes=[("wvT", c)])

    for i in range(8 if phases >= 4 else 0):
        p.dma("sync", lambda e, i=i: e.dma_start(
            out=tmf[i % 2][:], in_=wqk[128 * i:128 * i + 128, :]),
            writes=[("tmf", i % 2)])
        p.v(lambda e, i=i: e.tensor_copy(V[i][:], tmf[i % 2][:]),
            reads=[("tmf", i % 2)], writes=[("V", i)])
    for oc in range(8 if phases >= 4 else 0):
        for sg in range(4):
            PS, pk = (P2, "P2") if sg % 2 == 0 else (P3, "P3")
            vidx = 8 + (1 if sg >= 2 else 0)
            for ic in range(8):
                vt = 8 + 2 * ic + (1 if sg >= 2 else 0)
                coff = 512 * (sg % 2)
                p.t(lambda e, oc=oc, ic=ic, vt=vt, coff=coff, PS=PS: e.matmul(
                    PS[:], V[ic][:, 128 * oc:128 * oc + 128],
                    V[vt][:, coff:coff + 512],
                    start=(ic == 0), stop=(ic == 7)),
                    reads=[("V", ic), ("V", vt)], writes=[pk])
            p.s(lambda e, oc=oc, sg=sg, PS=PS: e.activation(
                qT[oc][:, 512 * sg:512 * sg + 512], PS[:], AF.Copy,
                scale=1.0 / 32.0), reads=[pk], writes=[("qT", oc)])

    for tc in range(32 if phases >= 5 else 0):
        for half, PS, pk in ((0, P0, "P0"), (1, P1, "P1")):
            for ec in range(8):
                p.t(lambda e, tc=tc, ec=ec, half=half, PS=PS: e.matmul(
                    PS[:], xT[ec][:, 128 * tc:128 * tc + 128],
                    wvT[ec][:, 512 * half:512 * half + 512],
                    start=(ec == 0), stop=(ec == 7)),
                    reads=[("xtT", ec), ("wvT", ec)], writes=[pk])
            p.v(lambda e, tc=tc, half=half, PS=PS: e.tensor_copy(
                V[tc][:, 512 * half:512 * half + 512], PS[:]),
                reads=[pk], writes=[("V", tc)])

    nslots = 16 if phases >= 7 else (1 if phases >= 6 else 0)
    for sl in range(nslots):
        nA = sl // 2 + 1
        nB = 2 * sl + 2
        for kc in range(nA):
            PS, pk = (P2, "P2") if kc % 2 == 0 else (P3, "P3")
            for dc in range(8):
                p.t(lambda e, sl=sl, kc=kc, dc=dc, PS=PS: e.matmul(
                    PS[:], qT[dc][:, 128 * sl:128 * sl + 128],
                    xT[dc][:, 512 * kc:512 * kc + 512],
                    start=(dc == 0), stop=(dc == 7)),
                    reads=[("qT", dc), ("xtT", dc)], writes=[pk])
            p.v(lambda e, PS=PS: e.tensor_add(tmf[0][:, 0:512], PS[:], a0[:]),
                reads=[pk, "a0"], writes=[("tmf", 0)])
            if kc == nA - 1:
                p.v(lambda e, sl=sl: e.tensor_add(
                    tmf[0][:, 0:512], tmf[0][:, 0:512],
                    tri_a[:, 512 * (sl % 2):512 * (sl % 2) + 512]),
                    reads=[("tmf", 0), "tri_a"], writes=[("tmf", 0)])
            p.v(lambda e, kc=kc: e.reduce_max(mcols[:, kc:kc + 1],
                                               tmf[0][:, 0:512], axis=AX),
                reads=[("tmf", 0)], writes=["mcols"])
        p.t(lambda e, sl=sl, nA=nA: e.matmul(P5[:, 0:nA], onesrow_f[:],
                                             offas[0:1, 8 * sl:8 * sl + nA],
                                             start=True, stop=True),
            reads=["onesrow_f", "offas"], writes=["P5"])
        p.v(lambda e, nA=nA: e.tensor_add(mcols[:, 0:nA], mcols[:, 0:nA], P5[:, 0:nA]),
            reads=["mcols", "P5"], writes=["mcols"])
        p.v(lambda e, nA=nA: e.reduce_max(mraw[:], mcols[:, 0:nA], axis=AX),
            reads=["mcols"], writes=["mraw"])
        p.v(lambda e, sl=sl: e.tensor_scalar(mneg[:], mraw[:], qpss[:, sl:sl + 1],
                                             -1.0, op0=OP.add, op1=OP.mult),
            reads=["mraw", "qpss"], writes=["mneg"])
        p.t(lambda e: e.transpose(P5[0:1, 0:128], mneg[:], identf[:]),
            reads=["mneg", "identf"], writes=["P5"])
        p.v(lambda e: e.tensor_copy(mrowb[0:1, 0:128], P5[0:1, 0:128]),
            reads=["P5"], writes=["mrowb"])
        for kc in range(nB):
            PS, pk = (P5, "P5") if kc % 2 == 0 else (P7, "P7")
            for dc in range(8):
                p.t(lambda e, sl=sl, kc=kc, dc=dc, PS=PS: e.matmul(
                    PS[:], xT[dc][:, 128 * kc:128 * kc + 128],
                    qT[dc][:, 128 * sl:128 * sl + 128],
                    start=(dc == 0), stop=False),
                    reads=[("xtT", dc), ("qT", dc)], writes=[pk])
            p.t(lambda e, PS=PS: e.matmul(
                PS[:], onesrow_b[:], mrowb[0:1, 0:128],
                start=False, stop=True),
                reads=["onesrow_b", "mrowb"], writes=[pk])
            cidx = colmap[(sl, kc)]
            if kc >= 2 * sl:
                idx = kc - 2 * sl
                p.v(lambda e, idx=idx, PS=PS: e.tensor_add(
                    pstage[:], PS[:], tri_b[:, 128 * idx:128 * idx + 128]),
                    reads=[pk, "tri_b"], writes=["pstage"])
                src, srck = pstage, "pstage"
            else:
                src, srck = PS, pk
            p.s(lambda e, kc=kc, cidx=cidx, src=src: e.activation(
                ptb[kc][:], src[:], AF.Exp, bias=biascol[:, cidx:cidx + 1]),
                reads=[srck, "biascol"], writes=[("ptb", kc)])
        for kc in range(nB):
            p.t(lambda e, sl=sl, kc=kc, nB=nB: e.matmul(
                P2[0:1, 0:128], onescol_b[:], ptb[kc][:],
                start=(kc == 0), stop=(kc == nB - 1)),
                reads=["onescol_b", ("ptb", kc)], writes=["P2"])
        p.v(lambda e: e.tensor_copy(sums_sb[:], P2[0:1, 0:128]),
            reads=["P2"], writes=["sums_sb"])
        p.t(lambda e: e.transpose(P5[:, 0:1], sums_sb[:], identf[0:1, 0:1]),
            reads=["sums_sb", "identf"], writes=["P5"])
        p.v(lambda e: e.reciprocal(rec[:], P5[:, 0:1]), reads=["P5"], writes=["rec"])
        for vc in range(8):
            PA, pak = (P0, "P0") if vc % 2 == 0 else (P1, "P1")
            for kc in range(nB):
                p.t(lambda e, sl=sl, kc=kc, vc=vc, PA=PA, nB=nB: e.matmul(
                    PA[:, 0:128],
                    V[kc][:, 128 * vc:128 * vc + 128], ptb[kc][:],
                    start=(kc == 0), stop=(kc == nB - 1)),
                    reads=[("V", kc), ("ptb", kc)], writes=[pak])
            PT, ptk = (P5, "P5") if vc % 2 == 0 else (P7, "P7")
            ab, abk = att_sb[vc % 2], ("att_sb", vc % 2)
            p.v(lambda e, PA=PA, ab=ab: e.tensor_copy(ab[:], PA[:, 0:128]),
                reads=[pak], writes=[abk])
            p.t(lambda e, ab=ab, PT=PT: e.transpose(PT[:], ab[:], identf[:]),
                reads=[abk, "identf"], writes=[ptk])
            p.v(lambda e, vc=vc, PT=PT: e.tensor_copy(
                at_tm[:, 128 * vc:128 * vc + 128], PT[:]),
                reads=[ptk], writes=["at_tm"])
        p.row_dma("sync", x2r[:], x2own_row(sl),
                  reads=["x2src"], writes=["x2r"])
        p.v(lambda e: e.scalar_tensor_tensor(
            s1[:], at_tm[:], rec[:], x2r[:], op0=OP.mult, op1=OP.add,
            accum_out=acc1[:]),
            reads=["at_tm", "rec", "x2r"], writes=["s1", "acc1"])
        p.s(lambda e: e.activation(tm[0][:], s1[:], AF.Square, accum_out=acc2[:]),
            reads=["s1"], writes=[("tm", 0), "acc2"])
        p.v(lambda e: e.tensor_scalar_mul(ln1[:], acc1[:], 1.0 / D),
            reads=["acc1"], writes=["ln1"])
        p.v(lambda e: e.tensor_scalar_mul(ln2[:], acc2[:], 1.0 / D),
            reads=["acc2"], writes=["ln2"])
        p.v(lambda e: e.tensor_mul(ln3[:], ln1[:], ln1[:]),
            reads=["ln1"], writes=["ln3"])
        p.v(lambda e: e.tensor_sub(ln2[:], ln2[:], ln3[:]),
            reads=["ln2", "ln3"], writes=["ln2"])
        p.s(lambda e: e.activation(ln3[:], ln2[:], AF.Sqrt, bias=epsc[:]),
            reads=["ln2", "epsc"], writes=["ln3"])
        p.v(lambda e: e.reciprocal(ln2[:], ln3[:]), reads=["ln3"], writes=["ln2"])
        p.v(lambda e: e.tensor_scalar_mul(ln3[:], ln1[:], -1.0),
            reads=["ln1"], writes=["ln3"])
        p.v(lambda e: e.tensor_scalar(at_tm[:], s1[:], ln3[:], ln2[:],
                                      op0=OP.add, op1=OP.mult),
            reads=["s1", "ln3", "ln2"], writes=["at_tm"])
        p.dma("sync", lambda e, sl=sl: e.dma_start(
            out=x3own[128 * sl:128 * sl + 128, :], in_=at_tm[:]),
            reads=["at_tm"], writes=[("x3own", sl)])

    if phases < 7:
        for sl2 in range(16 - nslots):
            sl = nslots + sl2
            p.dma("sync", lambda e, sl=sl: e.dma_start(
                out=x3own[128 * sl:128 * sl + 128, :], in_=tmf[1][:]),
                reads=[("tmf", 1)], writes=[("x3own", sl)])





def build_l1(phases=99):
    nc = bass.Bass("TRN2", target_bir_lowering=False, dynamic_dma_scratch_size=2048)
    p = Prog(nc)
    colmap, NBCOL = _l1_colmap()
    x2 = nc.dram_tensor("x2", [T, D], F32, kind="ExternalInput").ap()
    x2own = nc.dram_tensor("x2own", [HALF, D], F32, kind="ExternalInput").ap()
    wqk = nc.dram_tensor("wqk", [D, D], F32, kind="ExternalInput").ap()
    wv = nc.dram_tensor("wv", [D, D], F32, kind="ExternalInput").ap()
    slope = nc.dram_tensor("slope", [1, 1], F32, kind="ExternalInput").ap()
    d0_in = nc.dram_tensor("d0", [128, 512], F32, kind="ExternalInput").ap()
    ident_in = nc.dram_tensor("ident", [128, 128], F32, kind="ExternalInput").ap()
    identb_in = nc.dram_tensor("identb_in", [128, 128], BF16, kind="ExternalInput").ap()
    tri_a_in = nc.dram_tensor("tri_a", [2, 128, 512], BF16, kind="ExternalInput").ap()
    tri_b_in = nc.dram_tensor("tri_b", [2, 128, 128], BF16, kind="ExternalInput").ap()
    offa_in = nc.dram_tensor("offa", [1, 128], F32, kind="ExternalInput").ap()
    kpos_in = nc.dram_tensor("kpos", [128, NBCOL], F32, kind="ExternalInput").ap()
    kill_in = nc.dram_tensor("kill", [128, NBCOL], F32, kind="ExternalInput").ap()
    qps_in = nc.dram_tensor("qps", [128, 16], F32, kind="ExternalInput").ap()
    x3own = nc.dram_tensor("x3own", [HALF, D], F32, kind="ExternalOutput").ap()

    ctx = ExitStack()
    x2own_row = lambda sl: x2own[128 * sl:128 * sl + 128, :]
    emit_l1(nc, p, ctx, "", x2, x2own_row, wqk, wv, slope, d0_in, ident_in,
            identb_in, tri_a_in, tri_b_in, offa_in, kpos_in, kill_in, qps_in,
            x3own, NBCOL, colmap)
    p.emit()
    ctx.close()
    return nc


def run_l1(x2_full, wqk, wv, slope):
    nc = build_l1()
    in_maps = []
    for c in range(8):
        b, par = c // 2, c % 2
        cc = _l1_consts(par)
        qt = 2 * np.arange(16) + par
        rows = (qt[:, None] * 128 + np.arange(128)[None, :]).reshape(-1)
        in_maps.append({
            "x2": np.ascontiguousarray(x2_full[b]),
            "x2own": np.ascontiguousarray(x2_full[b][rows]),
            "wqk": wqk, "wv": wv,
            "slope": np.asarray(slope, np.float32).reshape(1, 1),
            "d0": cc["d0"], "ident": cc["ident"],
            "identb_in": cc["ident"].astype(BFD),
            "tri_a": cc["tri_a"].astype(BFD),
            "tri_b": cc["tri_b"].astype(BFD), "offa": cc["offa"],
            "kpos": cc["kpos"], "kill": cc["kill"], "qps": cc["qps"],
        })
    res = run_bass_kernel_spmd(nc, in_maps, core_ids=list(range(8)))
    x3 = np.zeros((B, T, D), np.float32)
    for c in range(8):
        b, par = c // 2, c % 2
        qt = 2 * np.arange(16) + par
        rows = (qt[:, None] * 128 + np.arange(128)[None, :]).reshape(-1)
        x3[b][rows] = res.results[c]["x3own"]
    return x3





def emit_moe(nc, p, ctx, pfx, x0own, pa_row, pb_row, gate_w, w1_in, b1t_in, w2_in, b2_in, lng_bc_in, lnb_in, identb_in, ident_in, x2own, phases=99):
    NT = 16
    x1own = nc.dram_tensor(pfx + "x1own_d", [HALF, D], F32, kind="Internal").ap()
    acc_d = nc.dram_tensor(pfx + "acc_d", [HALF, D], F32, kind="Internal").ap()
    sb = lambda name, shape, dt: ctx.enter_context(nc.sbuf_tensor(pfx + name, shape, dt))
    psa = lambda name, shape, dt=F32: ctx.enter_context(nc.psum_tensor(pfx + name, shape, dt))

    x1T = [sb(f"x1T{i}", [128, HALF], BF16) for i in range(8)]
    w1b = [sb(f"w1b{i}", [128, 2 * D], BF16) for i in range(8)]
    w2b = [sb(f"w2b{i}", [128, D], BF16) for i in range(16)]
    hT = [sb(f"hT{i}", [128, 512], BF16) for i in range(16)]
    tm = [sb(f"tm{i}", [128, D], BF16) for i in range(2)]
    tmf = [sb(f"tmf{i}", [128, D], F32) for i in range(2)]
    xtf = [sb(f"xtf{i}", [128, 128], F32) for i in range(2)]
    identb = sb("identb", [128, 128], BF16)
    identf = sb("identf", [128, 128], F32)
    onesrow_b = sb("onesrow_b", [1, 128], BF16)
    sc = sb("sc", [128, 128], F32)
    gate_sb = sb("gate_sb", [128, 64], F32)
    rank = sb("rank", [128, 8], F32)
    tmp8 = sb("tmp8", [128, 8], F32)
    t8b = sb("t8b", [128, 8], F32)
    ind1 = sb("ind1", [128, 8], F32)
    ind2 = sb("ind2", [128, 8], F32)
    c1 = sb("c1", [128, 1], F32)
    c2 = sb("c2", [128, 1], F32)
    c3 = sb("c3", [128, 1], F32)
    wE = sb("wE", [128, 128], F32)
    wET = sb("wET", [8, HALF], F32)
    lnb_sb = sb("lnb_sb", [8, D], F32)
    b1sb = [sb(f"b1sb{i}", [128, 16], F32) for i in range(2)]
    gbc = [sb(f"gbc{i}", [128, D], F32) for i in range(2)]
    b2f = sb("b2f", [1, D], F32)
    b2b = sb("b2b", [1, D], BF16)
    x1r = [sb(f"x1r{i}", [128, D], F32) for i in range(2)]
    stile = sb("stile", [128, D], F32)
    y1 = sb("y1", [128, D], F32)
    ystage = [sb(f"ystage{i}", [128, D], F32) for i in range(2)]
    accsb = [sb(f"accsb{i}", [128, D], F32) for i in range(2)]
    acc1a = sb("acc1a", [128, 1], F32)
    acc1b = sb("acc1b", [128, 1], F32)
    acc2 = sb("acc2", [128, 1], F32)
    ln1 = sb("ln1", [128, 1], F32)
    ln2 = sb("ln2", [128, 1], F32)
    ln3 = sb("ln3", [128, 1], F32)
    eps6 = sb("eps6", [128, 1], F32)
    eps5 = sb("eps5", [128, 1], F32)

    P0 = psa("P0", [128, 512])
    P1 = psa("P1", [128, 512])
    P2 = psa("P2", [128, 512])
    P3 = psa("P3", [128, 512])
    P4c = psa("P4c", [128, 128], BF16)
    P4d = psa("P4d", [128, 128], BF16)
    P5 = psa("P5", [128, 128])
    P7 = psa("P7", [128, 128])

    p.dma("sync", lambda e: e.dma_start(out=identb[:], in_=identb_in), writes=["identb"])
    p.dma("sync", lambda e: e.dma_start(out=identf[:], in_=ident_in), writes=["identf"])
    p.dma("sync", lambda e: e.dma_start(out=lnb_sb[:], in_=lnb_in), writes=["lnb_sb"])
    for dc in range(8):
        p.dma("sync", lambda e, dc=dc: e.dma_start(
            out=gate_sb[:, 8 * dc:8 * dc + 8], in_=gate_w[128 * dc:128 * dc + 128, :]),
            writes=["gate_sb"])
    p.v(lambda e: e.memset(onesrow_b[:], 1.0), writes=["onesrow_b"])
    p.v(lambda e: e.memset(eps6[:], 1e-6), writes=["eps6"])
    p.v(lambda e: e.memset(eps5[:], 1e-5), writes=["eps5"])

    for tc in range(NT):
        p.dma("sync", lambda e, tc=tc: e.dma_start(
            out=x1r[0][:], in_=x0own[128 * tc:128 * tc + 128, :]),
            writes=[("x1r", 0)])
        p.row_dma("sync", x1r[1][:], pa_row(tc),
                  reads=["pgsrc"], writes=[("x1r", 1)])
        p.row_dma("sync", accsb[0][:], pb_row(tc),
                  reads=["pgsrc"], writes=[("accsb", 0)])
        p.v(lambda e: e.tensor_add(x1r[1][:], x1r[1][:], accsb[0][:]),
            reads=[("x1r", 1), ("accsb", 0)], writes=[("x1r", 1)])
        p.v(lambda e: e.scalar_tensor_tensor(
            stile[:], x1r[1][:], 1.0, x1r[0][:], op0=OP.mult, op1=OP.add,
            accum_out=acc1a[:]),
            reads=[("x1r", 0), ("x1r", 1)], writes=["stile", "acc1a"])
        p.s(lambda e: e.activation(tm[0][:], stile[:], AF.Square, accum_out=acc2[:]),
            reads=["stile"], writes=[("tm", 0), "acc2"])
        p.v(lambda e: e.tensor_scalar_mul(ln1[:], acc1a[:], 1.0 / D),
            reads=["acc1a"], writes=["ln1"])
        p.v(lambda e: e.tensor_scalar_mul(ln2[:], acc2[:], 1.0 / D),
            reads=["acc2"], writes=["ln2"])
        p.v(lambda e: e.tensor_mul(ln3[:], ln1[:], ln1[:]),
            reads=["ln1"], writes=["ln3"])
        p.v(lambda e: e.tensor_sub(ln2[:], ln2[:], ln3[:]),
            reads=["ln2", "ln3"], writes=["ln2"])
        p.s(lambda e: e.activation(ln3[:], ln2[:], AF.Sqrt, bias=eps5[:]),
            reads=["ln2", "eps5"], writes=["ln3"])
        p.v(lambda e: e.reciprocal(ln2[:], ln3[:]), reads=["ln3"], writes=["ln2"])
        p.v(lambda e: e.tensor_scalar_mul(ln3[:], ln1[:], -1.0),
            reads=["ln1"], writes=["ln3"])
        p.v(lambda e: e.tensor_scalar(y1[:], stile[:], ln3[:], ln2[:],
                                      op0=OP.add, op1=OP.mult),
            reads=["stile", "ln3", "ln2"], writes=["y1"])
        p.dma("sync", lambda e, tc=tc: e.dma_start(
            out=x1own[128 * tc:128 * tc + 128, :], in_=y1[:]),
            reads=["y1"], writes=[("x1d", tc)])
        tmk = ("tm", tc % 2)
        p.v(lambda e, tc=tc: e.tensor_copy(tm[tc % 2][:], y1[:]),
            reads=["y1"], writes=[tmk])
        for dc in range(8):
            PT, ptk = (P4c, ("P4", 0)) if dc % 2 == 0 else (P4d, ("P4", 1))
            p.t(lambda e, tc=tc, dc=dc, PT=PT: e.transpose(
                PT[:], tm[tc % 2][:, 128 * dc:128 * dc + 128], identb[:]),
                reads=[tmk, "identb"], writes=[ptk])
            p.v(lambda e, tc=tc, dc=dc, PT=PT: e.tensor_copy(
                x1T[dc][:, 128 * tc:128 * tc + 128], PT[:]),
                reads=[ptk], writes=[("x1T", dc)])
            PF, pfk = (P5, "P5") if dc % 2 == 0 else (P7, "P7")
            p.t(lambda e, tc=tc, dc=dc, PF=PF: e.transpose(
                PF[:], y1[:, 128 * dc:128 * dc + 128], identf[:]),
                reads=["y1", "identf"], writes=[pfk])
            p.v(lambda e, dc=dc, PF=PF: e.tensor_copy(xtf[dc % 2][:], PF[:]),
                reads=[pfk], writes=[("xtf", dc % 2)])
            p.t(lambda e, tc=tc, dc=dc: e.matmul(
                P2[:, 0:8], xtf[dc % 2][:], gate_sb[:, 8 * dc:8 * dc + 8],
                start=(dc == 0), stop=(dc == 7)),
                reads=[("xtf", dc % 2), "gate_sb"], writes=["P2"])
        p.v(lambda e, tc=tc: e.tensor_copy(sc[:, 8 * tc:8 * tc + 8], P2[:, 0:8]),
            reads=["P2"], writes=["sc"])

    for tc in range(NT):
        scs = sc[:, 8 * tc:8 * tc + 8]
        for e_ in range(8):
            p.v(lambda e, scs=scs, e_=e_: e.tensor_scalar(
                tmp8[:], scs, scs[:, e_:e_ + 1], None, op0=OP.is_gt),
                reads=["sc"], writes=["tmp8"])
            p.v(lambda e, e_=e_: e.reduce_sum(rank[:, e_:e_ + 1], tmp8[:], axis=AX),
                reads=["tmp8"], writes=["rank"])
            if e_ > 0:
                p.v(lambda e, scs=scs, e_=e_: e.tensor_scalar(
                    t8b[:], scs, scs[:, e_:e_ + 1], None, op0=OP.is_equal),
                    reads=["sc"], writes=["t8b"])
                p.v(lambda e, e_=e_: e.reduce_sum(c1[:], t8b[:, 0:e_], axis=AX),
                    reads=["t8b"], writes=["c1"])
                p.v(lambda e, e_=e_: e.tensor_add(
                    rank[:, e_:e_ + 1], rank[:, e_:e_ + 1], c1[:]),
                    reads=["rank", "c1"], writes=["rank"])
        p.v(lambda e: e.tensor_scalar(ind1[:], rank[:], 0.0, None, op0=OP.is_equal),
            reads=["rank"], writes=["ind1"])
        p.v(lambda e: e.tensor_scalar(ind2[:], rank[:], 1.0, None, op0=OP.is_equal),
            reads=["rank"], writes=["ind2"])
        p.v(lambda e, scs=scs: e.tensor_mul(tmp8[:], scs, ind1[:]),
            reads=["sc", "ind1"], writes=["tmp8"])
        p.v(lambda e: e.reduce_sum(c1[:], tmp8[:], axis=AX),
            reads=["tmp8"], writes=["c1"])
        p.v(lambda e, scs=scs: e.tensor_mul(tmp8[:], scs, ind2[:]),
            reads=["sc", "ind2"], writes=["tmp8"])
        p.v(lambda e: e.reduce_sum(c2[:], tmp8[:], axis=AX),
            reads=["tmp8"], writes=["c2"])
        p.v(lambda e: e.tensor_sub(c3[:], c2[:], c1[:]),
            reads=["c1", "c2"], writes=["c3"])
        p.s(lambda e: e.activation(c2[:], c3[:], AF.Sigmoid),
            reads=["c3"], writes=["c2"])
        p.v(lambda e: e.tensor_scalar(c1[:], c2[:], -1.0, 1.0, op0=OP.mult, op1=OP.add),
            reads=["c2"], writes=["c1"])
        p.v(lambda e: e.tensor_scalar_mul(t8b[:], ind2[:], c2[:]),
            reads=["ind2", "c2"], writes=["t8b"])
        p.v(lambda e, tc=tc: e.scalar_tensor_tensor(
            wE[:, 8 * tc:8 * tc + 8], ind1[:], c1[:], t8b[:],
            op0=OP.mult, op1=OP.add),
            reads=["ind1", "c1", "t8b"], writes=["wE"])
        PF, pfk = (P5, "P5") if tc % 2 == 0 else (P7, "P7")
        p.t(lambda e, tc=tc, PF=PF: e.transpose(
            PF[0:8, 0:128], wE[:, 8 * tc:8 * tc + 8], identf[:]),
            reads=["wE", "identf"], writes=[pfk])
        p.v(lambda e, tc=tc, PF=PF: e.tensor_copy(
            wET[:, 128 * tc:128 * tc + 128], PF[0:8, 0:128]),
            reads=[pfk], writes=["wET"])

    for tc in range(NT):
        for oh in range(2):
            PS, pk = (P0, "P0") if oh == 0 else (P1, "P1")
            p.t(lambda e, tc=tc, oh=oh, PS=PS: e.matmul(
                PS[:], wET[:, 128 * tc:128 * tc + 128],
                lnb_sb[:, 512 * oh:512 * oh + 512], start=True, stop=True),
                reads=["wET", "lnb_sb"], writes=[pk])
            p.v(lambda e, tc=tc, oh=oh, PS=PS: e.tensor_copy(
                ystage[tc % 2][:, 512 * oh:512 * oh + 512], PS[:]),
                reads=[pk], writes=[("ystage", tc % 2)])
        p.dma("sync", lambda e, tc=tc: e.dma_start(
            out=acc_d[128 * tc:128 * tc + 128, :], in_=ystage[tc % 2][:]),
            reads=[("ystage", tc % 2)], writes=[("acc", tc)])

    nexp = 8 if phases >= 4 else 0
    for ex in range(nexp):
        exk = ex % 2
        p.dma("sync", lambda e, ex=ex: e.dma_start(
            out=b1sb[ex % 2][:], in_=b1t_in[ex]), writes=[("b1sb", exk)])
        p.dma("sync", lambda e, ex=ex: e.dma_start(
            out=gbc[ex % 2][:], in_=lng_bc_in[ex]), writes=[("gbc", exk)])
        p.dma("sync", lambda e, ex=ex: e.dma_start(
            out=b2f[:], in_=b2_in[ex:ex + 1, :]), writes=["b2f"])
        p.v(lambda e: e.tensor_copy(b2b[:], b2f[:]), reads=["b2f"], writes=["b2b"])
        for dc in range(8):
            for hh in range(2):
                p.dma("sync", lambda e, ex=ex, dc=dc, hh=hh: e.dma_start(
                    out=tmf[hh][:],
                    in_=w1_in[ex, 128 * dc:128 * dc + 128, D * hh:D * hh + D]),
                    writes=[("tmf", hh)])
                p.v(lambda e, dc=dc, hh=hh: e.tensor_copy(
                    w1b[dc][:, D * hh:D * hh + D], tmf[hh][:]),
                    reads=[("tmf", hh)], writes=[("w1b", dc)])
        for hc in range(16):
            p.dma("sync", lambda e, ex=ex, hc=hc: e.dma_start(
                out=tmf[hc % 2][:], in_=w2_in[ex, 128 * hc:128 * hc + 128, :]),
                writes=[("tmf", hc % 2)])
            p.s(lambda e, hc=hc: e.activation(w2b[hc][:], tmf[hc % 2][:], AF.Copy),
                reads=[("tmf", hc % 2)], writes=[("w2b", hc)])
        for tt in range(4):
            for hc in range(16):
                PS, pk = (P2, "P2") if hc % 2 == 0 else (P3, "P3")
                for dc in range(8):
                    p.t(lambda e, tt=tt, hc=hc, dc=dc, PS=PS: e.matmul(
                        PS[:], w1b[dc][:, 128 * hc:128 * hc + 128],
                        x1T[dc][:, 512 * tt:512 * tt + 512],
                        start=(dc == 0), stop=(dc == 7)),
                        reads=[("w1b", dc), ("x1T", dc)], writes=[pk])
                p.s(lambda e, hc=hc, PS=PS, exk=exk: e.activation(
                    hT[hc][:], PS[:], AF.Gelu, bias=b1sb[exk][:, hc:hc + 1]),
                    reads=[pk, ("b1sb", exk)], writes=[("hT", hc)])
            for ts4 in range(4):
                tc = 4 * tt + ts4
                for oh in range(2):
                    PS, pk = (P0, "P0") if oh == 0 else (P1, "P1")
                    for hc in range(16):
                        p.t(lambda e, ts4=ts4, oh=oh, hc=hc, PS=PS: e.matmul(
                            PS[:], hT[hc][:, 128 * ts4:128 * ts4 + 128],
                            w2b[hc][:, 512 * oh:512 * oh + 512],
                            start=(hc == 0), stop=False),
                            reads=[("hT", hc), ("w2b", hc)], writes=[pk])
                    p.t(lambda e, oh=oh, PS=PS: e.matmul(
                        PS[:], onesrow_b[:], b2b[0:1, 512 * oh:512 * oh + 512],
                        start=False, stop=True),
                        reads=["onesrow_b", "b2b"], writes=[pk])
                p.dma("sync", lambda e, tc=tc: e.dma_start(
                    out=x1r[tc % 2][:], in_=x1own[128 * tc:128 * tc + 128, :]),
                    reads=[("x1d", tc)], writes=[("x1r", tc % 2)])
                p.v(lambda e, tc=tc: e.scalar_tensor_tensor(
                    stile[:, 0:512], P0[:], 1.0, x1r[tc % 2][:, 0:512],
                    op0=OP.mult, op1=OP.add, accum_out=acc1a[:]),
                    reads=["P0", ("x1r", tc % 2)], writes=["stile", "acc1a"])
                p.v(lambda e, tc=tc: e.scalar_tensor_tensor(
                    stile[:, 512:1024], P1[:], 1.0, x1r[tc % 2][:, 512:1024],
                    op0=OP.mult, op1=OP.add, accum_out=acc1b[:]),
                    reads=["P1", ("x1r", tc % 2)], writes=["stile", "acc1b"])
                p.s(lambda e: e.activation(tm[0][:], stile[:], AF.Square,
                                           accum_out=acc2[:]),
                    reads=["stile"], writes=[("tm", 0), "acc2"])
                p.v(lambda e: e.tensor_add(acc1a[:], acc1a[:], acc1b[:]),
                    reads=["acc1a", "acc1b"], writes=["acc1a"])
                p.v(lambda e: e.tensor_scalar_mul(ln1[:], acc1a[:], 1.0 / D),
                    reads=["acc1a"], writes=["ln1"])
                p.v(lambda e: e.tensor_scalar_mul(ln2[:], acc2[:], 1.0 / D),
                    reads=["acc2"], writes=["ln2"])
                p.v(lambda e: e.tensor_mul(ln3[:], ln1[:], ln1[:]),
                    reads=["ln1"], writes=["ln3"])
                p.v(lambda e: e.tensor_sub(ln2[:], ln2[:], ln3[:]),
                    reads=["ln2", "ln3"], writes=["ln2"])
                p.s(lambda e: e.activation(ln3[:], ln2[:], AF.Sqrt, bias=eps6[:]),
                    reads=["ln2", "eps6"], writes=["ln3"])
                p.v(lambda e: e.reciprocal(ln2[:], ln3[:]), reads=["ln3"], writes=["ln2"])
                p.v(lambda e: e.tensor_scalar_mul(ln3[:], ln1[:], -1.0),
                    reads=["ln1"], writes=["ln3"])
                p.v(lambda e: e.tensor_scalar(y1[:], stile[:], ln3[:], ln2[:],
                                              op0=OP.add, op1=OP.mult),
                    reads=["stile", "ln3", "ln2"], writes=["y1"])
                p.g(lambda e, exk=exk: e.tensor_mul(y1[:], y1[:], gbc[exk][:]),
                    reads=["y1", ("gbc", exk)], writes=["y1"])
                col = 8 * tc + ex
                p.dma("sync", lambda e, tc=tc: e.dma_start(
                    out=accsb[tc % 2][:], in_=acc_d[128 * tc:128 * tc + 128, :]),
                    reads=[("acc", tc)], writes=[("accsb", tc % 2)])
                p.v(lambda e, col=col, tc=tc: e.scalar_tensor_tensor(
                    ystage[tc % 2][:], y1[:], wE[:, col:col + 1], accsb[tc % 2][:],
                    op0=OP.mult, op1=OP.add),
                    reads=["y1", "wE", ("accsb", tc % 2)], writes=[("ystage", tc % 2)])
                p.dma("sync", lambda e, tc=tc: e.dma_start(
                    out=acc_d[128 * tc:128 * tc + 128, :], in_=ystage[tc % 2][:]),
                    reads=[("ystage", tc % 2)], writes=[("acc", tc)])

    for tc in range(NT):
        p.dma("sync", lambda e, tc=tc: e.dma_start(
            out=accsb[tc % 2][:], in_=acc_d[128 * tc:128 * tc + 128, :]),
            reads=[("acc", tc)], writes=[("accsb", tc % 2)])
        p.dma("sync", lambda e, tc=tc: e.dma_start(
            out=x1r[tc % 2][:], in_=x1own[128 * tc:128 * tc + 128, :]),
            reads=[("x1d", tc)], writes=[("x1r", tc % 2)])
        p.v(lambda e, tc=tc: e.scalar_tensor_tensor(
            stile[:], accsb[tc % 2][:], 1.0, x1r[tc % 2][:],
            op0=OP.mult, op1=OP.add, accum_out=acc1a[:]),
            reads=[("accsb", tc % 2), ("x1r", tc % 2)], writes=["stile", "acc1a"])
        p.s(lambda e: e.activation(tm[0][:], stile[:], AF.Square, accum_out=acc2[:]),
            reads=["stile"], writes=[("tm", 0), "acc2"])
        p.v(lambda e: e.tensor_scalar_mul(ln1[:], acc1a[:], 1.0 / D),
            reads=["acc1a"], writes=["ln1"])
        p.v(lambda e: e.tensor_scalar_mul(ln2[:], acc2[:], 1.0 / D),
            reads=["acc2"], writes=["ln2"])
        p.v(lambda e: e.tensor_mul(ln3[:], ln1[:], ln1[:]),
            reads=["ln1"], writes=["ln3"])
        p.v(lambda e: e.tensor_sub(ln2[:], ln2[:], ln3[:]),
            reads=["ln2", "ln3"], writes=["ln2"])
        p.s(lambda e: e.activation(ln3[:], ln2[:], AF.Sqrt, bias=eps5[:]),
            reads=["ln2", "eps5"], writes=["ln3"])
        p.v(lambda e: e.reciprocal(ln2[:], ln3[:]), reads=["ln3"], writes=["ln2"])
        p.v(lambda e: e.tensor_scalar_mul(ln3[:], ln1[:], -1.0),
            reads=["ln1"], writes=["ln3"])
        p.v(lambda e, tc=tc: e.tensor_scalar(y1[:], stile[:], ln3[:], ln2[:],
                                             op0=OP.add, op1=OP.mult),
            reads=["stile", "ln3", "ln2"], writes=["y1"])
        p.dma("sync", lambda e, tc=tc: e.dma_start(
            out=x2own[128 * tc:128 * tc + 128, :], in_=y1[:]),
            reads=["y1"], writes=[("x2own", tc)])





def build_moe(phases=99):
    nc = bass.Bass("TRN2", target_bir_lowering=False, dynamic_dma_scratch_size=2048)
    p = Prog(nc)
    x0own = nc.dram_tensor("x0own", [HALF, D], F32, kind="ExternalInput").ap()
    paown = nc.dram_tensor("paown", [HALF, D], F32, kind="ExternalInput").ap()
    pbown = nc.dram_tensor("pbown", [HALF, D], F32, kind="ExternalInput").ap()
    gate_w = nc.dram_tensor("gate_w", [D, 8], F32, kind="ExternalInput").ap()
    w1_in = nc.dram_tensor("w1", [8, D, 2 * D], F32, kind="ExternalInput").ap()
    b1t_in = nc.dram_tensor("b1t", [8, 128, 16], F32, kind="ExternalInput").ap()
    w2_in = nc.dram_tensor("w2", [8, 2 * D, D], F32, kind="ExternalInput").ap()
    b2_in = nc.dram_tensor("b2", [8, D], F32, kind="ExternalInput").ap()
    lng_bc_in = nc.dram_tensor("lng_bc", [8, 128, D], F32, kind="ExternalInput").ap()
    lnb_in = nc.dram_tensor("lnb", [8, D], F32, kind="ExternalInput").ap()
    identb_in = nc.dram_tensor("identb_in", [128, 128], BF16, kind="ExternalInput").ap()
    ident_in = nc.dram_tensor("ident", [128, 128], F32, kind="ExternalInput").ap()
    x2own = nc.dram_tensor("x2own", [HALF, D], F32, kind="ExternalOutput").ap()


    ctx = ExitStack()
    pa_row = lambda tc: paown[128 * tc:128 * tc + 128, :]
    pb_row = lambda tc: pbown[128 * tc:128 * tc + 128, :]
    emit_moe(nc, p, ctx, "", x0own, pa_row, pb_row, gate_w, w1_in, b1t_in, w2_in, b2_in, lng_bc_in, lnb_in, identb_in, ident_in, x2own, phases=phases)
    p.emit()
    ctx.close()
    return nc

def run_moe(x0_full, pa, pb, gate_w, w1, b1, w2, b2, ln_g, ln_b):
    nc = build_moe()
    in_maps = []
    for c in range(8):
        b, par = c // 2, c % 2
        sl = slice(par * HALF, (par + 1) * HALF)
        in_maps.append({
            "x0own": np.ascontiguousarray(x0_full[b, sl]),
            "paown": np.ascontiguousarray(pa[b][sl]),
            "pbown": np.ascontiguousarray(pb[b][sl]),
            "gate_w": gate_w, "w1": w1,
            "b1t": np.ascontiguousarray(b1.reshape(8, 16, 128).transpose(0, 2, 1)),
            "w2": w2, "b2": b2,
            "lng_bc": np.ascontiguousarray(
                np.broadcast_to(ln_g[:, None, :], (8, 128, D))),
            "lnb": ln_b,
            "identb_in": np.eye(128).astype(BFD),
            "ident": np.eye(128, dtype=np.float32),
        })
    res = run_bass_kernel_spmd(nc, in_maps, core_ids=list(range(8)))
    x2 = np.zeros((B, T, D), np.float32)
    for c in range(8):
        b, par = c // 2, c % 2
        x2[b, par * HALF:(par + 1) * HALF] = res.results[c]["x2own"]
    return x2



def _l1f_consts(par):
    W0 = 4 * par
    c = {}
    pp = np.arange(128)[:, None]
    cc = np.arange(640)[None, :]
    c["mb"] = np.where((cc - pp < 0) | (cc - pp > 512), NEG, 0.0).astype(np.float32)
    jj = np.arange(128)[None, :]
    c["tri0"] = np.where(pp < jj, NEG, 0.0).astype(np.float32)
    c["tri4"] = np.where(pp > jj, NEG, 0.0).astype(np.float32)
    kill = np.zeros((128, 240), np.float32)
    col = 0
    for ls in range(12):
        w = (W0 - 3 + ls) if ls < 7 else (W0 - 1 + (ls - 7))
        for qs in range(4):
            a0 = 128 * qs
            for kcr in range(5):
                kg = a0 + 128 * kcr + pp[:, 0] - 512
                if w == 0:
                    kill[:, col] = np.where(kg < 0, NEG, 0.0)
                col += 1
    c["kill"] = kill
    msk = np.ones((1, 512), np.float32)
    if par == 0:
        msk[0, 511] = 0.0
    c["mskrow"] = msk
    DH, WIN = 64, 512
    half = np.arange(0, DH, 2, dtype=np.float64)
    inv = 1.0 / 10000.0 ** (half / DH)
    base = (half + 0.4 * DH) / (1.4 * DH)
    tabs = np.zeros((8, 4, 128, 512), np.float32)
    for wi in range(8):
        w = W0 - 4 + wi
        t = np.arange(w * 512, w * 512 + 512, dtype=np.float64)
        freqs = t[:, None] * inv[None, :]
        power = (t - T // 2) / float(WIN // 2)
        scale = base[None, :] ** power[:, None]
        cos = np.cos(freqs); sin = np.sin(freqs)
        qc = np.concatenate([cos * scale, cos * scale], 1).T
        qs_ = np.concatenate([-sin * scale, sin * scale], 1).T
        kc = np.concatenate([cos / scale, cos / scale], 1).T
        ks_ = np.concatenate([-sin / scale, sin / scale], 1).T
        QK = 8.0
        tabs[wi, 0] = np.tile(qc * QK, (2, 1))
        tabs[wi, 1] = np.tile(qs_ * QK, (2, 1))
        tabs[wi, 2] = np.tile(kc, (2, 1))
        tabs[wi, 3] = np.tile(ks_, (2, 1))
    c["ropetabs"] = tabs
    P = np.zeros((128, 128), np.float32)
    for h in range(2):
        for j in range(32):
            P[64 * h + 32 + j, 64 * h + j] = 1.0
            P[64 * h + j, 64 * h + 32 + j] = 1.0
    c["swp"] = P
    return c


def _l1f_blk2():
    a = np.zeros((128, 2), np.float32)
    a[0:64, 0] = 1.0
    a[64:128, 1] = 1.0
    return a


def _l1f_exp2():
    a = np.zeros((2, 128), np.float32)
    a[0, 0:64] = 1.0
    a[1, 64:128] = 1.0
    return a


def _l1f_x3ext(x3_full_b, par):
    W0 = 4 * par
    g0 = (W0 - 4) * 512 - 1
    ext = np.zeros((4097, D), np.float32)
    lo = max(0, -g0)
    ext[lo:] = x3_full_b[g0 + lo:g0 + 4097]
    return np.ascontiguousarray(ext.T)



def emit_l1f(nc, p, ctx, pfx, x3extT, x3own_row, wqkv_in, qsc_in, ksc_in, wo_in, lng_in, lnb_in, w1a_in, w1g_in, b1a_in, b1g_in, w2_in, b2_in, mb_in, tri0_in, tri4_in, kill_in, mskrow_in, tabs_in, swp_in, ident_in, identb_in, blk2_in, exp2_in, outown, extk="x3extT_noop", phases=99):
    HID = 2730
    NHC = 22
    W = 512
    extb_d = nc.dram_tensor(pfx + "extb_d", [D, 4097], BF16, kind="Internal").ap()
    y1f_d = nc.dram_tensor(pfx + "y1f_d", [D, 3584], F32, kind="Internal").ap()
    y1b_d = nc.dram_tensor(pfx + "y1b_d", [D, 3584], BF16, kind="Internal").ap()
    x4f_d = nc.dram_tensor(pfx + "x4f_d", [D, 3073], F32, kind="Internal").ap()
    x4b_d = nc.dram_tensor(pfx + "x4b_d", [D, 3073], BF16, kind="Internal").ap()
    y2f_d = nc.dram_tensor(pfx + "y2f_d", [D, 2560], F32, kind="Internal").ap()
    y2b_d = nc.dram_tensor(pfx + "y2b_d", [D, 2560], BF16, kind="Internal").ap()
    x5f_d = nc.dram_tensor(pfx + "x5f_d", [D, 2048], F32, kind="Internal").ap()
    sb = lambda name, shape, dt: ctx.enter_context(nc.sbuf_tensor(pfx + name, shape, dt))
    psa = lambda name, shape, dt=F32: ctx.enter_context(nc.psum_tensor(pfx + name, shape, dt))

    xsp = [sb(f"xsp{i}", [128, 1025], BF16) for i in range(8)]
    qf = sb("qf", [128, 2048], F32)
    kf = sb("kf", [128, 2048], F32)
    rsc = sb("rsc", [128, 2048], F32)
    tabq = [sb(f"tabq{i}", [128, 512], F32) for i in range(2)]
    tabk = [sb(f"tabk{i}", [128, 512], F32) for i in range(2)]
    swpf = sb("swpf", [128, 128], F32)
    blk2 = sb("blk2", [128, 2], F32)
    exp2 = sb("exp2", [2, 128], F32)
    ss2 = sb("ss2", [2, 512], F32)
    scs = sb("scs", [128, 2], F32)

    mbm = sb("mbm", [128, 640], F32)
    tri0 = sb("tri0_s", [128, 128], F32)
    tri4 = sb("tri4_s", [128, 128], F32)
    killf = sb("killf_s", [128, 240], F32)
    mskrow = sb("mskrow_s", [1, 512], F32)
    scrA = rsc[:, 0:640]
    mcol = sb("mcol", [128, 2], F32)
    mrow = sb("mrowf", [1, 128], BF16)
    pstage = sb("pstagef", [128, 128], F32)
    ptb = [sb(f"ptbf{i}", [128, 128], BF16) for i in range(5)]
    onescol_b = sb("onescol_bf", [128, 1], BF16)

    atout = [qf[:, 512 * i:512 * i + 512] for i in range(4)]
    atoutb = [sb(f"atoutb{i}", [128, 512], BF16) for i in range(4)]
    xwf = [sb(f"xwf{i}", [128, 512], F32) for i in range(2)]
    yst = sb("yst", [128, 512], F32)
    ystb = sb("ystb", [128, 512], BF16)
    onesrow_b = sb("onesrow_bf", [1, 128], BF16)
    onesrow_f = sb("onesrow_ff", [1, 128], F32)
    onescol_f = sb("onescol_ff", [128, 1], F32)
    identb = sb("identbf", [128, 128], BF16)
    identf = sb("identff", [128, 128], F32)
    eps5b = sb("eps5b", [128, 1], F32)
    zT = [sb(f"zT{i}", [128, 1024], BF16) for i in range(8)]
    hTf = [sb(f"hTf{i}", [128, 1024], BF16) for i in range(NHC)]
    w2b = [sb(f"w2bf{i}", [128, D], BF16) for i in range(NHC)]
    wbchunk = [sb(f"wbc{i}", [128, 1024], BF16) for i in range(2)]
    gsb = sb("gsb", [128, 1024], BF16)
    wqb = [zT[i][:, 0:512] for i in range(8)]
    wkb = [zT[i][:, 512:1024] for i in range(8)]
    wvb = [hTf[i][:, 0:512] for i in range(8)]
    wob = [hTf[8 + i][:, :] for i in range(4)]
    qb = [hTf[12 + i // 2][:, 512 * (i % 2):512 * (i % 2) + 512] for i in range(4)]
    kb = [hTf[14 + i][:, :] for i in range(4)]
    vb = [hTf[18 + i // 2][:, 512 * (i % 2):512 * (i % 2) + 512] for i in range(8)]
    QBK = lambda i: ("hTf", 12 + i // 2)
    VBK = lambda i: ("hTf", 18 + i // 2)
    wstr = [qf[:, 0:1024], qf[:, 1024:2048]]
    WSK = [(("qf", 0), ("qf", 1)), (("qf", 2), ("qf", 3))]
    zin = [kf[:, 0:1024], kf[:, 1024:2048]]
    ZIK = [(("kf", 0), ("kf", 1)), (("kf", 2), ("kf", 3))]
    sqd = rsc
    zsq = rsc[:, 0:512]
    b1as = sb("b1as", [128, 22], F32)
    b1gs = sb("b1gs", [128, 22], F32)
    lngs = sb("lngs", [128, 8], F32)
    lnbs = sb("lnbs", [128, 8], F32)
    b2s = sb("b2s", [128, 8], F32)
    statr = sb("statr", [1, 2048], F32)
    statq = sb("statq", [1, 2048], F32)
    strow = sb("strow", [1, 1024], F32)
    srA, srB = statr, statq
    yres = [sb(f"yres{i}", [128, 512], F32) for i in range(2)]
    fin = sb("finf", [128, D], F32)
    finb = sb("finb", [128, 128], F32)
    x3r = [sb("x3rf0", [128, D], F32)] * 2
    acc1 = sb("acc1f", [128, 1], F32)
    acc2 = sb("acc2f", [128, 1], F32)
    ln1 = sb("ln1f", [128, 1], F32)
    ln2 = sb("ln2f", [128, 1], F32)
    ln3 = sb("ln3f", [128, 1], F32)

    P0 = psa("P0", [128, 512])
    P1 = psa("P1", [128, 512])
    PA = psa("PA", [128, 640])
    P5 = psa("P5", [128, 128])
    P7 = psa("P7", [128, 128])
    PR = psa("PR", [128, 512])
    PQ = psa("PQ", [1, 512])

    for o, i_, k in ((mbm[:], mb_in, "mbm"), (tri0[:], tri0_in, "tri0"),
                     (tri4[:], tri4_in, "tri4"), (killf[:], kill_in, "killf"),
                     (mskrow[:], mskrow_in, "mskrow"), (swpf[:], swp_in, "swpf"),
                     (identf[:], ident_in, "identf"), (identb[:], identb_in, "identb")):
        p.dma("sync", lambda e, o=o, i_=i_: e.dma_start(out=o, in_=i_), writes=[k])
    p.v(lambda e: e.memset(onesrow_b[:], 1.0), writes=["onesrow_b"])
    p.v(lambda e: e.memset(onesrow_f[:], 1.0), writes=["onesrow_f"])
    p.v(lambda e: e.memset(onescol_b[:], 1.0), writes=["onescol_b"])
    p.v(lambda e: e.memset(onescol_f[:], 1.0), writes=["onescol_f"])
    p.v(lambda e: e.memset(eps5b[:], 1e-5), writes=["eps5b"])
    p.dma("sync", lambda e: e.dma_start(out=blk2[:], in_=blk2_in), writes=["blk2"])
    p.dma("sync", lambda e: e.dma_start(out=exp2[:], in_=exp2_in), writes=["exp2"])

    for g in range(9):
        w_ = 512 if g < 8 else 1
        for dc in range(8):
            p.dma("sync", lambda e, g=g, dc=dc, w_=w_: e.dma_start(
                out=wstr[dc % 2][:, 0:w_],
                in_=x3extT[128 * dc:128 * dc + 128, 512 * g:512 * g + w_],
                allow_slow_non_contiguous=(w_ == 1)),
                reads=[extk], writes=list(WSK[dc % 2]))
            p.v(lambda e, dc=dc, w_=w_: e.tensor_copy(
                wbchunk[dc % 2][:, 0:w_], wstr[dc % 2][:, 0:w_]),
                reads=list(WSK[dc % 2]), writes=[("wbchunk", dc % 2)])
            p.dma("sync", lambda e, g=g, dc=dc, w_=w_: e.dma_start(
                out=extb_d[128 * dc:128 * dc + 128, 512 * g:512 * g + w_],
                in_=wbchunk[dc % 2][:, 0:w_],
                allow_slow_non_contiguous=(w_ == 1)),
                reads=[("wbchunk", dc % 2)], writes=["extb"])

    def load_bf(dst, dstk, src_ap, n=1024):
        p.dma("sync", lambda e, dst=dst, src_ap=src_ap, n=n: e.dma_start(
            out=wstr[0][:, 0:n], in_=src_ap), writes=list(WSK[0]))
        p.v(lambda e, dst=dst, n=n: e.tensor_copy(dst, wstr[0][:, 0:n]),
            reads=list(WSK[0]), writes=[dstk])

    def attn_layer(l, nwin, lslot0, tslot0, src_f, srcf_k, src_b, srcb_k,
                   src_co0, dst_f, dstf_k, dst_b, dstb_k, masked_slot0):
        for ic in range(8):
            load_bf(wqb[ic], ("zT", ic), wqkv_in[l, 128 * ic:128 * ic + 128, 0:512], 512)
            load_bf(wkb[ic], ("zT", ic), wqkv_in[l, 128 * ic:128 * ic + 128, 512:1024], 512)
            load_bf(wvb[ic], ("hTf", ic), wqkv_in[l, 128 * ic:128 * ic + 128, 1024:1536], 512)
        for ic in range(4):
            load_bf(wob[ic], ("hTf", 8 + ic), wo_in[l, 128 * ic:128 * ic + 128, :], 1024)
        p.dma("sync", lambda e, l=l: e.dma_start(out=scs[:, 0:1], in_=qsc_in[l]),
              writes=["scal"])
        p.dma("sync", lambda e, l=l: e.dma_start(out=scs[:, 1:2], in_=ksc_in[l]),
              writes=["scal"])

        def hsl(dc, base, wdt):
            if dc < 4:
                return xsp[dc][:, base + 1:base + 1 + wdt]
            return xsp[dc][:, base:base + wdt]

        def norm_rope(bufap, bufks, scol, ti0, ti1, out_fn, outk_fn):
            bks = list(bufks)
            p.s(lambda e: e.activation(sqd[:], bufap, AF.Square),
                reads=bks, writes=["rsc"])
            for pt in range(4):
                PS, pk2 = (PR, "PR") if pt % 2 == 0 else (PA[:, 0:512], "PA")
                p.t(lambda e, pt=pt, PS=PS: e.matmul(
                    PS[0:2, 0:512], blk2[:], sqd[:, 512 * pt:512 * pt + 512],
                    start=True, stop=True), reads=["blk2", "rsc"], writes=[pk2])
                p.v(lambda e, PS=PS: e.tensor_copy(ss2[:], PS[0:2, 0:512]),
                    reads=[pk2], writes=["ss2"])
                p.s(lambda e: e.activation(ss2[:], ss2[:], AF.Sqrt),
                    reads=["ss2"], writes=["ss2"])
                p.v(lambda e: e.tensor_scalar_max(ss2[:], ss2[:], 1e-12),
                    reads=["ss2"], writes=["ss2"])
                p.v(lambda e: e.reciprocal(ss2[:], ss2[:]), reads=["ss2"], writes=["ss2"])
                p.t(lambda e, PS=PS: e.matmul(
                    PS[0:128, 0:512], exp2[:], ss2[:], start=True, stop=True),
                    reads=["exp2", "ss2"], writes=[pk2])
                p.v(lambda e, pt=pt, PS=PS, bufap=bufap: e.tensor_mul(
                    bufap[:, 512 * pt:512 * pt + 512],
                    bufap[:, 512 * pt:512 * pt + 512], PS[0:128, 0:512]),
                    reads=bks + [pk2], writes=bks)
            p.v(lambda e, bufap=bufap, scol=scol: e.tensor_scalar_mul(
                bufap, bufap, scs[:, scol:scol + 1]),
                reads=bks + ["scal"], writes=bks)
            for pt in range(4):
                PS, pk2 = (P5, "P5") if pt % 2 == 0 else (P7, "P7")
                for su in range(4):
                    cc0 = 512 * pt + 128 * su
                    p.t(lambda e, cc0=cc0, PS=PS, bufap=bufap: e.matmul(
                        PS[:], swpf[:], bufap[:, cc0:cc0 + 128],
                        start=True, stop=True), reads=["swpf"] + bks, writes=[pk2])
                    p.v(lambda e, cc0=cc0, PS=PS: e.tensor_copy(
                        rsc[:, cc0:cc0 + 128], PS[:]), reads=[pk2], writes=["rsc"])
            for pt in range(4):
                p.v(lambda e, pt=pt, ti1=ti1: e.tensor_mul(
                    rsc[:, 512 * pt:512 * pt + 512],
                    rsc[:, 512 * pt:512 * pt + 512], ti1),
                    reads=["rsc", "tabs"], writes=["rsc"])
                p.v(lambda e, pt=pt, ti0=ti0, bufap=bufap: e.tensor_mul(
                    bufap[:, 512 * pt:512 * pt + 512],
                    bufap[:, 512 * pt:512 * pt + 512], ti0),
                    reads=bks + ["tabs"], writes=bks)
                p.v(lambda e, pt=pt, bufap=bufap: e.tensor_add(
                    bufap[:, 512 * pt:512 * pt + 512],
                    bufap[:, 512 * pt:512 * pt + 512],
                    rsc[:, 512 * pt:512 * pt + 512]),
                    reads=bks + ["rsc"], writes=bks)
            for pt in range(4):
                p.v(lambda e, pt=pt, bufap=bufap: e.tensor_copy(
                    out_fn(pt), bufap[:, 512 * pt:512 * pt + 512]),
                    reads=bks, writes=[outk_fn(pt)])

        for wi in range(-1, nwin):
            co = src_co0 + wi * W
            if wi > -1:
                for pt in range(4):
                    p.v(lambda e, pt=pt: e.tensor_copy(
                        kb[pt][:, 0:512], kb[pt][:, 512:1024]),
                        reads=[("hTf", 14 + pt)], writes=[("hTf", 14 + pt)])
                for i4 in range(4):
                    p.v(lambda e, i4=i4: e.tensor_copy(vb[i4], vb[4 + i4]),
                        reads=[VBK(4 + i4)], writes=[VBK(i4)])
            lo = 511 if wi == -1 else -1
            for dc in range(8):
                p.dma("sync", lambda e, dc=dc, co=co, lo=lo: e.dma_start(
                    out=xsp[dc][:, (512 if lo == 511 else 0):1025],
                    in_=src_b[128 * dc:128 * dc + 128,
                              co + (0 if lo == 511 else -512) - 1:co + 512]),
                    reads=[srcb_k], writes=[("xsp", dc)])
            ts_ = tslot0 + wi
            for r in range(2):
                p.dma("sync", lambda e, ts_=ts_, r=r: e.dma_start(
                    out=tabq[r][:], in_=tabs_in[ts_, r]), writes=["tabs"])
                p.dma("sync", lambda e, ts_=ts_, r=r: e.dma_start(
                    out=tabk[r][:], in_=tabs_in[ts_, 2 + r]), writes=["tabs"])
            if wi >= 0:
                for oc in range(4):
                    PS, pk = (P0, "P0") if oc % 2 == 0 else (P1, "P1")
                    for ic in range(8):
                        p.t(lambda e, oc=oc, ic=ic, PS=PS: e.matmul(
                            PS[:], wqb[ic][:, 128 * oc:128 * oc + 128],
                            hsl(ic, W, W), start=(ic == 0), stop=(ic == 7)),
                            reads=[("zT", ic), ("xsp", ic)], writes=[pk])
                    p.v(lambda e, oc=oc, PS=PS: e.tensor_copy(
                        qf[:, 512 * oc:512 * oc + 512], PS[:]),
                        reads=[pk], writes=[("qf", oc)])
            for oc in range(4):
                PS, pk = (P0, "P0") if oc % 2 == 0 else (P1, "P1")
                for ic in range(8):
                    p.t(lambda e, oc=oc, ic=ic, PS=PS: e.matmul(
                        PS[:], wkb[ic][:, 128 * oc:128 * oc + 128],
                        hsl(ic, W, W), start=(ic == 0), stop=(ic == 7)),
                        reads=[("zT", ic), ("xsp", ic)], writes=[pk])
                p.v(lambda e, oc=oc, PS=PS: e.tensor_copy(
                    kf[:, 512 * oc:512 * oc + 512], PS[:]),
                    reads=[pk], writes=[("kf", oc)])
            for tc4 in range(4):
                PS, pk = (P0, "P0") if tc4 % 2 == 0 else (P1, "P1")
                for ic in range(8):
                    p.t(lambda e, tc4=tc4, ic=ic, PS=PS: e.matmul(
                        PS[:], hsl(ic, W + 128 * tc4, 128), wvb[ic],
                        start=(ic == 0), stop=(ic == 7)),
                        reads=[("hTf", ic), ("xsp", ic)], writes=[pk])
                p.v(lambda e, tc4=tc4, PS=PS: e.tensor_copy(
                    vb[4 + tc4], PS[:]), reads=[pk], writes=[VBK(4 + tc4)])
            norm_rope(kf[:], [("kf", i) for i in range(4)], 1,
                      tabk[0][:], tabk[1][:],
                      lambda pt: kb[pt][:, 512:1024], lambda pt: ("hTf", 14 + pt))
            if wi < 0:
                continue
            norm_rope(qf[:], [("qf", i) for i in range(4)], 0,
                      tabq[0][:], tabq[1][:],
                      lambda pt: qb[pt], lambda pt: QBK(pt))
            pend_at = None
            it_at = 0
            for h in range(8):
                pt, hb = h // 2, 64 * (h % 2)
                for qs in range(4):
                    a0 = 128 * qs
                    mi = it_at % 2
                    it_at += 1
                    p.t(lambda e, pt=pt, hb=hb, a0=a0: e.matmul(
                        PA[:, 0:512], qb[pt][hb:hb + 64, a0:a0 + 128],
                        kb[pt][hb:hb + 64, a0:a0 + 512], start=True, stop=True),
                        reads=[QBK(pt), ("hTf", 14 + pt)], writes=["PA"])
                    p.t(lambda e, pt=pt, hb=hb, a0=a0: e.matmul(
                        PA[:, 512:640], qb[pt][hb:hb + 64, a0:a0 + 128],
                        kb[pt][hb:hb + 64, a0 + 512:a0 + 640], start=True, stop=True),
                        reads=[QBK(pt), ("hTf", 14 + pt)], writes=["PA"])
                    p.v(lambda e: e.tensor_add(scrA, PA[:], mbm[:]),
                        reads=["PA", "mbm"], writes=["rsc"])
                    p.v(lambda e, mi=mi: e.reduce_max(
                        mcol[:, mi:mi + 1], scrA, axis=AX, negate=True),
                        reads=["rsc"], writes=[("mcol", mi)])
                    if pend_at is not None:
                        pend_at()

                    def mk_at(h=h, pt=pt, hb=hb, a0=a0, qs=qs, mi=mi, wi=wi):
                        def go():
                            p.t(lambda e: e.transpose(
                                P5[0:1, 0:128], mcol[:, mi:mi + 1], identf[:]),
                                reads=[("mcol", mi), "identf"], writes=["P5"])
                            p.v(lambda e: e.tensor_copy(mrow[:], P5[0:1, 0:128]),
                                reads=["P5"], writes=["mrow"])
                            for kcr in range(5):
                                PS, pk = (P5, "P5") if kcr % 2 == 0 else (P7, "P7")
                                p.t(lambda e, kcr=kcr, PS=PS: e.matmul(
                                    PS[:],
                                    kb[pt][hb:hb + 64,
                                           a0 + 128 * kcr:a0 + 128 * kcr + 128],
                                    qb[pt][hb:hb + 64, a0:a0 + 128],
                                    start=True, stop=False),
                                    reads=[("hTf", 14 + pt), QBK(pt)], writes=[pk])
                                p.t(lambda e, PS=PS: e.matmul(
                                    PS[:], onesrow_b[:], mrow[:],
                                    start=False, stop=True),
                                    reads=["onesrow_b", "mrow"], writes=[pk])
                                kcol = ((lslot0 + wi) * 4 + qs) * 5 + kcr
                                if kcr == 0 or kcr == 4:
                                    tri = tri0 if kcr == 0 else tri4
                                    trik = "tri0" if kcr == 0 else "tri4"
                                    p.v(lambda e, tri=tri, PS=PS: e.tensor_add(
                                        pstage[:], PS[:], tri[:]),
                                        reads=[pk, trik], writes=["pstage"])
                                    src, srck = pstage, "pstage"
                                else:
                                    src, srck = PS, pk
                                p.s(lambda e, kcr=kcr, kcol=kcol, src=src:
                                    e.activation(
                                    ptb[kcr][:], src[:], AF.Exp,
                                    bias=killf[:, kcol:kcol + 1]),
                                    reads=[srck, "killf"], writes=[("ptb", kcr)])
                            for kcr in range(5):
                                vt = qs + kcr
                                p.t(lambda e, vt=vt, kcr=kcr: e.matmul(
                                    P0[0:64, 0:128], vb[vt][:, 64 * h:64 * h + 64],
                                    ptb[kcr][:], start=(kcr == 0), stop=(kcr == 4)),
                                    reads=[VBK(vt), ("ptb", kcr)], writes=["P0"])
                            for kcr in range(5):
                                p.t(lambda e, kcr=kcr: e.matmul(
                                    P1[0:1, 0:128], onescol_b[:], ptb[kcr][:],
                                    start=(kcr == 0), stop=(kcr == 4)),
                                    reads=["onescol_b", ("ptb", kcr)], writes=["P1"])
                            p.v(lambda e: e.tensor_copy(
                                atout[pt][hb:hb + 64, a0:a0 + 128],
                                P0[0:64, 0:128]),
                                reads=["P0"], writes=[("qf", pt)])
                            srX, srk = (srA, "statr") if h % 2 == 0 else (srB, "statq")
                            p.v(lambda e, srX=srX: e.tensor_copy(
                                srX[0:1, 512 * pt + a0:512 * pt + a0 + 128],
                                P1[0:1, 0:128]),
                                reads=["P1"], writes=[srk])
                        return go
                    pend_at = mk_at()
            pend_at()
            p.v(lambda e: e.reciprocal(srA[:], srA[:]), reads=["statr"], writes=["statr"])
            p.v(lambda e: e.reciprocal(srB[:], srB[:]), reads=["statq"], writes=["statq"])
            for h in range(8):
                pt, hb = h // 2, 64 * (h % 2)
                srX, srk = (srA, "statr") if h % 2 == 0 else (srB, "statq")
                PS, pk2 = (PR, "PR") if h % 2 == 0 else (PA[:, 0:512], "PA")
                p.t(lambda e, pt=pt, srX=srX, PS=PS: e.matmul(
                    PS[0:64, 0:512], onesrow_f[0:1, 0:64],
                    srX[0:1, 512 * pt:512 * pt + 512],
                    start=True, stop=True), reads=["onesrow_f", srk], writes=[pk2])
                p.v(lambda e, pt=pt, hb=hb, PS=PS: e.tensor_mul(
                    atout[pt][hb:hb + 64, :], atout[pt][hb:hb + 64, :],
                    PS[0:64, 0:512]),
                    reads=[("qf", pt), pk2], writes=[("qf", pt)])
            for pt in range(4):
                p.v(lambda e, pt=pt: e.tensor_copy(atoutb[pt][:], atout[pt]),
                    reads=[("qf", pt)], writes=[("atoutb", pt)])
            if masked_slot0 and wi == 0:
                p.t(lambda e: e.matmul(PR[:], onesrow_f[:], mskrow[:],
                                       start=True, stop=True),
                    reads=["onesrow_f", "mskrow"], writes=["PR"])
            for dc in range(8):
                PS, pk = (P0, "P0") if dc % 2 == 0 else (P1, "P1")
                for ic in range(4):
                    p.t(lambda e, dc=dc, ic=ic, PS=PS: e.matmul(
                        PS[:], wob[ic][:, 128 * dc:128 * dc + 128], atoutb[ic][:],
                        start=(ic == 0), stop=(ic == 3)),
                        reads=[("wob", ic), ("atoutb", ic)], writes=[pk])
                p.dma("sync", lambda e, dc=dc, co=co: e.dma_start(
                    out=xwf[dc % 2][:], in_=src_f[128 * dc:128 * dc + 128, co:co + 512]),
                    reads=[srcf_k], writes=[("xwf", dc % 2)])
                p.v(lambda e, dc=dc, PS=PS: e.scalar_tensor_tensor(
                    yst[:], PS[:], 1.0, xwf[dc % 2][:], op0=OP.mult, op1=OP.add),
                    reads=[pk, ("xwf", dc % 2)], writes=["yst"])
                if masked_slot0 and wi == 0:
                    p.v(lambda e: e.tensor_mul(yst[:], yst[:], PR[:]),
                        reads=["yst", "PR"], writes=["yst"])
                p.dma("sync", lambda e, dc=dc, wi=wi: e.dma_start(
                    out=dst_f[128 * dc:128 * dc + 128, W * wi:W * wi + W],
                    in_=yst[:]), reads=["yst"], writes=[dstf_k])
                p.v(lambda e: e.tensor_copy(ystb[:], yst[:]),
                    reads=["yst"], writes=["ystb"])
                p.dma("sync", lambda e, dc=dc, wi=wi: e.dma_start(
                    out=dst_b[128 * dc:128 * dc + 128, W * wi:W * wi + W],
                    in_=ystb[:]), reads=["ystb"], writes=[dstb_k])

    def ffn_phase(l, src_f, srck_f, src_off, n_tokens, dst_f, dstk_f,
                  dst_b, dstk_b):
        p.dma("sync", lambda e, l=l: e.dma_start(out=b1as[:], in_=b1a_in[l]),
              writes=["b1as"])
        p.dma("sync", lambda e, l=l: e.dma_start(out=b1gs[:], in_=b1g_in[l]),
              writes=["b1gs"])
        p.dma("sync", lambda e, l=l: e.dma_start(out=lngs[:], in_=lng_in[l]),
              writes=["lngs"])
        p.dma("sync", lambda e, l=l: e.dma_start(out=lnbs[:], in_=lnb_in[l]),
              writes=["lnbs"])
        p.dma("sync", lambda e, l=l: e.dma_start(out=b2s[:], in_=b2_in[l]),
              writes=["b2s"])
        for hc in range(NHC):
            hw = 128 if hc < 21 else 42
            p.dma("sync", lambda e, l=l, hc=hc, hw=hw: e.dma_start(
                out=wstr[0][0:hw, :], in_=w2_in[l, 128 * hc:128 * hc + hw, :]),
                writes=list(WSK[0]))
            p.v(lambda e, hc=hc, hw=hw: e.tensor_copy(
                w2b[hc][0:hw, :], wstr[0][0:hw, :]),
                reads=list(WSK[0]), writes=[("w2b", hc)])
        slabs = []
        off = 0
        while off < n_tokens:
            ns = min(1024, n_tokens - off)
            slabs.append((off, ns))
            off += ns
        for off0, ns in slabs:
            cgs = []
            o2 = 0
            while o2 < ns:
                cw = min(512, ns - o2)
                cgs.append((o2, cw))
                o2 += cw
            for cg, cw in cgs:
                for dc in range(8):
                    sh = 1 if dc >= 4 else 0
                    p.dma("sync", lambda e, dc=dc, cg=cg, cw=cw, sh=sh, off0=off0: e.dma_start(
                        out=zin[dc % 2][:, 0:cw],
                        in_=src_f[128 * dc:128 * dc + 128,
                                  src_off + off0 + cg - sh:src_off + off0 + cg - sh + cw],
                        allow_slow_non_contiguous=(cw == 1)),
                        reads=[srck_f], writes=list(ZIK[dc % 2]))
                    p.s(lambda e, dc=dc, cw=cw: e.activation(
                        zsq[:, 0:cw], zin[dc % 2][:, 0:cw], AF.Square),
                        reads=list(ZIK[dc % 2]), writes=["rsc"])
                    p.t(lambda e, dc=dc, cw=cw: e.matmul(
                        PQ[0:1, 0:cw], onescol_f[:], zin[dc % 2][:, 0:cw],
                        start=(dc == 0), stop=(dc == 7)),
                        reads=list(ZIK[dc % 2]) + ["onescol_f"], writes=["PQ"])
                    p.t(lambda e, dc=dc, cw=cw: e.matmul(
                        PR[0:1, 0:cw], onescol_f[:], zsq[:, 0:cw],
                        start=(dc == 0), stop=(dc == 7)),
                        reads=["rsc", "onescol_f"], writes=["PR"])
                p.v(lambda e, cg=cg, cw=cw: e.tensor_copy(
                    statr[0:1, cg:cg + cw], PQ[0:1, 0:cw]),
                    reads=["PQ"], writes=["statr"])
                p.v(lambda e, cg=cg, cw=cw: e.tensor_copy(
                    statq[0:1, cg:cg + cw], PR[0:1, 0:cw]),
                    reads=["PR"], writes=["statq"])
            p.v(lambda e, ns=ns: e.tensor_scalar_mul(statr[0:1, 0:ns],
                                                     statr[0:1, 0:ns], 1.0 / D),
                reads=["statr"], writes=["statr"])
            p.v(lambda e, ns=ns: e.tensor_scalar_mul(statq[0:1, 0:ns],
                                                     statq[0:1, 0:ns], 1.0 / D),
                reads=["statq"], writes=["statq"])
            p.v(lambda e, ns=ns: e.tensor_mul(strow[0:1, 0:ns], statr[0:1, 0:ns],
                                              statr[0:1, 0:ns]),
                reads=["statr"], writes=["strow"])
            p.v(lambda e, ns=ns: e.tensor_sub(statq[0:1, 0:ns], statq[0:1, 0:ns],
                                              strow[0:1, 0:ns]),
                reads=["statq", "strow"], writes=["statq"])
            p.s(lambda e, ns=ns: e.activation(statq[0:1, 0:ns], statq[0:1, 0:ns],
                                              AF.Sqrt, bias=eps5b[0:1, :]),
                reads=["statq", "eps5b"], writes=["statq"])
            p.v(lambda e, ns=ns: e.reciprocal(statq[0:1, 0:ns], statq[0:1, 0:ns]),
                reads=["statq"], writes=["statq"])
            p.v(lambda e, ns=ns: e.tensor_mul(strow[0:1, 0:ns], statr[0:1, 0:ns],
                                              statq[0:1, 0:ns]),
                reads=["statr", "statq"], writes=["strow"])
            p.v(lambda e, ns=ns: e.tensor_scalar_mul(strow[0:1, 0:ns],
                                                     strow[0:1, 0:ns], -1.0),
                reads=["strow"], writes=["strow"])
            for cg, cw in cgs:
                p.t(lambda e, cg=cg, cw=cw: e.matmul(
                    PR[:, 0:cw], onesrow_f[:], statq[0:1, cg:cg + cw],
                    start=True, stop=True), reads=["onesrow_f", "statq"], writes=["PR"])
                p.t(lambda e, cg=cg, cw=cw: e.matmul(
                    PA[:, 0:cw], onesrow_f[:], strow[0:1, cg:cg + cw],
                    start=True, stop=True), reads=["onesrow_f", "strow"], writes=["PA"])
                for dc in range(8):
                    sh = 1 if dc >= 4 else 0
                    p.dma("sync", lambda e, dc=dc, cg=cg, cw=cw, sh=sh, off0=off0: e.dma_start(
                        out=zin[dc % 2][:, 0:cw],
                        in_=src_f[128 * dc:128 * dc + 128,
                                  src_off + off0 + cg - sh:src_off + off0 + cg - sh + cw],
                        allow_slow_non_contiguous=(cw == 1)),
                        reads=[srck_f], writes=list(ZIK[dc % 2]))
                    p.v(lambda e, dc=dc, cw=cw: e.tensor_mul(
                        zin[dc % 2][:, 0:cw], zin[dc % 2][:, 0:cw], PR[:, 0:cw]),
                        reads=list(ZIK[dc % 2]) + ["PR"], writes=list(ZIK[dc % 2]))
                    p.v(lambda e, dc=dc, cw=cw: e.tensor_add(
                        zin[dc % 2][:, 0:cw], zin[dc % 2][:, 0:cw], PA[:, 0:cw]),
                        reads=list(ZIK[dc % 2]) + ["PA"], writes=list(ZIK[dc % 2]))
                    p.v(lambda e, dc=dc, cg=cg, cw=cw: e.tensor_scalar(
                        zT[dc][:, cg:cg + cw], zin[dc % 2][:, 0:cw],
                        lngs[:, dc:dc + 1], lnbs[:, dc:dc + 1],
                        op0=OP.mult, op1=OP.add),
                        reads=list(ZIK[dc % 2]) + ["lngs", "lnbs"], writes=[("zT", dc)])
            for hc in range(NHC):
                hw = 128 if hc < 21 else 42
                p.dma("sync", lambda e, l=l, hc=hc, hw=hw: e.dma_start(
                    out=wstr[0][:, 0:8 * hw].rearrange("p (a c) -> p a c", a=8),
                    in_=w1g_in[l][:, 128 * hc:128 * hc + hw].rearrange(
                        "(a p) c -> p a c", p=128)),
                    writes=list(WSK[0]))
                p.v(lambda e, hw=hw: e.tensor_copy(
                    wbchunk[0][:, 0:8 * hw], wstr[0][:, 0:8 * hw]),
                    reads=list(WSK[0]), writes=[("wbchunk", 0)])
                p.dma("sync", lambda e, l=l, hc=hc, hw=hw: e.dma_start(
                    out=wstr[1][:, 0:8 * hw].rearrange("p (a c) -> p a c", a=8),
                    in_=w1a_in[l][:, 128 * hc:128 * hc + hw].rearrange(
                        "(a p) c -> p a c", p=128)),
                    writes=list(WSK[1]))
                p.v(lambda e, hw=hw: e.tensor_copy(
                    wbchunk[1][:, 0:8 * hw], wstr[1][:, 0:8 * hw]),
                    reads=list(WSK[1]), writes=[("wbchunk", 1)])
                for cg, cw in cgs:
                    for ic in range(8):
                        p.t(lambda e, hc=hc, hw=hw, ic=ic, cg=cg, cw=cw: e.matmul(
                            P0[0:hw, 0:cw], wbchunk[0][:, hw * ic:hw * ic + hw],
                            zT[ic][:, cg:cg + cw], start=(ic == 0), stop=(ic == 7)),
                            reads=[("wbchunk", 0), ("zT", ic)], writes=["P0"])
                    p.s(lambda e, hc=hc, hw=hw, cg=cg, cw=cw: e.activation(
                        gsb[0:hw, cg:cg + cw], P0[0:hw, 0:cw], AF.Gelu,
                        bias=b1gs[0:hw, hc:hc + 1]),
                        reads=["P0", "b1gs"], writes=["gsb"])
                    for ic in range(8):
                        p.t(lambda e, hc=hc, hw=hw, ic=ic, cg=cg, cw=cw: e.matmul(
                            P1[0:hw, 0:cw], wbchunk[1][:, hw * ic:hw * ic + hw],
                            zT[ic][:, cg:cg + cw], start=(ic == 0), stop=(ic == 7)),
                            reads=[("wbchunk", 1), ("zT", ic)], writes=["P1"])
                    p.v(lambda e, hc=hc, hw=hw, cg=cg, cw=cw: e.scalar_tensor_tensor(
                        hTf[hc][0:hw, cg:cg + cw], P1[0:hw, 0:cw],
                        b1as[0:hw, hc:hc + 1], gsb[0:hw, cg:cg + cw],
                        op0=OP.add, op1=OP.mult),
                        reads=["P1", "b1as", "gsb"], writes=[("hTf", hc)])
            for dc in range(8):
                for cg, cw in cgs:
                    for hc in range(NHC):
                        hw = 128 if hc < 21 else 42
                        p.t(lambda e, dc=dc, hc=hc, hw=hw, cg=cg, cw=cw: e.matmul(
                            PA[:, 0:cw], w2b[hc][0:hw, 128 * dc:128 * dc + 128],
                            hTf[hc][0:hw, cg:cg + cw],
                            start=(hc == 0), stop=(hc == NHC - 1)),
                            reads=[("w2b", hc), ("hTf", hc)], writes=["PA"])
                    p.dma("sync", lambda e, dc=dc, cg=cg, cw=cw, off0=off0: e.dma_start(
                        out=yres[dc % 2][:, 0:cw],
                        in_=src_f[128 * dc:128 * dc + 128,
                                  src_off + off0 + cg:src_off + off0 + cg + cw],
                        allow_slow_non_contiguous=(cw == 1)),
                        reads=[srck_f], writes=[("yres", dc % 2)])
                    p.v(lambda e, dc=dc, cg=cg, cw=cw: e.scalar_tensor_tensor(
                        yst[:, 0:cw], PA[:, 0:cw], b2s[:, dc:dc + 1],
                        yres[dc % 2][:, 0:cw], op0=OP.add, op1=OP.add),
                        reads=["PA", "b2s", ("yres", dc % 2)], writes=["yst"])
                    p.dma("sync", lambda e, dc=dc, cg=cg, cw=cw, off0=off0: e.dma_start(
                        out=dst_f[128 * dc:128 * dc + 128,
                                  off0 + cg:off0 + cg + cw], in_=yst[:, 0:cw],
                        allow_slow_non_contiguous=(cw == 1)),
                        reads=["yst"], writes=[dstk_f])
                    if dst_b is not None:
                        p.v(lambda e, cw=cw: e.tensor_copy(ystb[:, 0:cw],
                                                           yst[:, 0:cw]),
                            reads=["yst"], writes=["ystb"])
                        p.dma("sync", lambda e, dc=dc, cg=cg, cw=cw, off0=off0: e.dma_start(
                            out=dst_b[128 * dc:128 * dc + 128,
                                      off0 + cg:off0 + cg + cw], in_=ystb[:, 0:cw],
                            allow_slow_non_contiguous=(cw == 1)),
                            reads=["ystb"], writes=[dstk_b])

    if phases >= 2:
        attn_layer(0, 7, 0, 1, x3extT, extk, extb_d, "extb", 513,
                   y1f_d, ("y1f",), y1b_d, ("y1b",), False)
    if phases >= 3:
        ffn_phase(0, y1f_d, ("y1f",), 511, 3073, x4f_d, ("x4f",), x4b_d, ("x4b",))
    if phases >= 4:
        attn_layer(1, 5, 7, 3, x4f_d, ("x4f",), x4b_d, ("x4b",), 513,
                   y2f_d, ("y2f",), y2b_d, ("y2b",), True)
    if phases >= 5:
        ffn_phase(1, y2f_d, ("y2f",), 512, 2048, x5f_d, ("x5f",), None, None)
    if phases >= 6:
        for tc in range(16):
            for dc in range(8):
                PT, ptk = (P5, "P5") if dc % 2 == 0 else (P7, "P7")
                p.dma("sync", lambda e, tc=tc, dc=dc: e.dma_start(
                    out=finb[:], in_=x5f_d[128 * dc:128 * dc + 128,
                                           128 * tc:128 * tc + 128]),
                    reads=[("x5f",)], writes=["finb"])
                p.t(lambda e, PT=PT: e.transpose(PT[:], finb[:], identf[:]),
                    reads=["finb", "identf"], writes=[ptk])
                p.v(lambda e, dc=dc, PT=PT: e.tensor_copy(
                    fin[:, 128 * dc:128 * dc + 128], PT[:]),
                    reads=[ptk], writes=["fin"])
            p.row_dma("sync", x3r[tc % 2][:], x3own_row(tc),
                      reads=["x3src"], writes=[("x3r", 0)])
            p.v(lambda e, tc=tc: e.scalar_tensor_tensor(
                fin[:], fin[:], 1.0, x3r[tc % 2][:], op0=OP.mult, op1=OP.add,
                accum_out=acc1[:]),
                reads=["fin", ("x3r", 0)], writes=["fin", "acc1"])
            p.s(lambda e: e.activation(rsc[:, 0:1024], fin[:], AF.Square,
                                       accum_out=acc2[:]),
                reads=["fin"], writes=["rsc", "acc2"])
            p.v(lambda e: e.tensor_scalar_mul(ln1[:], acc1[:], 1.0 / D),
                reads=["acc1"], writes=["ln1"])
            p.v(lambda e: e.tensor_scalar_mul(ln2[:], acc2[:], 1.0 / D),
                reads=["acc2"], writes=["ln2"])
            p.v(lambda e: e.tensor_mul(ln3[:], ln1[:], ln1[:]),
                reads=["ln1"], writes=["ln3"])
            p.v(lambda e: e.tensor_sub(ln2[:], ln2[:], ln3[:]),
                reads=["ln2", "ln3"], writes=["ln2"])
            p.s(lambda e: e.activation(ln3[:], ln2[:], AF.Sqrt, bias=eps5b[:]),
                reads=["ln2", "eps5b"], writes=["ln3"])
            p.v(lambda e: e.reciprocal(ln2[:], ln3[:]), reads=["ln3"], writes=["ln2"])
            p.v(lambda e: e.tensor_scalar_mul(ln3[:], ln1[:], -1.0),
                reads=["ln1"], writes=["ln3"])
            p.v(lambda e, tc=tc: e.tensor_scalar(
                x3r[tc % 2][:], fin[:], ln3[:], ln2[:], op0=OP.add, op1=OP.mult),
                reads=["fin", "ln3", "ln2"], writes=[("x3r", 0)])
            p.dma("sync", lambda e, tc=tc: e.dma_start(
                out=outown[128 * tc:128 * tc + 128, :], in_=x3r[tc % 2][:]),
                reads=[("x3r", 0)], writes=[("out", tc)])





def build_l1f(phases=99):
    nc = bass.Bass("TRN2", target_bir_lowering=False, dynamic_dma_scratch_size=2048)
    p = Prog(nc)
    HID = 2730
    x3extT = nc.dram_tensor("x3extT", [D, 4097], F32, kind="ExternalInput").ap()
    x3own = nc.dram_tensor("x3own_in", [HALF, D], F32, kind="ExternalInput").ap()
    wqkv_in = nc.dram_tensor("wqkv", [2, D, 1536], F32, kind="ExternalInput").ap()
    qsc_in = nc.dram_tensor("qsc", [2, 128, 1], F32, kind="ExternalInput").ap()
    ksc_in = nc.dram_tensor("ksc", [2, 128, 1], F32, kind="ExternalInput").ap()
    wo_in = nc.dram_tensor("wo", [2, 512, D], F32, kind="ExternalInput").ap()
    lng_in = nc.dram_tensor("lng", [2, 128, 8], F32, kind="ExternalInput").ap()
    lnb_in = nc.dram_tensor("lnb", [2, 128, 8], F32, kind="ExternalInput").ap()
    w1a_in = nc.dram_tensor("w1a", [2, D, HID], F32, kind="ExternalInput").ap()
    w1g_in = nc.dram_tensor("w1g", [2, D, HID], F32, kind="ExternalInput").ap()
    b1a_in = nc.dram_tensor("b1a", [2, 128, 22], F32, kind="ExternalInput").ap()
    b1g_in = nc.dram_tensor("b1g", [2, 128, 22], F32, kind="ExternalInput").ap()
    w2_in = nc.dram_tensor("w2f", [2, HID, D], F32, kind="ExternalInput").ap()
    b2_in = nc.dram_tensor("b2f", [2, 128, 8], F32, kind="ExternalInput").ap()
    mb_in = nc.dram_tensor("mb", [128, 640], F32, kind="ExternalInput").ap()
    tri0_in = nc.dram_tensor("tri0", [128, 128], F32, kind="ExternalInput").ap()
    tri4_in = nc.dram_tensor("tri4", [128, 128], F32, kind="ExternalInput").ap()
    kill_in = nc.dram_tensor("killf", [128, 240], F32, kind="ExternalInput").ap()
    mskrow_in = nc.dram_tensor("mskrow", [1, 512], F32, kind="ExternalInput").ap()
    tabs_in = nc.dram_tensor("ropetabs", [8, 4, 128, 512], F32, kind="ExternalInput").ap()
    swp_in = nc.dram_tensor("swp", [128, 128], F32, kind="ExternalInput").ap()
    ident_in = nc.dram_tensor("ident", [128, 128], F32, kind="ExternalInput").ap()
    identb_in = nc.dram_tensor("identb_in", [128, 128], BF16, kind="ExternalInput").ap()
    blk2_in = nc.dram_tensor("blk2c", [128, 2], F32, kind="ExternalInput").ap()
    exp2_in = nc.dram_tensor("exp2c", [2, 128], F32, kind="ExternalInput").ap()
    outown = nc.dram_tensor("outown", [HALF, D], F32, kind="ExternalOutput").ap()



    ctx = ExitStack()
    x3own_row = lambda tc: x3own[128 * tc:128 * tc + 128, :]
    emit_l1f(nc, p, ctx, "", x3extT, x3own_row, wqkv_in, qsc_in, ksc_in, wo_in, lng_in, lnb_in, w1a_in, w1g_in, b1a_in, b1g_in, w2_in, b2_in, mb_in, tri0_in, tri4_in, kill_in, mskrow_in, tabs_in, swp_in, ident_in, identb_in, blk2_in, exp2_in, outown, phases=phases)
    p.emit()
    ctx.close()
    return nc

def run_l1f(x3_full, wqkv, q_scale, k_scale, wo, ln_g, ln_b, w1, b1, w2, b2):
    nc = build_l1f()
    w1a = np.ascontiguousarray(w1[:, :, :2730])
    w1g = np.ascontiguousarray(w1[:, :, 2730:])
    def colpack(v):
        n = v.shape[1]
        nb = (n + 127) // 128
        a = np.zeros((2, 128 * nb), np.float32)
        a[:, :n] = v
        return np.ascontiguousarray(a.reshape(2, nb, 128).transpose(0, 2, 1))
    in_maps = []
    for c in range(8):
        b, par = c // 2, c % 2
        cc = _l1f_consts(par)
        in_maps.append({
            "x3extT": _l1f_x3ext(x3_full[b], par),
            "x3own_in": np.ascontiguousarray(x3_full[b, par * HALF:(par + 1) * HALF]),
            "wqkv": wqkv,
            "qsc": np.ascontiguousarray(np.tile(q_scale, (1, 2))[:, :, None]),
            "ksc": np.ascontiguousarray(np.tile(k_scale, (1, 2))[:, :, None]),
            "wo": wo, "lng": colpack(ln_g), "lnb": colpack(ln_b),
            "w1a": w1a, "w1g": w1g,
            "b1a": colpack(b1[:, :2730]), "b1g": colpack(b1[:, 2730:]),
            "w2f": w2, "b2f": colpack(b2),
            "mb": cc["mb"], "tri0": cc["tri0"], "tri4": cc["tri4"],
            "killf": cc["kill"], "mskrow": cc["mskrow"],
            "ropetabs": cc["ropetabs"], "swp": cc["swp"],
            "ident": np.eye(128, dtype=np.float32),
            "identb_in": np.eye(128).astype(BFD),
            "blk2c": _l1f_blk2(), "exp2c": _l1f_exp2(),
        })
    res = run_bass_kernel_spmd(nc, in_maps, core_ids=list(range(8)))
    out = np.zeros((B, T, D), np.float32)
    for c in range(8):
        b, par = c // 2, c % 2
        out[b, par * HALF:(par + 1) * HALF] = res.results[c]["outown"]
    return out



def _l0_consts():
    c = {}
    pp = np.arange(128)[:, None]
    jj = np.arange(512)[None, :]
    tri = np.zeros((4, 128, 512), np.float32)
    for v in range(4):
        tri[v] = np.where(jj > pp + 128 * v, NEG, 0.0)
    c["tri_a"] = tri
    c["tri_b"] = np.where(pp > jj[:, :128], NEG, 0.0).astype(np.float32)
    f = 1.0 / 10000.0 ** (np.arange(0, 64, 2, dtype=np.float64) / 64)
    t = np.arange(T, dtype=np.float64)
    a = np.outer(f, t)
    c["cos32"] = np.cos(a).astype(np.float32)
    c["sin32"] = np.sin(a).astype(np.float32)
    P = np.zeros((128, 128), np.float32)
    for b_ in range(2):
        for j in range(32):
            P[64 * b_ + 32 + j, 64 * b_ + j] = 1.0
            P[64 * b_ + j, 64 * b_ + 32 + j] = 1.0
    c["swp"] = P
    c["slt"] = (np.arange(128)[:, None] < np.arange(128)[None, :]).astype(np.float32)
    c["slt32"] = (np.arange(32)[:, None] < np.arange(32)[None, :]).astype(np.float32)
    c["slotrow"] = np.arange(256, dtype=np.float32)[None, :]
    return c


def emit_l0(nc, p, ctx, pfx, x0, w_cq, g_qn, w_dqn, w_dqr, w_ckv, g_kn, w_dkn, w_dv, w_kr, w_imp, w_selk, w_selv, w_wink, w_winv, w_gate, w_proj, tri_a_in, tri_b_in, cos_in, sin_in, swp_in, slt_in, slt32_in, slot_in, ident_in, identb_in, partial, phases=99):
    NH, VD, NOPE, RD, KEEP = 8, 32, 32, 64, 256
    SC = 1.0 / np.sqrt(96.0)
    xtm_d = nc.dram_tensor(pfx + "xtm_d", [T, D], BF16, kind="Internal").ap()
    sb = lambda name, shape, dt: ctx.enter_context(nc.sbuf_tensor(pfx + name, shape, dt))
    psa = lambda name, shape, dt=F32: ctx.enter_context(nc.psum_tensor(pfx + name, shape, dt))

    xT = [sb(f"xT{i}", [128, T], BF16) for i in range(8)]
    accb = [sb(f"accb{i}", [128, T], BF16) for i in range(2)]
    kropeT = sb("kropeT", [64, T], BF16)
    nqT = sb("nqT", [96, T], BF16)
    ckvT = sb("ckvT", [32, T], BF16)
    ctab = [sb("ctab0", [32, 512], F32)] * 2
    stab = [sb("stab0", [32, 512], F32)] * 2
    rp1 = sb("rp1", [32, 512], F32)
    rp2 = sb("rp2", [32, 512], F32)
    scr5B = sb("scr5B", [32, 512], F32)
    v1b = [sb(f"v1b{i}", [128, 264], BF16) for i in range(32)]
    vwb = [sb(f"vwb{i}", [128, 264], BF16) for i in range(32)]
    vsb = [sb(f"vsb{i}", [128, 264], BF16) for i in range(2)]
    k1t = sb("k1t", [97, T], BF16)
    kwt = sb("kwt", [97, T], BF16)
    kst = sb("kst", [97, 256], BF16)
    qt_ = sb("qt_", [97, T], BF16)
    tm = [accb[0][:, 0:D], accb[0][:, D:2 * D]]
    tmf = [sb(f"tmf{i}", [128, D], F32) for i in range(2)]
    tri_a = sb("tri_a_s", [128, 2048], BF16)
    tri_b = sb("tri_b_s", [128, 128], BF16)
    swpf = sb("swpf", [128, 128], F32)
    sltf = sb("sltf", [128, 128], F32)
    slt32f = sb("slt32f", [32, 32], F32)
    slotb = sb("slotb", [128, 256], F32)
    identb = sb("identbl", [128, 128], BF16)
    identf = sb("identfl", [128, 128], F32)
    onesrow_f = sb("onesrow_fl", [1, 128], F32)
    onescolf = sb("onescolf", [128, 1], F32)
    scol = sb("scol", [128, 32], F32)
    prefS = sb("prefS", [128, 32], F32)
    indS = sb("indS", [128, 32], F32)
    indT = sb("indT", [128, 32], F32)
    gch = [sb(f"gch{i}", [128, 256], BF16) for i in range(2)]
    selT = [sb(f"selT{i}", [128, 256], BF16) for i in range(8)]
    r32a = sb("r32a", [1, 32], F32)
    r32b = sb("r32b", [1, 32], F32)
    c11 = [sb(f"c11_{i}", [1, 1], F32) for i in range(6)]
    gsc = sb("gsc", [128, 4], F32)
    nqf = sb("nqf", [96, 512], F32)
    rmsr = sb("rmsr", [1, 512], F32)
    scr5 = sb("scr5", [128, 512], F32)
    scr5b = sb("scr5b", [128, 512], BF16)
    mcols = sb("mcols", [128, 16], F32)
    mraw = sb("mraw", [128, 1], F32)
    mrawq = sb("mrawq", [128, 4], F32)
    mrow = sb("mrowl", [1, 512], BF16)
    ptb = [sb(f"ptbl{i}", [128, 512], BF16) for i in range(2)]
    avsb = sb("avsb", [32, 512], F32)
    comb = sb("comb", [32, 512], F32)
    prow = sb("prow", [1, 512], F32)
    gq = sb("gq", [96, 1], F32)
    gk = sb("gk", [32, 1], F32)
    wst = sb("wst", [128, 768], BF16)
    outt = None

    outt = tmf[1]
    P0 = psa("P0", [128, 512])
    P1 = psa("P1", [128, 512])
    P2 = psa("P2", [128, 512])
    P3 = psa("P3", [128, 512])
    PAV = psa("PAV", [33, 512])
    P4c = psa("P4c", [128, 128], BF16)
    P4d = psa("P4d", [128, 128], BF16)
    P5 = psa("P5", [128, 128])

    for o, i_, k in ((swpf[:], swp_in, "swpf"), (sltf[:], slt_in, "sltf"),
                     (slt32f[:], slt32_in, "slt32f"),
                     (identf[:], ident_in, "identf"), (identb[:], identb_in, "identb"),
                     (tri_b[:], tri_b_in, "tri_b"),
                     (gq[:], g_qn, "gq"), (gk[:], g_kn, "gk")):
        p.dma("sync", lambda e, o=o, i_=i_: e.dma_start(out=o, in_=i_), writes=[k])
    for v in range(4):
        p.dma("sync", lambda e, v=v: e.dma_start(
            out=tri_a[:, 512 * v:512 * v + 512], in_=tri_a_in[v]), writes=["tri_a"])
    p.v(lambda e: e.memset(onesrow_f[:], 1.0), writes=["onesrow_f"])
    p.v(lambda e: e.memset(onescolf[:], 1.0), writes=["onescolf"])
    p.v(lambda e: e.memset(k1t[96:97, :], 1.0), writes=["k1t_ones"])
    p.v(lambda e: e.memset(kwt[96:97, :], 1.0), writes=["kwt_ones"])
    p.v(lambda e: e.memset(kst[96:97, :], 1.0), writes=["kst_ones"])
    p.dma("sync", lambda e: e.dma_start(out=slotb[0:1, :], in_=slot_in), writes=["slotb"])
    p.t(lambda e: e.matmul(P0[:, 0:256], onesrow_f[:], slotb[0:1, 0:256],
                           start=True, stop=True),
        reads=["onesrow_f", "slotb"], writes=["P0"])
    p.v(lambda e: e.tensor_copy(slotb[:], P0[:, 0:256]), reads=["P0"], writes=["slotb"])

    wimpb = sb("wimpb", [128, D], F32)
    p.dma("sync", lambda e: e.dma_start(
        out=wimpb[0:1, :], in_=w_imp.rearrange("d 1 -> 1 d")), writes=["wimpb"])
    for g in range(2):
        p.t(lambda e, g=g: e.matmul(P0[:], onesrow_f[:],
                                    wimpb[0:1, 512 * g:512 * g + 512],
                                    start=True, stop=True),
            reads=["onesrow_f", "wimpb"], writes=["P0"])
        p.v(lambda e, g=g: e.tensor_copy(wimpb[:, 512 * g:512 * g + 512], P0[:]),
            reads=["P0"], writes=["wimpb"])
    for tc in range(32):
        tmfk, tmk = ("tmf", tc % 2), ("accb", 0)
        p.dma("sync", lambda e, tc=tc: e.dma_start(
            out=tmf[tc % 2][:], in_=x0[128 * tc:128 * tc + 128, :]), writes=[tmfk])
        p.v(lambda e, tc=tc: e.tensor_copy(tm[tc % 2][:], tmf[tc % 2][:]),
            reads=[tmfk], writes=[tmk])
        p.dma("sync", lambda e, tc=tc: e.dma_start(
            out=xtm_d[128 * tc:128 * tc + 128, :], in_=tm[tc % 2][:]),
            reads=[tmk], writes=["xtm"])
        p.v(lambda e, tc=tc: e.scalar_tensor_tensor(
            scr5[:], tmf[tc % 2][:, 0:512], 1.0, wimpb[:, 0:512],
            op0=OP.mult, op1=OP.mult, accum_out=mraw[:]),
            reads=[tmfk, "wimpb"], writes=["scr5", "mraw"])
        p.v(lambda e, tc=tc: e.scalar_tensor_tensor(
            scr5[:], tmf[tc % 2][:, 512:1024], 1.0, wimpb[:, 512:1024],
            op0=OP.mult, op1=OP.mult, accum_out=scol[:, tc:tc + 1]),
            reads=[tmfk, "wimpb"], writes=["scr5", ("scol", "b")])
        p.v(lambda e, tc=tc: e.tensor_add(scol[:, tc:tc + 1], scol[:, tc:tc + 1],
                                          mraw[:]),
            reads=[("scol", "b"), "mraw"], writes=["scol"])
        for dc in range(8):
            PT, ptk = (P4c, ("P4", 0)) if dc % 2 == 0 else (P4d, ("P4", 1))
            p.t(lambda e, tc=tc, dc=dc, PT=PT: e.transpose(
                PT[:], tm[tc % 2][:, 128 * dc:128 * dc + 128], identb[:]),
                reads=[tmk, "identb"], writes=[ptk])
            p.v(lambda e, tc=tc, dc=dc, PT=PT: e.tensor_copy(
                xT[dc][:, 128 * tc:128 * tc + 128], PT[:]),
                reads=[ptk], writes=[("xT", dc)])

    if phases >= 2:
        p.v(lambda e: e.reduce_max(mraw[:], scol[:], axis=AX),
            reads=["scol"], writes=["mraw"])
        p.t(lambda e: e.transpose(P5[0:1, 0:128], mraw[:], identf[:]),
            reads=["mraw", "identf"], writes=["P5"])
        p.v(lambda e: e.tensor_copy(rmsr[0:1, 0:128], P5[0:1, 0:128]),
            reads=["P5"], writes=["rmsr"])
        p.v(lambda e: e.reduce_max(c11[1][:], rmsr[0:1, 0:128], axis=AX),
            reads=["rmsr"], writes=[("c11", 1)])
        p.v(lambda e: e.reduce_max(mraw[:], scol[:], axis=AX, negate=True),
            reads=["scol"], writes=["mraw"])
        p.t(lambda e: e.transpose(P5[0:1, 0:128], mraw[:], identf[:]),
            reads=["mraw", "identf"], writes=["P5"])
        p.v(lambda e: e.tensor_copy(rmsr[0:1, 0:128], P5[0:1, 0:128]),
            reads=["P5"], writes=["rmsr"])
        p.v(lambda e: e.reduce_max(c11[2][:], rmsr[0:1, 0:128], axis=AX),
            reads=["rmsr"], writes=[("c11", 2)])
        p.v(lambda e: e.tensor_scalar(c11[0][:], c11[2][:], -1.0, -1.0,
                                      op0=OP.mult, op1=OP.add),
            reads=[("c11", 2)], writes=[("c11", 0)])
        for it in range(45):
            p.v(lambda e: e.tensor_add(c11[3][:], c11[0][:], c11[1][:]),
                reads=[("c11", 0), ("c11", 1)], writes=[("c11", 3)])
            p.v(lambda e: e.tensor_scalar_mul(c11[3][:], c11[3][:], 0.5),
                reads=[("c11", 3)], writes=[("c11", 3)])
            p.t(lambda e: e.matmul(P5[:, 0:1], onesrow_f[:], c11[3][:],
                                   start=True, stop=True),
                reads=["onesrow_f", ("c11", 3)], writes=["P5"])
            p.v(lambda e: e.tensor_copy(mraw[:], P5[:, 0:1]),
                reads=["P5"], writes=["mraw"])
            p.v(lambda e: e.tensor_scalar(indS[:], scol[:], mraw[:], None,
                                          op0=OP.is_gt),
                reads=["scol", "mraw"], writes=["indS"])
            p.v(lambda e: e.reduce_sum(mraw[:], indS[:], axis=AX),
                reads=["indS"], writes=["mraw"])
            p.t(lambda e: e.matmul(P5[0:1, 0:128], mraw[:], identf[:],
                                    start=True, stop=True),
                reads=["mraw", "identf"], writes=["P5"])
            p.v(lambda e: e.tensor_copy(rmsr[0:1, 0:128], P5[0:1, 0:128]),
                reads=["P5"], writes=["rmsr"])
            p.v(lambda e: e.reduce_sum(c11[4][:], rmsr[0:1, 0:128], axis=AX),
                reads=["rmsr"], writes=[("c11", 4)])
            p.v(lambda e: e.tensor_scalar(c11[5][:], c11[4][:], 256.0, None,
                                          op0=OP.is_ge),
                reads=[("c11", 4)], writes=[("c11", 5)])
            p.v(lambda e: e.tensor_sub(c11[4][:], c11[3][:], c11[0][:]),
                reads=[("c11", 3), ("c11", 0)], writes=[("c11", 4)])
            p.v(lambda e: e.scalar_tensor_tensor(c11[0][:], c11[4][:], c11[5][:],
                                                 c11[0][:], op0=OP.mult, op1=OP.add),
                reads=[("c11", 4), ("c11", 5), ("c11", 0)], writes=[("c11", 0)])
            p.v(lambda e: e.tensor_scalar(c11[5][:], c11[5][:], -1.0, 1.0,
                                          op0=OP.mult, op1=OP.add),
                reads=[("c11", 5)], writes=[("c11", 5)])
            p.v(lambda e: e.tensor_sub(c11[4][:], c11[3][:], c11[1][:]),
                reads=[("c11", 3), ("c11", 1)], writes=[("c11", 4)])
            p.v(lambda e: e.scalar_tensor_tensor(c11[1][:], c11[4][:], c11[5][:],
                                                 c11[1][:], op0=OP.mult, op1=OP.add),
                reads=[("c11", 4), ("c11", 5), ("c11", 1)], writes=[("c11", 1)])
        p.t(lambda e: e.matmul(P5[:, 0:1], onesrow_f[:], c11[1][:],
                               start=True, stop=True),
            reads=["onesrow_f", ("c11", 1)], writes=["P5"])
        p.v(lambda e: e.tensor_copy(mraw[:], P5[:, 0:1]), reads=["P5"], writes=["mraw"])
        p.v(lambda e: e.tensor_scalar(indS[:], scol[:], mraw[:], None, op0=OP.is_gt),
            reads=["scol", "mraw"], writes=["indS"])
        p.t(lambda e: e.matmul(P5[:, 0:1], onesrow_f[:], c11[0][:],
                               start=True, stop=True),
            reads=["onesrow_f", ("c11", 0)], writes=["P5"])
        p.v(lambda e: e.tensor_copy(mraw[:], P5[:, 0:1]), reads=["P5"], writes=["mraw"])
        p.v(lambda e: e.tensor_scalar(indT[:], scol[:], mraw[:], None, op0=OP.is_gt),
            reads=["scol", "mraw"], writes=["indT"])
        p.v(lambda e: e.tensor_sub(indT[:], indT[:], indS[:]),
            reads=["indT", "indS"], writes=["indT"])

        def prefix32(src, srck, dst, dstk):
            p.t(lambda e, src=src: e.matmul(P2[:, 0:32], sltf[:], src[:],
                                            start=True, stop=True),
                reads=["sltf", srck], writes=["P2"])
            p.t(lambda e, src=src: e.matmul(P5[0:1, 0:32], onescolf[:], src[:],
                                            start=True, stop=True),
                reads=["onescolf", srck], writes=["P5"])
            p.v(lambda e: e.tensor_copy(r32a[:], P5[0:1, 0:32]),
                reads=["P5"], writes=["r32a"])
            p.t(lambda e: e.transpose(P5[0:32, 0:1], r32a[:], identf[0:1, 0:1]),
                reads=["r32a", "identf"], writes=["P5"])
            p.v(lambda e: e.tensor_copy(mraw[0:32, 0:1], P5[0:32, 0:1]),
                reads=["P5"], writes=["mraw"])
            p.t(lambda e: e.matmul(P5[0:32, 0:1], slt32f[:], mraw[0:32, 0:1],
                                   start=True, stop=True),
                reads=["slt32f", "mraw"], writes=["P5"])
            p.v(lambda e: e.tensor_copy(mraw[0:32, 0:1], P5[0:32, 0:1]),
                reads=["P5"], writes=["mraw"])
            p.t(lambda e: e.transpose(P5[0:1, 0:32], mraw[0:32, 0:1],
                                      identf[0:32, 0:32]),
                reads=["mraw", "identf"], writes=["P5"])
            p.v(lambda e: e.tensor_copy(r32b[:], P5[0:1, 0:32]),
                reads=["P5"], writes=["r32b"])
            p.t(lambda e: e.matmul(P3[:, 0:32], onesrow_f[:], r32b[:],
                                   start=True, stop=True),
                reads=["onesrow_f", "r32b"], writes=["P3"])
            p.v(lambda e, dst=dst: e.tensor_copy(dst[:], P2[:, 0:32]),
                reads=["P2"], writes=[dstk])
            p.v(lambda e, dst=dst: e.tensor_add(dst[:], dst[:], P3[:, 0:32]),
                reads=[dstk, "P3"], writes=[dstk])

        prefix32(indT, "indT", prefS, "prefS")
        p.t(lambda e: e.matmul(P5[0:1, 0:32], onescolf[:], indS[:],
                               start=True, stop=True),
            reads=["onescolf", "indS"], writes=["P5"])
        p.v(lambda e: e.tensor_copy(r32a[:], P5[0:1, 0:32]), reads=["P5"], writes=["r32a"])
        p.v(lambda e: e.reduce_sum(c11[2][:], r32a[:], axis=AX),
            reads=["r32a"], writes=[("c11", 2)])
        p.t(lambda e: e.matmul(P5[:, 0:1], onesrow_f[:], c11[2][:],
                               start=True, stop=True),
            reads=["onesrow_f", ("c11", 2)], writes=["P5"])
        p.v(lambda e: e.tensor_copy(mraw[:], P5[:, 0:1]), reads=["P5"], writes=["mraw"])
        p.v(lambda e: e.tensor_scalar(prefS[:], prefS[:], mraw[:], None, op0=OP.add),
            reads=["prefS", "mraw"], writes=["prefS"])
        p.v(lambda e: e.tensor_scalar(prefS[:], prefS[:], 256.0, None, op0=OP.is_lt),
            reads=["prefS"], writes=["prefS"])
        p.v(lambda e: e.tensor_mul(indT[:], indT[:], prefS[:]),
            reads=["indT", "prefS"], writes=["indT"])
        p.v(lambda e: e.tensor_add(indS[:], indS[:], indT[:]),
            reads=["indS", "indT"], writes=["indS"])
        prefix32(indS, "indS", prefS, "prefS")
        for sh in range(2):
            for tc in range(32):
                p.v(lambda e, tc=tc: e.tensor_scalar(
                    gch[tc % 2][:], slotb[:, 0:256],
                    prefS[:, tc:tc + 1], indS[:, tc:tc + 1],
                    op0=OP.is_equal, op1=OP.mult),
                    reads=["slotb", "prefS", "indS"], writes=[("gch", tc % 2)])
                p.dma("sync", lambda e, tc=tc: e.dma_start(
                    out=tm[tc % 2][:], in_=xtm_d[128 * tc:128 * tc + 128, :]),
                    reads=["xtm"], writes=[("accb", 0)])
                for oh in range(2):
                    PS, pk = (P2, "P2") if oh == 0 else (P3, "P3")
                    p.t(lambda e, tc=tc, sh=sh, oh=oh, PS=PS: e.matmul(
                        PS[:], gch[tc % 2][:, 128 * sh:128 * sh + 128],
                        tm[tc % 2][:, 512 * oh:512 * oh + 512],
                        start=(tc == 0), stop=(tc == 31)),
                        reads=[("gch", tc % 2), ("accb", 0)], writes=[pk])
            for oh in range(2):
                PS, pk = (P2, "P2") if oh == 0 else (P3, "P3")
                for q4 in range(4):
                    p.v(lambda e, PS=PS, q4=q4: e.tensor_copy(
                        scr5b[:, 128 * q4:128 * q4 + 128],
                        PS[:, 128 * q4:128 * q4 + 128]),
                        reads=[pk], writes=["scr5b"])
                for q4 in range(4):
                    dc = 4 * oh + q4
                    PT, ptk = (P4c, ("P4", 0)) if q4 % 2 == 0 else (P4d, ("P4", 1))
                    p.t(lambda e, q4=q4, PT=PT: e.transpose(
                        PT[:], scr5b[:, 128 * q4:128 * q4 + 128], identb[:]),
                        reads=["scr5b", "identb"], writes=[ptk])
                    p.v(lambda e, dc=dc, sh=sh, PT=PT: e.tensor_copy(
                        selT[dc][:, 128 * sh:128 * sh + 128], PT[:]),
                        reads=[ptk], writes=[("selT", dc)])
    wprojb = [tri_a[:, 1024 * i:1024 * i + 1024] for i in range(2)]

    def loadw_bf(dst, dstk, src_ap, n, rows=128):
        p.dma("sync", lambda e, src_ap=src_ap, n=n, rows=rows: e.dma_start(
            out=tmf[0][0:rows, 0:n], in_=src_ap), writes=[("tmf", 0)])
        p.v(lambda e, dst=dst, n=n, rows=rows: e.tensor_copy(
            dst, tmf[0][0:rows, 0:n]),
            reads=[("tmf", 0)], writes=[dstk])

    def rope_rows(dst_bf, dstk, src1, src2, srck, base, width, possl, scale, dco=0):
        for cg in range(0, width, 512):
            cw = min(512, width - cg)
            p.dma("sync", lambda e, cg=cg, cw=cw, possl=possl: e.dma_start(
                out=ctab[0][:, 0:cw],
                in_=cos_in[:, possl + cg:possl + cg + cw]),
                writes=["ctab"])
            p.dma("sync", lambda e, cg=cg, cw=cw, possl=possl: e.dma_start(
                out=stab[0][:, 0:cw],
                in_=sin_in[:, possl + cg:possl + cg + cw]),
                writes=["stab"])
            tk, sk = "ctab", "stab"
            ct, st = ctab[0], stab[0]
            p.v(lambda e, cg=cg, cw=cw, src1=src1, ct=ct: e.tensor_mul(
                rp1[:, 0:cw], src1[0:32, cg:cg + cw], ct[:, 0:cw]),
                reads=[srck, tk], writes=["rp1"])
            p.v(lambda e, cg=cg, cw=cw, src2=src2, st=st: e.tensor_mul(
                rp2[:, 0:cw], src2[0:32, cg:cg + cw], st[:, 0:cw]),
                reads=[srck, sk], writes=["rp2"])
            p.v(lambda e, cw=cw: e.tensor_sub(rp1[:, 0:cw], rp1[:, 0:cw],
                                              rp2[:, 0:cw]),
                reads=["rp1", "rp2"], writes=["rp1"])
            p.v(lambda e, cg=cg, cw=cw, dst_bf=dst_bf, base=base, scale=scale,
                dco=dco: e.tensor_scalar_mul(
                    dst_bf[base:base + 32, dco + cg:dco + cg + cw],
                    rp1[:, 0:cw], scale),
                reads=["rp1"], writes=[dstk])
            p.v(lambda e, cg=cg, cw=cw, src1=src1, st=st: e.tensor_mul(
                rp1[:, 0:cw], src1[0:32, cg:cg + cw], st[:, 0:cw]),
                reads=[srck, sk], writes=["rp1"])
            p.v(lambda e, cg=cg, cw=cw, src2=src2, ct=ct: e.tensor_mul(
                rp2[:, 0:cw], src2[0:32, cg:cg + cw], ct[:, 0:cw]),
                reads=[srck, tk], writes=["rp2"])
            p.v(lambda e, cw=cw: e.tensor_add(rp1[:, 0:cw], rp1[:, 0:cw],
                                              rp2[:, 0:cw]),
                reads=["rp1", "rp2"], writes=["rp1"])
            p.v(lambda e, cg=cg, cw=cw, dst_bf=dst_bf, base=base, scale=scale,
                dco=dco: e.tensor_scalar_mul(
                    dst_bf[base + 32:base + 64, dco + cg:dco + cg + cw],
                    rp1[:, 0:cw], scale),
                reads=["rp1"], writes=[dstk])

    eps6l = sb("eps6l", [1, 1], F32)
    SCF = float(1.0 / np.sqrt(96.0))
    if phases >= 3:
        p.v(lambda e: e.memset(eps6l[:], 1e-6), writes=["eps6l"])
        for ic in range(8):
            loadw_bf(wst[:, 96 * ic:96 * ic + 96], "wst",
                     w_cq[128 * ic:128 * ic + 128, :], 96)

        def rmsproj(nrows, wtiles, wk, gcol, gk_, outT, outk, eps):
            for grp in range(8):
                for ic in range(8):
                    p.t(lambda e, grp=grp, ic=ic, wtiles=wtiles, nrows=nrows:
                        e.matmul(P0[0:nrows, 0:512], wtiles(ic),
                                 xT[ic][:, 512 * grp:512 * grp + 512],
                                 start=(ic == 0), stop=(ic == 7)),
                        reads=[wk, ("xT", ic)], writes=["P0"])
                p.v(lambda e, nrows=nrows: e.tensor_copy(
                    nqf[0:nrows, 0:512], P0[0:nrows, 0:512]),
                    reads=["P0"], writes=["nqf"])
                p.s(lambda e, nrows=nrows: e.activation(
                    scr5[0:nrows, :], nqf[0:nrows, 0:512], AF.Square),
                    reads=["nqf"], writes=["scr5"])
                p.t(lambda e, nrows=nrows: e.matmul(
                    P1[0:1, 0:512], onescolf[0:nrows, :], scr5[0:nrows, :],
                    start=True, stop=True),
                    reads=["onescolf", "scr5"], writes=["P1"])
                p.s(lambda e, nrows=nrows, eps=eps: e.activation(
                    rmsr[:], P1[0:1, 0:512], AF.Sqrt, bias=eps6l[:],
                    scale=1.0 / nrows),
                    reads=["P1", "eps6l"], writes=["rmsr"])
                p.v(lambda e: e.reciprocal(rmsr[:], rmsr[:]),
                    reads=["rmsr"], writes=["rmsr"])
                p.t(lambda e, nrows=nrows: e.matmul(
                    P1[0:nrows, 0:512], onesrow_f[0:1, 0:nrows], rmsr[:],
                    start=True, stop=True),
                    reads=["onesrow_f", "rmsr"], writes=["P1"])
                p.v(lambda e, nrows=nrows: e.tensor_mul(
                    nqf[0:nrows, 0:512], nqf[0:nrows, 0:512], P1[0:nrows, 0:512]),
                    reads=["nqf", "P1"], writes=["nqf"])
                p.v(lambda e, grp=grp, nrows=nrows, outT=outT, gcol=gcol:
                    e.tensor_scalar_mul(outT[0:nrows, 512 * grp:512 * grp + 512],
                                        nqf[0:nrows, 0:512], gcol),
                    reads=["nqf", gk_], writes=[outk])

        rmsproj(96, lambda ic: wst[:, 96 * ic:96 * ic + 96], "wst",
                gq[0:96, :], "gq", nqT, "nqT", 1e-6)
        for ic in range(8):
            loadw_bf(wst[:, 32 * ic:32 * ic + 32], "wst",
                     w_ckv[128 * ic:128 * ic + 128, :], 32)
        rmsproj(32, lambda ic: wst[:, 32 * ic:32 * ic + 32], "wst",
                gk[0:32, :], "gk", ckvT, "ckvT", 1e-6)
        for ic in range(8):
            loadw_bf(wst[:, 64 * ic:64 * ic + 64], "wst",
                     w_kr[128 * ic:128 * ic + 128, :], 64)
        for grp in range(8):
            for ic in range(8):
                p.t(lambda e, grp=grp, ic=ic: e.matmul(
                    P0[0:64, 0:512], wst[:, 64 * ic:64 * ic + 64],
                    xT[ic][:, 512 * grp:512 * grp + 512],
                    start=(ic == 0), stop=(ic == 7)),
                    reads=["wst", ("xT", ic)], writes=["P0"])
            p.v(lambda e, grp=grp: e.tensor_copy(scr5[0:32, :], P0[0:32, 0:512]),
                reads=["P0"], writes=["scr5"])
            p.v(lambda e, grp=grp: e.tensor_copy(scr5B[:], P0[32:64, 0:512]),
                reads=["P0"], writes=["scr5"])
            rope_rows(kropeT, "kropeT", scr5, scr5B, "scr5", 0, 512, 512 * grp,
                      1.0 / 16.0, dco=512 * grp)
        loadw_bf(wst[0:32, 0:256], "wst", w_dv, 256, rows=32)
        for tc in range(32):
            p.t(lambda e, tc=tc: e.matmul(
                P0[:, 0:256], ckvT[:, 128 * tc:128 * tc + 128], wst[0:32, 0:256],
                start=True, stop=True),
                reads=["ckvT", "wst"], writes=["P0"])
            p.v(lambda e, tc=tc: e.tensor_copy(
                v1b[tc].rearrange("p (h c) -> p h c", h=8)[:, :, 0:32],
                P0[:, 0:256].rearrange("p (h c) -> p h c", h=8)),
                reads=["P0"], writes=[("v1b", tc)])
            p.v(lambda e, tc=tc: e.memset(
                v1b[tc].rearrange("p (h c) -> p h c", h=8)[:, :, 32:33], 1.0),
                reads=[], writes=[("v1b", tc)])
        wvstage = [accb[1][:, 256 * i:256 * i + 256] for i in range(8)]
        for ic in range(8):
            loadw_bf(wvstage[ic][:], ("wvst", ic),
                     w_winv[128 * ic:128 * ic + 128, :], 256)
        for tc in range(32):
            for ic in range(8):
                p.t(lambda e, tc=tc, ic=ic: e.matmul(
                    P0[:, 0:256], xT[ic][:, 128 * tc:128 * tc + 128],
                    wvstage[ic][:], start=(ic == 0), stop=(ic == 7)),
                    reads=[("xT", ic), ("wvst", ic)], writes=["P0"])
            p.v(lambda e, tc=tc: e.tensor_copy(
                vwb[tc].rearrange("p (h c) -> p h c", h=8)[:, :, 0:32],
                P0[:, 0:256].rearrange("p (h c) -> p h c", h=8)),
                reads=["P0"], writes=[("vwb", tc)])
            p.v(lambda e, tc=tc: e.memset(
                vwb[tc].rearrange("p (h c) -> p h c", h=8)[:, :, 32:33], 1.0),
                reads=[], writes=[("vwb", tc)])
        for ic in range(8):
            loadw_bf(wvstage[ic][:], ("wvst", ic),
                     w_selv[128 * ic:128 * ic + 128, :], 256)
        for sh in range(2):
            for ic in range(8):
                p.t(lambda e, sh=sh, ic=ic: e.matmul(
                    P0[:, 0:256], selT[ic][:, 128 * sh:128 * sh + 128],
                    wvstage[ic][:], start=(ic == 0), stop=(ic == 7)),
                    reads=[("selT", ic), ("wvst", ic)], writes=["P0"])
            p.v(lambda e, sh=sh: e.tensor_copy(
                vsb[sh].rearrange("p (h c) -> p h c", h=8)[:, :, 0:32],
                P0[:, 0:256].rearrange("p (h c) -> p h c", h=8)),
                reads=["P0"], writes=[("vsb", sh)])
            p.v(lambda e, sh=sh: e.memset(
                vsb[sh].rearrange("p (h c) -> p h c", h=8)[:, :, 32:33], 1.0),
                reads=[], writes=[("vsb", sh)])
        for ic in range(8):
            loadw_bf(wst[:, 3 * ic:3 * ic + 3], "wst",
                     w_gate[128 * ic:128 * ic + 128, :], 3)
        p.v(lambda e: e.memset(mcols[0:3, 0:1], 0.0), writes=["mcols"])
        for grp in range(8):
            for ic in range(8):
                p.t(lambda e, grp=grp, ic=ic: e.matmul(
                    P0[0:3, 0:512], wst[:, 3 * ic:3 * ic + 3],
                    xT[ic][:, 512 * grp:512 * grp + 512],
                    start=(ic == 0), stop=(ic == 7)),
                    reads=["wst", ("xT", ic)], writes=["P0"])
            p.v(lambda e: e.reduce_sum(mraw[0:3, 0:1], P0[0:3, 0:512], axis=AX),
                reads=["P0"], writes=["mraw"])
            p.v(lambda e: e.tensor_add(mcols[0:3, 0:1], mcols[0:3, 0:1],
                                       mraw[0:3, 0:1]),
                reads=["mcols", "mraw"], writes=["mcols"])
        p.t(lambda e: e.transpose(P5[0:1, 0:3], mcols[0:3, 0:1],
                                  identf[0:3, 0:3]),
            reads=["mcols", "identf"], writes=["P5"])
        p.v(lambda e: e.tensor_scalar_mul(r32a[0:1, 0:3], P5[0:1, 0:3], 1.0 / T),
            reads=["P5"], writes=["r32a"])
        p.v(lambda e: e.reduce_max(c11[0][:], r32a[0:1, 0:3], axis=AX, negate=True),
            reads=["r32a"], writes=[("c11", 0)])
        p.s(lambda e: e.activation(r32b[0:1, 0:3], r32a[0:1, 0:3], AF.Exp,
                                   bias=c11[0][:]),
            reads=["r32a", ("c11", 0)], writes=["r32b"])
        p.v(lambda e: e.reduce_sum(c11[1][:], r32b[0:1, 0:3], axis=AX),
            reads=["r32b"], writes=[("c11", 1)])
        p.v(lambda e: e.reciprocal(c11[1][:], c11[1][:]),
            reads=[("c11", 1)], writes=[("c11", 1)])
        p.v(lambda e: e.tensor_scalar_mul(r32b[0:1, 0:3], r32b[0:1, 0:3],
                                          c11[1][:]),
            reads=["r32b", ("c11", 1)], writes=["r32b"])
        p.t(lambda e: e.matmul(P5[:, 0:3], onesrow_f[:], r32b[0:1, 0:3],
                               start=True, stop=True),
            reads=["onesrow_f", "r32b"], writes=["P5"])
        p.v(lambda e: e.tensor_copy(gsc[:, 0:3], P5[:, 0:3]),
            reads=["P5"], writes=["gsc"])

    if phases >= 4:
        for h in range(NH):
            loadw_bf(wst[0:96, 0:32], "wst", w_dqn[:, 32 * h:32 * h + 32], 32,
                     rows=96)
            loadw_bf(wst[0:96, 32:96], "wst", w_dqr[:, 64 * h:64 * h + 64], 64,
                     rows=96)
            for grp in range(8):
                p.t(lambda e, grp=grp: e.matmul(
                    P0[0:32, 0:512], wst[0:96, 0:32],
                    nqT[:, 512 * grp:512 * grp + 512], start=True, stop=True),
                    reads=["wst", "nqT"], writes=["P0"])
                p.t(lambda e, grp=grp: e.matmul(
                    P1[0:64, 0:512], wst[0:96, 32:96],
                    nqT[:, 512 * grp:512 * grp + 512], start=True, stop=True),
                    reads=["wst", "nqT"], writes=["P1"])
                p.s(lambda e, grp=grp: e.activation(
                    qt_[0:32, 512 * grp:512 * grp + 512], P0[0:32, 0:512],
                    AF.Copy, scale=SCF),
                    reads=["P0"], writes=["qt_"])
                p.v(lambda e, grp=grp: e.tensor_copy(scr5[0:32, :], P1[0:32, 0:512]),
                    reads=["P1"], writes=["scr5"])
                p.v(lambda e, grp=grp: e.tensor_copy(scr5B[:], P1[32:64, 0:512]),
                    reads=["P1"], writes=["scr5"])
                rope_rows(qt_, "qt_", scr5, scr5B, "scr5", 32, 512, 512 * grp, SCF,
                          dco=512 * grp)
            loadw_bf(wst[0:32, 0:32], "wst", w_dkn[:, 32 * h:32 * h + 32], 32, rows=32)
            for grp in range(8):
                p.t(lambda e, grp=grp: e.matmul(
                    P0[0:32, 0:512], wst[0:32, 0:32],
                    ckvT[:, 512 * grp:512 * grp + 512], start=True, stop=True),
                    reads=["wst", "ckvT"], writes=["P0"])
                p.s(lambda e, grp=grp: e.activation(
                    k1t[0:32, 512 * grp:512 * grp + 512], P0[0:32, 0:512], AF.Copy),
                    reads=["P0"], writes=["k1t"])
            p.v(lambda e: e.tensor_copy(k1t[32:64, :], kropeT[0:32, :]),
                reads=["kropeT"], writes=["k1t"])
            p.v(lambda e: e.tensor_copy(k1t[64:96, :], kropeT[32:64, :]),
                reads=["kropeT"], writes=["k1t"])
            loadw_bf(wst[:, 0:96], "wst", w_wink[0:128, 96 * h:96 * h + 96], 96)
            for ic in range(1, 8):
                loadw_bf(wst[:, 96 * 0 + 0:96] if False else
                         wvstage[ic][:, 0:96], ("wvst", ic),
                         w_wink[128 * ic:128 * ic + 128, 96 * h:96 * h + 96], 96)
            for grp in range(8):
                for ic in range(8):
                    lhs = wst[:, 0:96] if ic == 0 else wvstage[ic][:, 0:96]
                    lk = "wst" if ic == 0 else ("wvst", ic)
                    p.t(lambda e, grp=grp, ic=ic, lhs=lhs: e.matmul(
                        P0[0:96, 0:512], lhs,
                        xT[ic][:, 512 * grp:512 * grp + 512],
                        start=(ic == 0), stop=(ic == 7)),
                        reads=[lk, ("xT", ic)], writes=["P0"])
                p.s(lambda e, grp=grp: e.activation(
                    kwt[0:32, 512 * grp:512 * grp + 512], P0[0:32, 0:512], AF.Copy),
                    reads=["P0"], writes=["kwt"])
                p.v(lambda e, grp=grp: e.tensor_copy(scr5[0:32, :], P0[32:64, 0:512]),
                    reads=["P0"], writes=["scr5"])
                p.v(lambda e, grp=grp: e.tensor_copy(scr5B[:], P0[64:96, 0:512]),
                    reads=["P0"], writes=["scr5"])
                rope_rows(kwt, "kwt", scr5, scr5B, "scr5", 32, 512, 512 * grp, 1.0,
                          dco=512 * grp)
            loadw_bf(wst[:, 0:96], "wst", w_selk[0:128, 96 * h:96 * h + 96], 96)
            for ic in range(1, 8):
                loadw_bf(wvstage[ic][:, 0:96], ("wvst", ic),
                         w_selk[128 * ic:128 * ic + 128, 96 * h:96 * h + 96], 96)
            for ic in range(8):
                lhs = wst[:, 0:96] if ic == 0 else wvstage[ic][:, 0:96]
                lk = "wst" if ic == 0 else ("wvst", ic)
                p.t(lambda e, ic=ic, lhs=lhs: e.matmul(
                    P0[0:96, 0:256], lhs, selT[ic][:],
                    start=(ic == 0), stop=(ic == 7)),
                    reads=[lk, ("selT", ic)], writes=["P0"])
            p.s(lambda e: e.activation(kst[0:32, :], P0[0:32, 0:256], AF.Copy),
                reads=["P0"], writes=["kst"])
            p.v(lambda e: e.tensor_copy(scr5[0:32, 0:256], P0[32:64, 0:256]),
                reads=["P0"], writes=["scr5"])
            p.v(lambda e: e.tensor_copy(scr5B[:, 0:256], P0[64:96, 0:256]),
                reads=["P0"], writes=["scr5"])
            rope_rows(kst, "kst", scr5, scr5B, "scr5", 32, 256, 0, 1.0)
            for qg in range(8):
                pend_tail = None
                for br in range(3):
                    kt, ktk, nkc = ((k1t, "k1t", 4 * (qg + 1)) if br == 0 else
                                    (kst, "kst", 2) if br == 1 else
                                    (kwt, "kwt", 4 * (qg + 1)))
                    vt_ = (v1b if br == 0 else vsb if br == 1 else vwb)
                    vkk = ("v1b" if br == 0 else "vsb" if br == 1 else "vwb")
                    pend_mt = None
                    for q4 in range(4):
                        qt = 4 * qg + q4
                        nA = qt // 4 + 1 if br != 1 else 1
                        mb_ = 8 * (q4 % 2)
                        mck = ("mcols", q4 % 2)
                        if q4 == 0 and pend_tail is not None:
                            pend_tail[0]()
                        for kc in range(nA):
                            PS, pk = (P2, "P2") if kc % 2 == 0 else (P3, "P3")
                            kwid = 256 if br == 1 else 512
                            p.t(lambda e, qt=qt, kc=kc, kt=kt, PS=PS, kwid=kwid:
                                e.matmul(PS[:, 0:kwid],
                                         qt_[0:96, 128 * qt:128 * qt + 128],
                                         kt[0:96, kwid * kc:kwid * kc + kwid],
                                         start=True, stop=True),
                                reads=["qt_", ktk], writes=[pk])
                            if br != 1 and kc == nA - 1:
                                p.v(lambda e, qt=qt, PS=PS: e.tensor_add(
                                    scr5[:], PS[:, 0:512],
                                    tri_a[:, 512 * (qt % 4):512 * (qt % 4) + 512]),
                                    reads=[pk, "tri_a"], writes=["scr5"])
                                p.v(lambda e, kc=kc, mb_=mb_: e.reduce_max(
                                    mcols[:, mb_ + kc:mb_ + kc + 1], scr5[:],
                                    axis=AX),
                                    reads=["scr5"], writes=[mck])
                            else:
                                p.v(lambda e, kc=kc, PS=PS, kwid=kwid, mb_=mb_:
                                    e.reduce_max(
                                    mcols[:, mb_ + kc:mb_ + kc + 1],
                                    PS[:, 0:kwid], axis=AX),
                                    reads=[pk], writes=[mck])
                        p.v(lambda e, nA=nA, mb_=mb_, q4=q4: e.reduce_max(
                            mrawq[:, q4:q4 + 1], mcols[:, mb_:mb_ + nA],
                            axis=AX, negate=True),
                            reads=[mck], writes=[("mrawq", q4)])
                        if q4 == 0 and pend_tail is not None:
                            pend_tail[1]()
                            pend_tail = None
                        if pend_mt is not None:
                            pend_mt()

                        def mk_mt(q4=q4, qg=qg):
                            def go():
                                p.t(lambda e, q4=q4: e.transpose(
                                    P5[0:1, 0:128], mrawq[:, q4:q4 + 1],
                                    identf[:]),
                                    reads=[("mrawq", q4), "identf"],
                                    writes=["P5"])
                                p.v(lambda e, qg=qg, q4=q4: e.tensor_copy(
                                    qt_[96:97, 512 * qg + 128 * q4:
                                        512 * qg + 128 * q4 + 128],
                                    P5[0:1, 0:128]),
                                    reads=["P5"], writes=["qt_"])
                            return go
                        pend_mt = mk_mt()
                    if pend_tail is not None:
                        pend_tail[0]()
                        pend_tail[1]()
                        pend_tail = None
                    pend_mt()
                    nkcb = 2 if br == 1 else 4 * (qg + 1)
                    pend_ea = None
                    for kc in range(nkcb):
                        off = 0
                        if br != 1 and kc // 4 == qg:
                            off = 128 * (kc % 4)
                        PS, pk = (P2, "P2") if kc % 2 == 0 else (P3, "P3")
                        p.t(lambda e, qg=qg, kc=kc, kt=kt, PS=PS, off=off: e.matmul(
                            PS[:, off:512], kt[0:97, 128 * kc:128 * kc + 128],
                            qt_[0:97, 512 * qg + off:512 * qg + 512],
                            start=True, stop=True),
                            reads=[ktk, "qt_"], writes=[pk])
                        if br != 1 and kc // 4 == qg:
                            j = kc % 4
                            p.v(lambda e, j=j, PS=PS: e.tensor_add(
                                PS[:, 128 * j:128 * j + 128],
                                PS[:, 128 * j:128 * j + 128], tri_b[:]),
                                reads=[pk, "tri_b"], writes=[pk])
                        if pend_ea is not None:
                            pend_ea()

                        def mk_ea(kc=kc, PS=PS, pk=pk, off=off, vt_=vt_,
                                  vkk=vkk, nkcb=nkcb, h=h):
                            def go():
                                p.s(lambda e: e.activation(
                                    ptb[kc % 2][:, off:512], PS[:, off:512],
                                    AF.Exp),
                                    reads=[pk], writes=[("ptb", kc % 2)])
                                p.t(lambda e: e.matmul(
                                    PAV[:, off:512],
                                    vt_[kc].rearrange("p (g c) -> p g c",
                                                      g=8)[:, h, 0:33],
                                    ptb[kc % 2][:, off:512],
                                    start=(kc == 0), stop=(kc == nkcb - 1)),
                                    reads=[(vkk, kc), ("ptb", kc % 2)],
                                    writes=["PAV"])
                            return go
                        pend_ea = mk_ea()
                    pend_ea()

                    def mk_tail(br=br, h=h, qg=qg):
                        def front():
                            p.v(lambda e: e.tensor_copy(avsb[:],
                                                        PAV[0:32, 0:512]),
                                reads=["PAV"], writes=["avsb"])
                            p.v(lambda e: e.reciprocal(prow[:],
                                                       PAV[32:33, 0:512]),
                                reads=["PAV"], writes=["prow"])
                        def back():
                            p.t(lambda e: e.matmul(PAV[0:32, 0:512],
                                                   onesrow_f[0:1, 0:32],
                                                   prow[:],
                                                   start=True, stop=True),
                                reads=["onesrow_f", "prow"], writes=["PAV"])
                            p.v(lambda e: e.tensor_mul(avsb[:], avsb[:],
                                                       PAV[0:32, 0:512]),
                                reads=["avsb", "PAV"], writes=["avsb"])
                            if br == 0:
                                p.v(lambda e: e.tensor_scalar_mul(
                                    comb[:], avsb[:], gsc[0:32, br:br + 1]),
                                    reads=["avsb", "gsc"], writes=["comb"])
                            else:
                                p.v(lambda e: e.scalar_tensor_tensor(
                                    comb[:], avsb[:], gsc[0:32, br:br + 1],
                                    comb[:], op0=OP.mult, op1=OP.add),
                                    reads=["avsb", "gsc", "comb"],
                                    writes=["comb"])
                        return (front, back)
                    pend_tail = mk_tail()
                pend_tail[0]()
                pend_tail[1]()
                p.v(lambda e, h=h, qg=qg: e.tensor_copy(
                    accb[h // 4][32 * (h % 4):32 * (h % 4) + 32,
                                 512 * qg:512 * qg + 512], comb[:]),
                    reads=["comb"], writes=[("accb", h // 4)])

    if phases >= 5:
        loadw_bf(wprojb[0][:], "tri_a", w_proj[0:128, :], 1024)
        loadw_bf(wprojb[1][:], "tri_a", w_proj[128:256, :], 1024)
        for tc in range(32):
            for oh in range(2):
                PS, pk = (P0, "P0") if oh == 0 else (P1, "P1")
                for cc in range(2):
                    p.t(lambda e, tc=tc, oh=oh, cc=cc, PS=PS: e.matmul(
                        PS[:], accb[cc][:, 128 * tc:128 * tc + 128],
                        wprojb[cc][:, 512 * oh:512 * oh + 512],
                        start=(cc == 0), stop=(cc == 1)),
                        reads=[("accb", cc), "tri_a"], writes=[pk])
                p.v(lambda e, oh=oh, PS=PS: e.tensor_copy(
                    outt[:, 512 * oh:512 * oh + 512], PS[:]),
                    reads=[pk], writes=["outt"])
            p.dma("sync", lambda e, tc=tc: e.dma_start(
                out=partial[128 * tc:128 * tc + 128, :], in_=outt[:]),
                reads=["outt"], writes=[("part", tc)])





def build_l0(phases=99):
    nc = bass.Bass("TRN2", target_bir_lowering=False, dynamic_dma_scratch_size=2048)
    p = Prog(nc)
    x0 = nc.dram_tensor("x0", [T, D], F32, kind="ExternalInput").ap()
    w_cq = nc.dram_tensor("w_cq", [D, 96], F32, kind="ExternalInput").ap()
    g_qn = nc.dram_tensor("g_qn", [96, 1], F32, kind="ExternalInput").ap()
    w_dqn = nc.dram_tensor("w_dqn", [96, 256], F32, kind="ExternalInput").ap()
    w_dqr = nc.dram_tensor("w_dqr", [96, 512], F32, kind="ExternalInput").ap()
    w_ckv = nc.dram_tensor("w_ckv", [D, 32], F32, kind="ExternalInput").ap()
    g_kn = nc.dram_tensor("g_kn", [32, 1], F32, kind="ExternalInput").ap()
    w_dkn = nc.dram_tensor("w_dkn", [32, 256], F32, kind="ExternalInput").ap()
    w_dv = nc.dram_tensor("w_dv", [32, 256], F32, kind="ExternalInput").ap()
    w_kr = nc.dram_tensor("w_kr", [D, 64], F32, kind="ExternalInput").ap()
    w_imp = nc.dram_tensor("w_imp", [D, 1], F32, kind="ExternalInput").ap()
    w_selk = nc.dram_tensor("w_selk", [D, 768], F32, kind="ExternalInput").ap()
    w_selv = nc.dram_tensor("w_selv", [D, 256], F32, kind="ExternalInput").ap()
    w_wink = nc.dram_tensor("w_wink", [D, 768], F32, kind="ExternalInput").ap()
    w_winv = nc.dram_tensor("w_winv", [D, 256], F32, kind="ExternalInput").ap()
    w_gate = nc.dram_tensor("w_gate", [D, 3], F32, kind="ExternalInput").ap()
    w_proj = nc.dram_tensor("w_proj", [256, D], F32, kind="ExternalInput").ap()
    tri_a_in = nc.dram_tensor("tri_a", [4, 128, 512], BF16, kind="ExternalInput").ap()
    tri_b_in = nc.dram_tensor("tri_b", [128, 128], BF16, kind="ExternalInput").ap()
    cos_in = nc.dram_tensor("cos32", [32, T], F32, kind="ExternalInput").ap()
    sin_in = nc.dram_tensor("sin32", [32, T], F32, kind="ExternalInput").ap()
    swp_in = nc.dram_tensor("swp", [128, 128], F32, kind="ExternalInput").ap()
    slt_in = nc.dram_tensor("slt", [128, 128], F32, kind="ExternalInput").ap()
    slt32_in = nc.dram_tensor("slt32", [32, 32], F32, kind="ExternalInput").ap()
    slot_in = nc.dram_tensor("slotrow", [1, 256], F32, kind="ExternalInput").ap()
    ident_in = nc.dram_tensor("ident", [128, 128], F32, kind="ExternalInput").ap()
    identb_in = nc.dram_tensor("identb_in", [128, 128], BF16, kind="ExternalInput").ap()
    partial = nc.dram_tensor("partial", [T, D], F32, kind="ExternalOutput").ap()


    ctx = ExitStack()

    emit_l0(nc, p, ctx, "", x0, w_cq, g_qn, w_dqn, w_dqr, w_ckv, g_kn, w_dkn, w_dv, w_kr, w_imp, w_selk, w_selv, w_wink, w_winv, w_gate, w_proj, tri_a_in, tri_b_in, cos_in, sin_in, swp_in, slt_in, slt32_in, slot_in, ident_in, identb_in, partial, phases=phases)
    p.emit()
    ctx.close()
    return nc

def _l0_inmap(ins_d, x0_b, hg):
    cc = _l0_consts()
    gq = ins_d["L0_attn_g_qnorm"].astype(np.float32).reshape(96, 1)
    gk = ins_d["L0_attn_g_kvnorm"].astype(np.float32).reshape(32, 1)
    return {
        "x0": np.ascontiguousarray(x0_b),
        "w_cq": ins_d["L0_attn_w_cq"], "g_qn": gq,
        "w_dqn": np.ascontiguousarray(ins_d["L0_attn_w_dq_nope"][:, 256 * hg:256 * hg + 256]),
        "w_dqr": np.ascontiguousarray(ins_d["L0_attn_w_dq_rope"][:, 512 * hg:512 * hg + 512]),
        "w_ckv": ins_d["L0_attn_w_ckv"], "g_kn": gk,
        "w_dkn": np.ascontiguousarray(ins_d["L0_attn_w_dk_nope"][:, 256 * hg:256 * hg + 256]),
        "w_dv": np.ascontiguousarray(ins_d["L0_attn_w_dv"][:, 256 * hg:256 * hg + 256]),
        "w_kr": ins_d["L0_attn_w_krope"], "w_imp": ins_d["L0_attn_w_imp"],
        "w_selk": np.ascontiguousarray(ins_d["L0_attn_w_selk"][:, 768 * hg:768 * hg + 768]),
        "w_selv": np.ascontiguousarray(ins_d["L0_attn_w_selv"][:, 256 * hg:256 * hg + 256]),
        "w_wink": np.ascontiguousarray(ins_d["L0_attn_w_wink"][:, 768 * hg:768 * hg + 768]),
        "w_winv": np.ascontiguousarray(ins_d["L0_attn_w_winv"][:, 256 * hg:256 * hg + 256]),
        "w_gate": ins_d["L0_attn_w_gate"],
        "w_proj": np.ascontiguousarray(ins_d["L0_attn_w_proj"][256 * hg:256 * hg + 256, :]),
        "tri_a": cc["tri_a"].astype(BFD), "tri_b": cc["tri_b"].astype(BFD),
        "cos32": cc["cos32"], "sin32": cc["sin32"], "swp": cc["swp"],
        "slt": cc["slt"], "slt32": cc["slt32"], "slotrow": cc["slotrow"],
        "ident": np.eye(128, dtype=np.float32),
        "identb_in": np.eye(128).astype(BFD),
    }


def run_l0(ins_d, x0):
    nc = build_l0()
    in_maps = [_l0_inmap(ins_d, x0[c // 2], c % 2) for c in range(8)]
    res = run_bass_kernel_spmd(nc, in_maps, core_ids=list(range(8)))
    return [res.results[c]["partial"] for c in range(8)]


def kernel(**inputs):
    return run_fused(inputs)


def kernel_unfused(**inputs):
    ins = {k: np.asarray(v) for k, v in inputs.items()}
    x0 = np.ascontiguousarray(ins["L0_attn_x"].astype(np.float32))

    parts = run_l0(ins, x0)
    pa = [parts[2 * b] for b in range(B)]
    pb = [parts[2 * b + 1] for b in range(B)]

    x2 = run_moe(x0, pa, pb, ins["L0_ffn_gate_w"], ins["L0_ffn_w1"],
                 ins["L0_ffn_b1"], ins["L0_ffn_w2"], ins["L0_ffn_b2"],
                 ins["L0_ffn_ln_g"], ins["L0_ffn_ln_b"])

    x3 = run_l1(x2, ins["L1_attn_W_qk"], ins["L1_attn_W_v"],
                ins["L1_attn_alibi_slope"])

    out = run_l1f(x3, ins["L1_ffn_wqkv"], ins["L1_ffn_q_scale"],
                  ins["L1_ffn_k_scale"], ins["L1_ffn_wo"], ins["L1_ffn_ln_g"],
                  ins["L1_ffn_ln_b"], ins["L1_ffn_w1"], ins["L1_ffn_b1"],
                  ins["L1_ffn_w2"], ins["L1_ffn_b2"])
    return out



def emit_ext(nc, p, ctx, pfx, x3g_d, ext_d, ident_in):
    sb = lambda name, shape, dt: ctx.enter_context(nc.sbuf_tensor(pfx + name, shape, dt))
    psa = lambda name, shape, dt=F32: ctx.enter_context(nc.psum_tensor(pfx + name, shape, dt))
    zt = sb("zt", [128, 512], F32)
    ft = [sb(f"ft{i}", [128, D], F32) for i in range(2)]
    fsb = [sb(f"fsb{i}", [128, 128], F32) for i in range(2)]
    idf = sb("idf", [128, 128], F32)
    PT0 = psa("PT0", [128, 128])
    PT1 = psa("PT1", [128, 128])
    p.v(lambda e: e.memset(zt[:], 0.0), writes=["zt"])
    p.dma("sync", lambda e: e.dma_start(out=idf[:], in_=ident_in),
          writes=["idf"])
    for dc in range(8):
        for g in range(4):
            p.dma("sync", lambda e, dc=dc, g=g: e.dma_start(
                out=ext_d[128 * dc:128 * dc + 128, 512 * g:512 * g + 512],
                in_=zt[:]), reads=["zt"], writes=[("extz", dc)])
        p.dma("sync", lambda e, dc=dc: e.dma_start(
            out=ext_d[128 * dc:128 * dc + 128, 2048:2050], in_=zt[:, 0:2]),
            reads=["zt"], writes=[("extz", dc)])
    for j in range(32):
        row = (j % 2) * 2048 + (j // 2) * 128
        p.dma("sync", lambda e, j=j, row=row: e.dma_start(
            out=ft[j % 2][:], in_=x3g_d[row:row + 128, :]),
            reads=["x3gath"], writes=[("ft", j % 2)])
        for dc in range(8):
            PT, ptk = (PT0, "PT0") if dc % 2 == 0 else (PT1, "PT1")
            p.t(lambda e, j=j, dc=dc, PT=PT: e.transpose(
                PT[:], ft[j % 2][:, 128 * dc:128 * dc + 128], idf[:]),
                reads=[("ft", j % 2), "idf"], writes=[ptk])
            p.v(lambda e, dc=dc, PT=PT: e.tensor_copy(fsb[dc % 2][:], PT[:]),
                reads=[ptk], writes=[("fsb", dc % 2)])
            c0a, c0b = 2049 + 128 * j, 1 + 128 * j
            p.pdma("sync",
                   lambda e, dc=dc, c0=c0a: e.dma_start(
                       out=ext_d[128 * dc:128 * dc + 128, c0:c0 + 128],
                       in_=fsb[dc % 2][:]),
                   lambda e, dc=dc, c0=c0b: e.dma_start(
                       out=ext_d[128 * dc:128 * dc + 128, c0:c0 + 128],
                       in_=fsb[dc % 2][:]),
                   reads=[("fsb", dc % 2), ("extz", dc)],
                   writes=[("extw", j, dc)])


def build_fused():
    nc = bass.Bass("TRN2", target_bir_lowering=False, dynamic_dma_scratch_size=2048)
    p = Prog(nc)
    colmap, NBCOL = _l1_colmap()
    HID = 2730
    E = lambda n, sh, dt=F32: nc.dram_tensor(n, sh, dt, kind="ExternalInput").ap()
    x0 = E("x0", [T, D]); w_cq = E("w_cq", [D, 96]); g_qn = E("g_qn", [96, 1])
    w_dqn = E("w_dqn", [96, 256]); w_dqr = E("w_dqr", [96, 512])
    w_ckv = E("w_ckv", [D, 32]); g_kn = E("g_kn", [32, 1])
    w_dkn = E("w_dkn", [32, 256]); w_dv = E("w_dv", [32, 256])
    w_kr = E("w_kr", [D, 64]); w_imp = E("w_imp", [D, 1])
    w_selk = E("w_selk", [D, 768]); w_selv = E("w_selv", [D, 256])
    w_wink = E("w_wink", [D, 768]); w_winv = E("w_winv", [D, 256])
    w_gate = E("w_gate", [D, 3]); w_proj = E("w_proj", [256, D])
    tri_a_l0 = E("tri_a_l0", [4, 128, 512], BF16)
    tri_b_l0 = E("tri_b_l0", [128, 128], BF16)
    cos32 = E("cos32", [32, T]); sin32 = E("sin32", [32, T])
    swp_l0 = E("swp_l0", [128, 128]); slt = E("slt", [128, 128])
    slt32 = E("slt32", [32, 32]); slotrow = E("slotrow", [1, 256])
    ident = E("ident", [128, 128]); identb = E("identb_in", [128, 128], BF16)
    x0own = E("x0own", [HALF, D]); gate_w = E("gate_w", [D, 8])
    w1m = E("w1", [8, D, 2 * D]); b1t = E("b1t", [8, 128, 16])
    w2m = E("w2", [8, 2 * D, D]); b2m = E("b2", [8, D])
    lng_bc = E("lng_bc", [8, 128, D]); lnb_moe = E("lnb", [8, D])
    wqk = E("wqk", [D, D]); wv = E("wv", [D, D]); slope = E("slope", [1, 1])
    d0 = E("d0", [128, 512]); tri_a_l1 = E("tri_a", [2, 128, 512], BF16)
    tri_b_l1 = E("tri_b", [2, 128, 128], BF16); offa = E("offa", [1, 128])
    kpos = E("kpos", [128, NBCOL]); kill = E("kill", [128, NBCOL])
    qps = E("qps", [128, 16])
    wqkv = E("wqkv", [2, D, 1536]); qsc = E("qsc", [2, 128, 1])
    ksc = E("ksc", [2, 128, 1]); wo = E("wo", [2, 512, D])
    lng_f = E("lng", [2, 128, 8]); lnb_f = E("lnb_f", [2, 128, 8])
    w1a = E("w1a", [2, D, HID]); w1g = E("w1g", [2, D, HID])
    b1a = E("b1a", [2, 128, 22]); b1g = E("b1g", [2, 128, 22])
    w2f = E("w2f", [2, HID, D]); b2f = E("b2f", [2, 128, 8])
    mb = E("mb", [128, 640]); tri0 = E("tri0", [128, 128])
    tri4 = E("tri4", [128, 128]); killf = E("killf", [128, 240])
    mskrow = E("mskrow", [1, 512]); ropetabs = E("ropetabs", [8, 4, 128, 512])
    swp_f = E("swp", [128, 128]); blk2c = E("blk2c", [128, 2])
    exp2c = E("exp2c", [2, 128])
    outown = nc.dram_tensor("outown", [HALF, D], F32, kind="ExternalOutput").ap()
    I = lambda n, sh, dt=F32: nc.dram_tensor(n, sh, dt, kind="Internal").ap()
    partial_d = I("partial_d", [T, D])
    pg_d = I("pg_d", [2 * T, D])
    x2own_d = I("x2own_d", [HALF, D])
    x2f_d = I("x2f_d", [T, D])
    x3s_d = I("x3s_d", [HALF, D])
    x3g_d = I("x3g_d", [T, D])
    ext_d = I("ext_d", [D, 6145])

    pidcell = []
    def pid_init(e):
        pid = nc.partition_id([mybir.EngineType.SP])
        pidcell.append(pid % 2)
        return e.nop()
    p.op("sync", pid_init)
    p.parcond = lambda: pidcell[0] == 0

    c = ExitStack()
    emit_l0(nc, p, c, "a_", x0, w_cq, g_qn, w_dqn, w_dqr, w_ckv, g_kn, w_dkn,
            w_dv, w_kr, w_imp, w_selk, w_selv, w_wink, w_winv, w_gate, w_proj,
            tri_a_l0, tri_b_l0, cos32, sin32, swp_l0, slt, slt32, slotrow,
            ident, identb, partial_d)
    p.barrier()
    c.close()
    p.g(lambda e: e.collective_compute(
        "AllGather", OP.bypass, replica_groups=[[0, 1], [2, 3], [4, 5], [6, 7]],
        ins=[partial_d.opt()], outs=[pg_d.opt()]),
        reads=[("part", tc) for tc in range(32)], writes=["pgsrc"])

    def _pair(dram, o0, o1, tc):
        return (dram[o0 + 128 * tc:o0 + 128 * tc + 128, :],
                dram[o1 + 128 * tc:o1 + 128 * tc + 128, :])
    pa_row = lambda tc: _pair(pg_d, 0, 2048, tc)
    pb_row = lambda tc: _pair(pg_d, 4096, 6144, tc)
    c = ExitStack()
    emit_moe(nc, p, c, "b_", x0own, pa_row, pb_row, gate_w, w1m, b1t, w2m, b2m,
             lng_bc, lnb_moe, identb, ident, x2own_d)
    p.barrier()
    c.close()
    p.g(lambda e: e.collective_compute(
        "AllGather", OP.bypass, replica_groups=[[0, 1], [2, 3], [4, 5], [6, 7]],
        ins=[x2own_d.opt()], outs=[x2f_d.opt()]),
        reads=[("x2own", tc) for tc in range(16)], writes=["x2src"])

    x2own_row = lambda sl: _pair(x2f_d, 0, 128, 2 * sl)
    c = ExitStack()
    emit_l1(nc, p, c, "c_", x2f_d, x2own_row, wqk, wv, slope, d0, ident,
            identb, tri_a_l1, tri_b_l1, offa, kpos, kill, qps, x3s_d,
            NBCOL, colmap)
    p.barrier()
    c.close()
    p.g(lambda e: e.collective_compute(
        "AllGather", OP.bypass, replica_groups=[[0, 1], [2, 3], [4, 5], [6, 7]],
        ins=[x3s_d.opt()], outs=[x3g_d.opt()]),
        reads=[("x3own", sl) for sl in range(16)], writes=["x3gath", "x3src"])

    c = ExitStack()
    emit_ext(nc, p, c, "e_", x3g_d, ext_d, ident)
    p.barrier()
    c.close()

    def x3own_row(tc):
        off = 2048 * (tc % 2) + 128 * (tc // 2)
        return _pair(x3g_d, off, off + 1024, 0)
    c = ExitStack()
    emit_l1f(nc, p, c, "d_", ext_d, x3own_row, wqkv, qsc, ksc, wo, lng_f,
             lnb_f, w1a, w1g, b1a, b1g, w2f, b2f, mb, tri0, tri4, killf,
             mskrow, ropetabs, swp_f, ident, identb, blk2c, exp2c, outown)
    p.emit()
    c.close()
    return nc


def run_fused(inputs):
    ins = {k: np.asarray(v) for k, v in inputs.items()}
    x0 = np.ascontiguousarray(ins["L0_attn_x"].astype(np.float32))
    nc = build_fused()
    b1 = ins["L1_ffn_b1"]
    w1f = ins["L1_ffn_w1"]

    def colpack(v):
        n = v.shape[1]
        nb = (n + 127) // 128
        a = np.zeros((2, 128 * nb), np.float32)
        a[:, :n] = v
        return np.ascontiguousarray(a.reshape(2, nb, 128).transpose(0, 2, 1))

    in_maps = []
    for cidx in range(8):
        b, par = cidx // 2, cidx % 2
        im = _l0_inmap(ins, x0[b], par)
        im = {("tri_a_l0" if k == "tri_a" else
               "tri_b_l0" if k == "tri_b" else
               "swp_l0" if k == "swp" else k): v for k, v in im.items()}
        im.update({
            "x0own": np.ascontiguousarray(x0[b, par * HALF:(par + 1) * HALF]),
            "gate_w": ins["L0_ffn_gate_w"], "w1": ins["L0_ffn_w1"],
            "b1t": np.ascontiguousarray(
                ins["L0_ffn_b1"].reshape(8, 16, 128).transpose(0, 2, 1)),
            "w2": ins["L0_ffn_w2"], "b2": ins["L0_ffn_b2"],
            "lng_bc": np.ascontiguousarray(
                np.broadcast_to(ins["L0_ffn_ln_g"][:, None, :], (8, 128, D))),
            "lnb": ins["L0_ffn_ln_b"],
        })
        cc1 = _l1_consts(par)
        im.update({
            "wqk": ins["L1_attn_W_qk"], "wv": ins["L1_attn_W_v"],
            "slope": np.asarray(ins["L1_attn_alibi_slope"], np.float32).reshape(1, 1),
            "d0": cc1["d0"], "tri_a": cc1["tri_a"].astype(BFD),
            "tri_b": cc1["tri_b"].astype(BFD), "offa": cc1["offa"],
            "kpos": cc1["kpos"], "kill": cc1["kill"], "qps": cc1["qps"],
        })
        ccf = _l1f_consts(par)
        im.update({
            "wqkv": ins["L1_ffn_wqkv"],
            "qsc": np.ascontiguousarray(np.tile(ins["L1_ffn_q_scale"], (1, 2))[:, :, None]),
            "ksc": np.ascontiguousarray(np.tile(ins["L1_ffn_k_scale"], (1, 2))[:, :, None]),
            "wo": ins["L1_ffn_wo"], "lng": colpack(ins["L1_ffn_ln_g"]),
            "lnb_f": colpack(ins["L1_ffn_ln_b"]),
            "w1a": np.ascontiguousarray(w1f[:, :, :2730]),
            "w1g": np.ascontiguousarray(w1f[:, :, 2730:]),
            "b1a": colpack(b1[:, :2730]), "b1g": colpack(b1[:, 2730:]),
            "w2f": ins["L1_ffn_w2"], "b2f": colpack(ins["L1_ffn_b2"]),
            "mb": ccf["mb"], "tri0": ccf["tri0"], "tri4": ccf["tri4"],
            "killf": ccf["kill"], "mskrow": ccf["mskrow"],
            "ropetabs": ccf["ropetabs"], "swp": ccf["swp"],
            "blk2c": _l1f_blk2(), "exp2c": _l1f_exp2(),
        })
        in_maps.append(im)
    res = run_bass_kernel_spmd(nc, in_maps, core_ids=list(range(8)))
    out = np.zeros((B, T, D), np.float32)
    for cidx in range(8):
        b, par = cidx // 2, cidx % 2
        out[b, par * HALF:(par + 1) * HALF] = res.results[cidx]["outown"]
    return out
